# version 05
# baseline (speedup 1.0000x reference)
import numpy as np
import ml_dtypes
from contextlib import ExitStack
import concourse.bass as bass
import concourse.mybir as mybir
from concourse.bass_utils import run_bass_kernel_spmd

F32 = mybir.dt.float32
BF16 = mybir.dt.bfloat16
AF = mybir.ActivationFunctionType
ALU = mybir.AluOpType
AX = mybir.AxisListType.X
BFNP = ml_dtypes.bfloat16
ENGINES = ("sync", "gpsimd", "vector", "scalar", "tensor")
EPS = 1e-5


class B:

    def __init__(self, nc, ctx):
        self.nc = nc
        self.ctx = ctx
        self.ops = {e: [] for e in ENGINES}
        self.sems = {}
        self.cnt = {}

    def sem(self, name):
        h = self.ctx.enter_context(self.nc.semaphore(name))
        self.sems[name] = h
        self.cnt[name] = 0
        return h

    def at(self, name):
        return self.cnt[name]

    def emit(self, eng, fn, wait=(), inc=None):
        if isinstance(wait, (str, tuple)) and (not wait or isinstance(wait, str) or isinstance(wait[1], int)):
            wait = [wait] if wait else []
        wl = []
        for w in wait:
            if isinstance(w, str):
                wl.append((self.sems[w], self.cnt[w]))
            else:
                wl.append((self.sems[w[0]], w[1]))
        inch = None
        amt = 0
        if inc is not None:
            if isinstance(inc, tuple):
                name, amt = inc
            else:
                name, amt = inc, 1
            self.cnt[name] += amt
            inch = self.sems[name]

        def runner(e, fn=fn, wl=tuple(wl), inch=inch, amt=amt):
            for h, v in wl:
                if v > 0:
                    e.wait_ge(h, v)
            inst = fn(e)
            if inch is not None:
                assert inst is not None, "emit with inc= must return an instruction"
                inst.then_inc(inch, amt)
        self.ops[eng].append(runner)

    def install(self, block):
        for name, fns in self.ops.items():
            def runner(e, fns=tuple(fns)):
                for f in fns:
                    f(e)
            getattr(block, name)(runner)


def mmgroup(pb, pairs):
    def fn(t, pb=pb, pairs=tuple(pairs)):
        n = len(pairs)
        inst = None
        for q, (l, r) in enumerate(pairs):
            inst = t.matmul(pb, l, r, start=(q == 0), stop=(q == n - 1))
        return inst
    return fn


def pack_lhsT(wt):
    K, M = wt.shape
    KT = K // 128
    Mw = min(M, 128)
    MT = M // Mw
    return np.ascontiguousarray(
        wt.reshape(KT, 128, MT, Mw).transpose(2, 0, 1, 3).reshape(MT * KT, 128, Mw)
        .transpose(1, 0, 2).reshape(128, MT * KT * Mw))


def vec_pm(v):
    D = v.shape[0]
    return np.ascontiguousarray(v.reshape(D // 128, 128).T)


def fm(x):
    N, D = x.shape
    NT = D // 128
    return np.ascontiguousarray(x.T.reshape(NT, 128, N).transpose(1, 0, 2).reshape(128, NT * N))


def unfm(a, N, D):
    NT = D // 128
    return np.ascontiguousarray(a.reshape(128, NT, N).transpose(1, 0, 2).reshape(D, N).T)


def tm(x):
    N, D = x.shape
    return np.ascontiguousarray(x.reshape(N // 128, 128, D).transpose(1, 0, 2).reshape(128, (N // 128) * D))


def untm(a, N, D):
    return np.ascontiguousarray(a.reshape(128, N // 128, D).transpose(1, 0, 2).reshape(N, D))


def emit_ln(b, S, src_f32, dst, CH, *, src_tm=None, affine=None, out_bf=None):
    NTT = CH // 128
    sq, stat, murow, rsrow = S["sq"], S["stat"], S["murow"], S["rsrow"]
    ps_st, ps_tr, ps_tr2, ps_bc = S["ps_st"], S["ps_tr"], S["ps_tr2"], S["ps_bc"]
    ones, ones1, idm = S["ones"], S["ones1"], S["idm"]

    for j in range(NTT):
        if src_tm is not None:
            b.emit("scalar", lambda s, j=j: s.square(sq[:, 0:1024], src_tm[:, 1024 * j:1024 * (j + 1)]),
                   wait=("dv" if j > 0 else ()), inc="ac")
            b.emit("vector", lambda v, j=j: v.reduce_sum(stat[:, 8 * j:8 * j + 1], src_tm[:, 1024 * j:1024 * (j + 1)], axis=AX))
            b.emit("vector", lambda v, j=j: v.reduce_sum(stat[:, 8 * j + 1:8 * j + 2], sq[:, 0:1024], axis=AX),
                   wait="ac", inc="dv")
            b.emit("vector", lambda v, j=j: v.tensor_scalar(stat[:, 8 * j:8 * j + 2], stat[:, 8 * j:8 * j + 2], 1.0 / 1024.0, None, ALU.mult),
                   wait="dv", inc="dv")
        else:
            if j == 0:
                b.emit("scalar", lambda s: s.square(sq[:], src_f32[:]), wait="dv", inc="ac")
            b.emit("tensor", mmgroup(ps_st[:, 0:1],
                   [(src_f32[:, u * CH + 128 * j:u * CH + 128 * (j + 1)], ones[:, :]) for u in range(8)]),
                   wait=("ac" if j == 0 else "dv"), inc="mm")
            b.emit("tensor", mmgroup(ps_st[:, 1:2],
                   [(sq[:, u * CH + 128 * j:u * CH + 128 * (j + 1)], ones[:, :]) for u in range(8)]),
                   inc="mm")
            b.emit("vector", lambda v, j=j: v.tensor_scalar(stat[:, 8 * j:8 * j + 2], ps_st[:, 0:2], 1.0 / 1024.0, None, ALU.mult),
                   wait="mm", inc="dv")
        b.emit("vector", lambda v, j=j: v.tensor_tensor(stat[:, 8 * j + 2:8 * j + 3], stat[:, 8 * j:8 * j + 1], stat[:, 8 * j:8 * j + 1], ALU.mult),
               wait="dv", inc="dv")
        b.emit("vector", lambda v, j=j: v.tensor_tensor(stat[:, 8 * j + 3:8 * j + 4], stat[:, 8 * j + 1:8 * j + 2], stat[:, 8 * j + 2:8 * j + 3], ALU.subtract),
               wait="dv", inc="dv")
        b.emit("scalar", lambda s, j=j: s.activation(stat[:, 8 * j + 4:8 * j + 5], stat[:, 8 * j + 3:8 * j + 4], AF.Sqrt, bias=S["epsv"][:, 0:1]),
               wait="dv", inc="ac")
        b.emit("vector", lambda v, j=j: v.reciprocal(stat[:, 8 * j + 5:8 * j + 6], stat[:, 8 * j + 4:8 * j + 5]),
               wait="ac", inc="dv")
        def trfn(t, j=j):
            t.transpose(ps_tr[0:1, 0:128], stat[:, 8 * j:8 * j + 1], idm[:, :])
            return t.transpose(ps_tr2[0:1, 0:128], stat[:, 8 * j + 5:8 * j + 6], idm[:, :])
        b.emit("tensor", trfn, wait="dv", inc="mm")
        b.emit("vector", lambda v, j=j: v.tensor_copy(murow[0:1, 128 * j:128 * (j + 1)], ps_tr[0:1, 0:128]),
               wait="mm")
        b.emit("vector", lambda v, j=j: v.tensor_copy(rsrow[0:1, 128 * j:128 * (j + 1)], ps_tr2[0:1, 0:128]),
               inc="dv")
        def bcfn(t, j=j):
            t.matmul(ps_bc[:, 0:128], ones1[:, :], murow[0:1, 128 * j:128 * (j + 1)], start=True, stop=True)
            return t.matmul(ps_bc[:, 128:256], ones1[:, :], rsrow[0:1, 128 * j:128 * (j + 1)], start=True, stop=True)
        b.emit("tensor", bcfn, wait="dv", inc="mm")
        for u in range(8):
            sl = slice(u * CH + 128 * j, u * CH + 128 * (j + 1))
            b.emit("vector", lambda v, sl=sl: v.tensor_tensor(dst[:, sl], src_f32[:, sl], ps_bc[:, 0:128], ALU.subtract),
                   wait=("mm" if u == 0 else ()), inc="dv")
            b.emit("vector", lambda v, sl=sl: v.tensor_tensor(dst[:, sl], dst[:, sl], ps_bc[:, 128:256], ALU.mult),
                   wait="dv", inc="dv")
            if affine is not None:
                ga, ba = affine
                b.emit("vector", lambda v, sl=sl, u=u: v.tensor_scalar(dst[:, sl], dst[:, sl], ga[:, u:u + 1], ba[:, u:u + 1], ALU.mult, ALU.add),
                       wait="dv", inc="dv")
    if out_bf is not None:
        b.emit("scalar", lambda s: s.copy(out_bf[:], dst[:]), wait="dv", inc="ac")


def build_preA(ntok, CH=512, stages=(1, 2, 3, 4, 5, 6, 7)):
    nc = bass.Bass("TRN2", target_bir_lowering=False)
    NCH = ntok // CH
    NTT = CH // 128
    xT = nc.dram_tensor("xT", [128, 8 * ntok], F32, kind="ExternalInput").ap()
    xTM = nc.dram_tensor("xTM", [ntok, 1024], F32, kind="ExternalInput").ap()
    win = nc.dram_tensor("win", [128, 16 * 8 * 128], F32, kind="ExternalInput").ap()
    ln_g = nc.dram_tensor("ln_g", [128, 8], F32, kind="ExternalInput").ap()
    ln_b = nc.dram_tensor("ln_b", [128, 8], F32, kind="ExternalInput").ap()
    b_u = nc.dram_tensor("b_u", [128, 8], F32, kind="ExternalInput").ap()
    b_v = nc.dram_tensor("b_v", [128, 8], F32, kind="ExternalInput").ap()
    b_h = nc.dram_tensor("b_h", [128, 8], F32, kind="ExternalInput").ap()
    rw = nc.dram_tensor("rw", [128, 8 * 8], F32, kind="ExternalInput").ap()
    rb_bc = nc.dram_tensor("rb_bc", [128, 8], F32, kind="ExternalInput").ap()
    iota8 = nc.dram_tensor("iota8", [128, 8], F32, kind="ExternalInput").ap()
    id128 = nc.dram_tensor("id128", [128, 128], F32, kind="ExternalInput").ap()
    xn_o = nc.dram_tensor("xn_o", [128, 8 * ntok], F32, kind="ExternalOutput").ap()
    vpre_o = nc.dram_tensor("vpre_o", [128, 8 * ntok], F32, kind="ExternalOutput").ap()
    gg_o = nc.dram_tensor("gg_o", [128, 8 * ntok], F32, kind="ExternalOutput").ap()
    mask_o = nc.dram_tensor("mask_o", [ntok, 8], F32, kind="ExternalOutput").ap()
    gates_o = nc.dram_tensor("gates_o", [ntok, 8], F32, kind="ExternalOutput").ap()

    with ExitStack() as ctx:
        e = ctx.enter_context
        wsb = e(nc.sbuf_tensor("wsb", [128, 16 * 8 * 128], BF16))
        idm = e(nc.sbuf_tensor("idm", [128, 128], F32))
        ones = e(nc.sbuf_tensor("ones", [128, 1], F32))
        ones1 = e(nc.sbuf_tensor("ones1", [1, 128], F32))
        epsv = e(nc.sbuf_tensor("epsv", [128, 1], F32))
        lng = e(nc.sbuf_tensor("lng", [128, 8], F32))
        lnb = e(nc.sbuf_tensor("lnb", [128, 8], F32))
        bu = e(nc.sbuf_tensor("bu", [128, 8], F32))
        bv = e(nc.sbuf_tensor("bv", [128, 8], F32))
        bh = e(nc.sbuf_tensor("bh", [128, 8], F32))
        rws = e(nc.sbuf_tensor("rws", [128, 64], F32))
        rwh = e(nc.sbuf_tensor("rwh", [128, 64], BF16))
        rwl = e(nc.sbuf_tensor("rwl", [128, 64], BF16))
        rwt = e(nc.sbuf_tensor("rwt", [128, 64], F32))
        xlb = e(nc.sbuf_tensor("xlb", [128, 8 * CH], BF16))
        rbbc = e(nc.sbuf_tensor("rbbc", [128, 8], F32))
        iot = e(nc.sbuf_tensor("iot", [128, 8], F32))
        xf = e(nc.sbuf_tensor("xf", [128, 8 * CH], F32))
        xtm = e(nc.sbuf_tensor("xtm", [128, NTT * 1024], F32))
        sq = e(nc.sbuf_tensor("sq", [128, 8 * CH], F32))
        xn = e(nc.sbuf_tensor("xn", [128, 8 * CH], F32))
        xnbf = e(nc.sbuf_tensor("xnbf", [128, 8 * CH], BF16))
        s0l = e(nc.sbuf_tensor("s0l", [128, 8 * CH], F32))
        s0lb = e(nc.sbuf_tensor("s0lb", [128, 8 * CH], BF16))
        stat = e(nc.sbuf_tensor("stat", [128, 8 * NTT], F32))
        murow = e(nc.sbuf_tensor("murow", [1, 128 * NTT], F32))
        rsrow = e(nc.sbuf_tensor("rsrow", [1, 128 * NTT], F32))
        vout = e(nc.sbuf_tensor("vout", [128, 8 * CH], F32))
        gout = e(nc.sbuf_tensor("gout", [128, 8 * CH], F32))
        rmask = e(nc.sbuf_tensor("rmask", [128, 8 * NTT], F32))
        rgate = e(nc.sbuf_tensor("rgate", [128, 8 * NTT], F32))
        rt1 = e(nc.sbuf_tensor("rt1", [128, 8], F32))
        rt2 = e(nc.sbuf_tensor("rt2", [128, 8], F32))
        rt3 = e(nc.sbuf_tensor("rt3", [128, 8], F32))
        rsc = e(nc.sbuf_tensor("rsc", [128, 4], F32))
        ps_st = e(nc.psum_tensor("ps_st", [128, 2], F32))
        ps_tr = e(nc.psum_tensor("ps_tr", [1, 128], F32))
        ps_tr2 = e(nc.psum_tensor("ps_tr2", [1, 128], F32))
        ps_bc = e(nc.psum_tensor("ps_bc", [128, 256], F32))
        ps_uv0 = e(nc.psum_tensor("ps_uv0", [128, CH], F32))
        ps_uv1 = e(nc.psum_tensor("ps_uv1", [128, CH], F32))
        ps_rt = e(nc.psum_tensor("ps_rt", [128, 8], F32))
        b = B(nc, ctx)
        for sname in ("ldc", "ldg", "ldx", "mm", "dv", "ac", "st"):
            b.sem(sname)
        block = e(nc.Block())
        S = dict(ones=ones, ones1=ones1, idm=idm, sq=sq, stat=stat, murow=murow,
                 rsrow=rsrow, ps_st=ps_st, ps_tr=ps_tr, ps_tr2=ps_tr2, ps_bc=ps_bc, epsv=epsv)

        def dve(fn):
            b.emit("vector", fn, wait=["dv", "ac"], inc="dv")

        for dst, src in ((idm, id128), (lng, ln_g), (lnb, ln_b), (bu, b_u),
                         (bv, b_v), (bh, b_h), (rbbc, rb_bc), (iot, iota8)):
            b.emit("sync", lambda s, d=dst, sr=src: s.dma_start(out=d[:], in_=sr[:, :]), inc=("ldc", 16))
        b.emit("gpsimd", lambda g: g.dma_start(out=wsb[:], in_=win[:, :]), inc=("ldg", 16))
        b.emit("gpsimd", lambda g: g.dma_start(out=rws[:], in_=rw[:, :]), inc=("ldg", 16))
        ld0 = [("ldc", b.at("ldc")), ("ldg", b.at("ldg"))]
        b.emit("vector", lambda v: v.memset(ones[:], 1.0))
        b.emit("vector", lambda v: v.memset(ones1[:], 1.0))
        b.emit("vector", lambda v: v.memset(epsv[:], EPS), inc="dv")
        b.emit("vector", lambda v: v.tensor_add(bv[:], bv[:], bh[:]), wait=ld0, inc="dv")
        b.emit("scalar", lambda s: s.copy(rwh[:], rws[:]), wait=ld0, inc="ac")
        b.emit("scalar", lambda s: s.copy(rwt[:], rwh[:]), wait="ac", inc="ac")
        b.emit("vector", lambda v: v.tensor_tensor(rwt[:], rws[:], rwt[:], ALU.subtract), wait="ac", inc="dv")
        b.emit("scalar", lambda s: s.copy(rwl[:], rwt[:]), wait="dv", inc="ac")
        for eng in ("scalar", "tensor"):
            b.emit(eng, lambda x: None, wait=ld0 + ["dv"])

        for c in range(NCH):
            st0 = b.at("st")
            b.emit("sync", lambda s, c=c: s.dma_start(
                out=xf[:].rearrange("p (u n) -> p u n", u=8),
                in_=xT.rearrange("p (u n) -> p u n", u=8)[:, :, c * CH:(c + 1) * CH]),
                wait=[("st", st0)] + ld0, inc=("ldx", 16))
            b.emit("sync", lambda s, c=c: s.dma_start(
                out=xtm[:].rearrange("p (j d) -> p j d", d=1024),
                in_=xTM[c * CH:(c + 1) * CH, :].rearrange("(j p) d -> p j d", p=128)),
                inc=("ldx", 16))
            ldv = b.at("ldx")
            for eng in ("vector", "scalar", "tensor"):
                b.emit(eng, lambda x: None, wait=[("ldx", ldv), ("ldg", b.at("ldg")), ("st", st0)] + ld0)

            emit_ln(b, S, xf, xn, CH, src_tm=xtm, affine=None, out_bf=xnbf)
            dv_xn = b.at("dv")
            b.emit("sync", lambda s, c=c: s.dma_start(
                out=xn_o.rearrange("p (u n) -> p u n", u=8)[:, :, c * CH:(c + 1) * CH],
                in_=xn[:].rearrange("p (u n) -> p u n", u=8)),
                wait=[("dv", dv_xn)], inc=("st", 16))

            if 2 not in stages:
                continue
            emit_ln(b, S, xn, s0l, CH, src_tm=None, affine=(lng, lnb), out_bf=s0lb)
            ac_s0lb = b.at("ac")

            do_uv = 3 in stages
            bank_guard = {0: ("ac", 0), 1: ("ac", 0)}
            for ei in (range(16) if do_uv else ()):
                pb = ps_uv0 if ei % 2 == 0 else ps_uv1
                waits = [("ac", ac_s0lb), bank_guard[ei % 2]]
                b.emit("tensor", mmgroup(pb[:, :],
                       [(wsb[:, (ei * 8 + k) * 128:(ei * 8 + k + 1) * 128], s0lb[:, k * CH:(k + 1) * CH]) for k in range(8)]),
                       wait=waits, inc="mm")
                if ei < 8:
                    b.emit("scalar", lambda s, ei=ei, pb=pb: s.activation(
                        gout[:, ei * CH:(ei + 1) * CH], pb[:, :], AF.Sigmoid, bias=bu[:, ei:ei + 1]),
                        wait="mm", inc="ac")
                    bank_guard[ei % 2] = ("ac", b.at("ac"))
                else:
                    u = ei - 8
                    b.emit("vector", lambda v, u=u, pb=pb: v.tensor_scalar(
                        vout[:, u * CH:(u + 1) * CH], pb[:, :], bv[:, u:u + 1], None, ALU.add),
                        wait="mm", inc="dv")
                    bank_guard[ei % 2] = ("dv", b.at("dv"))
            if do_uv:
                b.emit("sync", lambda s, c=c: s.dma_start(
                    out=vpre_o.rearrange("p (u n) -> p u n", u=8)[:, :, c * CH:(c + 1) * CH],
                    in_=vout[:].rearrange("p (u n) -> p u n", u=8)),
                    wait=["dv"], inc=("st", 16))
                b.emit("sync", lambda s, c=c: s.dma_start(
                    out=gg_o.rearrange("p (u n) -> p u n", u=8)[:, :, c * CH:(c + 1) * CH],
                    in_=gout[:].rearrange("p (u n) -> p u n", u=8)),
                    wait=["ac"], inc=("st", 16))

            if 4 not in stages:
                continue
            b.emit("scalar", lambda s: s.copy(s0l[:], xnbf[:]), wait=["dv", "mm"], inc="ac")
            b.emit("vector", lambda v: v.tensor_tensor(sq[:], xn[:], s0l[:], ALU.subtract), wait="ac", inc="dv")
            b.emit("scalar", lambda s: s.copy(xlb[:], sq[:]), wait="dv", inc="ac")
            ac_split = b.at("ac")
            rt_guard = ("dv", 0)
            for j in range(NTT):
                b.emit("tensor", mmgroup(ps_rt[:, :],
                       [(xa[:, u * CH + 128 * j:u * CH + 128 * (j + 1)], wb[:, u * 8:(u + 1) * 8])
                        for (xa, wb) in ((xnbf, rwh), (xnbf, rwl), (xlb, rwh), (xlb, rwl))
                        for u in range(8)]),
                       wait=[("ac", ac_split), rt_guard], inc="mm")
                base = 8 * j
                b.emit("vector", lambda v: v.tensor_tensor(rt1[:, :], ps_rt[:, :], rbbc[:, :], ALU.add),
                       wait=["dv", "mm", "ac"], inc="dv")
                rt_guard = ("dv", b.at("dv"))
                if 5 not in stages:
                    continue
                dve(lambda v: v.reduce_max(rsc[:, 0:1], rt1[:, :], axis=AX))
                dve(lambda v: v.tensor_scalar(rsc[:, 1:2], rsc[:, 0:1], -1.0, None, ALU.mult))
                b.emit("scalar", lambda s: s.activation(rt2[:, :], rt1[:, :], AF.Exp, bias=rsc[:, 1:2]),
                       wait="dv", inc="ac")
                b.emit("vector", lambda v: v.reduce_sum(rsc[:, 2:3], rt2[:, :], axis=AX), wait="ac", inc="dv")
                dve(lambda v: v.reciprocal(rsc[:, 3:4], rsc[:, 2:3]))
                dve(lambda v, b8=base: v.tensor_scalar(rgate[:, b8:b8 + 8], rt2[:, :], rsc[:, 3:4], None, ALU.mult))
                if 6 not in stages:
                    continue
                dve(lambda v, b8=base: v.reduce_max(rsc[:, 0:1], rgate[:, b8:b8 + 8], axis=AX))
                dve(lambda v, b8=base: v.tensor_scalar(rt1[:, :], rgate[:, b8:b8 + 8], rsc[:, 0:1], None, ALU.is_ge))
                dve(lambda v: v.tensor_scalar(rt2[:, :], rt1[:, :], -1e9, 1e9, ALU.mult, ALU.add))
                dve(lambda v: v.tensor_tensor(rt2[:, :], rt2[:, :], iot[:, :], ALU.add))
                dve(lambda v: v.tensor_reduce(rsc[:, 0:1], rt2[:, :], AX, ALU.min))
                dve(lambda v: v.tensor_scalar(rt1[:, :], rt2[:, :], rsc[:, 0:1], None, ALU.is_le))
                if 7 not in stages:
                    dve(lambda v, b8=base: v.tensor_copy(rmask[:, b8:b8 + 8], rt1[:, :]))
                    continue
                dve(lambda v, b8=base: v.tensor_scalar(rt2[:, :], rt1[:, :], -2.0, None, ALU.mult))
                dve(lambda v, b8=base: v.tensor_tensor(rt2[:, :], rt2[:, :], rgate[:, b8:b8 + 8], ALU.add))
                dve(lambda v: v.reduce_max(rsc[:, 0:1], rt2[:, :], axis=AX))
                dve(lambda v: v.tensor_scalar(rt3[:, :], rt2[:, :], rsc[:, 0:1], None, ALU.is_ge))
                dve(lambda v: v.tensor_scalar(rt2[:, :], rt3[:, :], -1e9, 1e9, ALU.mult, ALU.add))
                dve(lambda v: v.tensor_tensor(rt2[:, :], rt2[:, :], iot[:, :], ALU.add))
                dve(lambda v: v.tensor_reduce(rsc[:, 0:1], rt2[:, :], AX, ALU.min))
                dve(lambda v: v.tensor_scalar(rt3[:, :], rt2[:, :], rsc[:, 0:1], None, ALU.is_le))
                dve(lambda v, b8=base: v.tensor_tensor(rmask[:, b8:b8 + 8], rt1[:, :], rt3[:, :], ALU.max))
            b.emit("sync", lambda s, c=c: s.dma_start(
                out=mask_o[c * CH:(c + 1) * CH, :].rearrange("(j p) d -> p j d", p=128),
                in_=rmask[:, 0:8 * NTT].rearrange("p (j d) -> p j d", d=8)), wait=["dv"], inc=("st", 16))
            b.emit("sync", lambda s, c=c: s.dma_start(
                out=gates_o[c * CH:(c + 1) * CH, :].rearrange("(j p) d -> p j d", p=128),
                in_=rgate[:, 0:8 * NTT].rearrange("p (j d) -> p j d", d=8)), inc=("st", 16))

        b.emit("sync", lambda s: None, wait=["st"])
        b.install(block)
    return nc


def preA_inputs(x_slice, W_in0, ln_g0, ln_b0, b_in0, b_h0, router_w, router_b):
    ntok = x_slice.shape[0]
    return {
        "xT": fm(x_slice),
        "xTM": np.ascontiguousarray(x_slice),
        "win": pack_lhsT(np.ascontiguousarray(W_in0.T)),
        "ln_g": vec_pm(ln_g0), "ln_b": vec_pm(ln_b0),
        "b_u": vec_pm(b_in0[:1024]), "b_v": vec_pm(b_in0[1024:]), "b_h": vec_pm(b_h0),
        "rw": np.ascontiguousarray(router_w.reshape(8, 128, 8).transpose(1, 0, 2).reshape(128, 64)),
        "rb_bc": np.ascontiguousarray(np.tile(router_b[None, :], (128, 1)).astype(np.float32)),
        "iota8": np.ascontiguousarray(np.tile(np.arange(8, dtype=np.float32)[None, :], (128, 1))),
        "id128": np.eye(128, dtype=np.float32),
    }



def build_scan(T, CHUNK=128):
    nc = bass.Bass("TRN2", target_bir_lowering=False)
    w = nc.dram_tensor("w", [128, 64 * 128], F32, kind="ExternalInput").ap()
    vpre = nc.dram_tensor("vpre", [128, 16 * T], F32, kind="ExternalInput").ap()
    g = nc.dram_tensor("g", [128, 16 * T], F32, kind="ExternalInput").ap()
    hout = nc.dram_tensor("hout", [128, 16 * T], F32, kind="ExternalOutput").ap()
    nch = T // CHUNK
    with ExitStack() as ctx:
        e = ctx.enter_context
        ws = e(nc.sbuf_tensor("ws", [128, 64 * 128], BF16))
        vs = e(nc.sbuf_tensor("vs", [128, 16 * CHUNK * 2], F32))
        gs = e(nc.sbuf_tensor("gs", [128, 16 * CHUNK * 2], F32))
        gms = e(nc.sbuf_tensor("gms", [128, 16 * CHUNK * 2], F32))
        hall = e(nc.sbuf_tensor("hall", [128, 16 * CHUNK * 2], F32))
        z16 = e(nc.sbuf_tensor("z16", [128, 16], F32))
        hb = e(nc.sbuf_tensor("hb", [128, 16], BF16))
        zs = e(nc.sbuf_tensor("zs", [128, 16], F32))
        cs = e(nc.sbuf_tensor("cs", [128, 16], F32))
        t1 = e(nc.sbuf_tensor("t1", [128, 16], F32))
        t2 = e(nc.sbuf_tensor("t2", [128, 16], F32))
        ps = e(nc.psum_tensor("ps", [128, 16], F32))
        ldg = e(nc.semaphore("ldg"))
        ldx0 = e(nc.semaphore("ldx0"))
        ldx1 = e(nc.semaphore("ldx1"))
        mm_s = e(nc.semaphore("mm_s"))
        z_s = e(nc.semaphore("z_s"))
        c_s = e(nc.semaphore("c_s"))
        h_s = e(nc.semaphore("h_s"))
        e_s = e(nc.semaphore("e_s"))
        gm_s = e(nc.semaphore("gm_s"))
        hb_s = e(nc.semaphore("hb_s"))
        stc = e(nc.semaphore("stc"))
        block = e(nc.Block())

        @block.gpsimd
        def _(gp):
            gp.dma_start(out=ws[:], in_=w[:, :]).then_inc(ldg, 16)

        @block.sync
        def _(sy):
            for c in range(nch):
                buf = c % 2
                sem = ldx0 if buf == 0 else ldx1
                if c >= 2:
                    sy.wait_ge(h_s, (c - 1) * CHUNK)
                    sy.wait_ge(stc, 16 * (c - 1))
                sy.dma_start(out=vs[:, 16 * CHUNK * buf:16 * CHUNK * (buf + 1)],
                             in_=vpre[:, 16 * CHUNK * c:16 * CHUNK * (c + 1)]).then_inc(sem, 16)
                sy.dma_start(out=gs[:, 16 * CHUNK * buf:16 * CHUNK * (buf + 1)],
                             in_=g[:, 16 * CHUNK * c:16 * CHUNK * (c + 1)]).then_inc(sem, 16)
                sy.wait_ge(h_s, (c + 1) * CHUNK)
                sy.dma_start(out=hout[:, 16 * CHUNK * c:16 * CHUNK * (c + 1)],
                             in_=hall[:, 16 * CHUNK * buf:16 * CHUNK * (buf + 1)]).then_inc(stc, 16)
            sy.wait_ge(stc, 16 * nch)

        @block.vector
        def _(v):
            v.memset(z16[:], 0.0)
            v.memset(hb[:], 0.0).then_inc(hb_s, 1)
            for c in range(nch):
                buf = c % 2
                sem = ldx0 if buf == 0 else ldx1
                v.wait_ge(sem, 32 * (c // 2 + 1))
                v.tensor_scalar(gms[:, 16 * CHUNK * buf:16 * CHUNK * (buf + 1)],
                                gs[:, 16 * CHUNK * buf:16 * CHUNK * (buf + 1)],
                                -1.0, 1.0, ALU.mult, ALU.add).then_inc(gm_s, 1)
                v.wait_ge(gm_s, c + 1)
                for tt in range(CHUNK):
                    t = c * CHUNK + tt
                    col = slice(16 * (CHUNK * buf + tt), 16 * (CHUNK * buf + tt + 1))
                    if tt == 0:
                        pcol = None if c == 0 else slice(16 * (CHUNK * ((c - 1) % 2) + CHUNK - 1),
                                                         16 * (CHUNK * ((c - 1) % 2) + CHUNK))
                    else:
                        pcol = slice(16 * (CHUNK * buf + tt - 1), 16 * (CHUNK * buf + tt))
                    hprev = z16[:, 0:16] if pcol is None else hall[:, pcol]
                    v.wait_ge(mm_s, t + 1)
                    v.tensor_add(zs[:], ps[:, 0:16], vs[:, col]).then_inc(z_s, 1)
                    v.tensor_mul(t2[:], gms[:, col], hprev)
                    v.wait_ge(c_s, t + 1)
                    v.tensor_mul(t1[:], gs[:, col], cs[:]).then_inc(e_s, 1)
                    v.wait_ge(e_s, t + 1)
                    v.tensor_add(hall[:, col], t1[:], t2[:]).then_inc(h_s, 1)
                    v.wait_ge(h_s, t + 1)
                    v.tensor_copy(hb[:], hall[:, col]).then_inc(hb_s, 1)

        @block.scalar
        def _(s):
            for t in range(T):
                s.wait_ge(z_s, t + 1)
                s.activation(out=cs[:], in_=zs[:], func=AF.Tanh).then_inc(c_s, 1)

        @block.tensor
        def _(te):
            te.wait_ge(ldg, 16)
            for t in range(T):
                te.wait_ge(hb_s, t + 1)
                for i in range(8):
                    for k in range(8):
                        inst = te.matmul(ps[:, 2 * i:2 * i + 2],
                                         ws[:, (i * 8 + k) * 128:(i * 8 + k + 1) * 128],
                                         hb[:, 2 * k:2 * k + 2],
                                         start=(k == 0), stop=(k == 7))
                inst.then_inc(mm_s, 1)
    return nc


def scan_inputs(W_h, vpre_tdb, g_tdb):
    T = vpre_tdb.shape[0]

    def pk(x):
        return np.ascontiguousarray(x.reshape(T, 8, 128, 2).transpose(2, 0, 1, 3).reshape(128, T * 16))
    return {"w": pack_lhsT(np.ascontiguousarray(W_h.T)), "vpre": pk(vpre_tdb), "g": pk(g_tdb)}


def scan_h_unpack(hout, T):
    return np.ascontiguousarray(hout.reshape(128, T, 8, 2).transpose(1, 2, 0, 3).reshape(T, 1024, 2))


def build_midC(ntok, CH=512):
    nc = bass.Bass("TRN2", target_bir_lowering=False)
    NCH = ntok // CH
    NTT = CH // 128
    h0T = nc.dram_tensor("h0T", [128, 8 * ntok], F32, kind="ExternalInput").ap()
    xnT = nc.dram_tensor("xnT", [128, 8 * ntok], F32, kind="ExternalInput").ap()
    wout = nc.dram_tensor("wout", [128, 8 * 8 * 128], F32, kind="ExternalInput").ap()
    win1 = nc.dram_tensor("win1", [128, 16 * 8 * 128], F32, kind="ExternalInput").ap()
    ln_g = nc.dram_tensor("ln_g", [128, 8], F32, kind="ExternalInput").ap()
    ln_b = nc.dram_tensor("ln_b", [128, 8], F32, kind="ExternalInput").ap()
    b_out = nc.dram_tensor("b_out", [128, 8], F32, kind="ExternalInput").ap()
    b_u = nc.dram_tensor("b_u", [128, 8], F32, kind="ExternalInput").ap()
    b_v = nc.dram_tensor("b_v", [128, 8], F32, kind="ExternalInput").ap()
    b_h = nc.dram_tensor("b_h", [128, 8], F32, kind="ExternalInput").ap()
    id128 = nc.dram_tensor("id128", [128, 128], F32, kind="ExternalInput").ap()
    s0o_o = nc.dram_tensor("s0o_o", [128, 8 * ntok], F32, kind="ExternalOutput").ap()
    vpre_o = nc.dram_tensor("vpre_o", [128, 8 * ntok], F32, kind="ExternalOutput").ap()
    gg_o = nc.dram_tensor("gg_o", [128, 8 * ntok], F32, kind="ExternalOutput").ap()
    with ExitStack() as ctx:
        e = ctx.enter_context
        wo = e(nc.sbuf_tensor("wo", [128, 8 * 8 * 128], BF16))
        w1 = e(nc.sbuf_tensor("w1", [128, 16 * 8 * 128], BF16))
        idm = e(nc.sbuf_tensor("idm", [128, 128], F32))
        ones = e(nc.sbuf_tensor("ones", [128, 1], F32))
        ones1 = e(nc.sbuf_tensor("ones1", [1, 128], F32))
        epsv = e(nc.sbuf_tensor("epsv", [128, 1], F32))
        lng = e(nc.sbuf_tensor("lng", [128, 8], F32))
        lnb = e(nc.sbuf_tensor("lnb", [128, 8], F32))
        bo = e(nc.sbuf_tensor("bo", [128, 8], F32))
        bu = e(nc.sbuf_tensor("bu", [128, 8], F32))
        bv = e(nc.sbuf_tensor("bv", [128, 8], F32))
        bh = e(nc.sbuf_tensor("bh", [128, 8], F32))
        h0b = e(nc.sbuf_tensor("h0b", [128, 8 * CH], BF16))
        xnf = e(nc.sbuf_tensor("xnf", [128, 8 * CH], F32))
        s0o = e(nc.sbuf_tensor("s0o", [128, 8 * CH], F32))
        sq = e(nc.sbuf_tensor("sq", [128, 8 * CH], F32))
        s1l = e(nc.sbuf_tensor("s1l", [128, 8 * CH], F32))
        s1lb = e(nc.sbuf_tensor("s1lb", [128, 8 * CH], BF16))
        stat = e(nc.sbuf_tensor("stat", [128, 8 * NTT], F32))
        murow = e(nc.sbuf_tensor("murow", [1, 128 * NTT], F32))
        rsrow = e(nc.sbuf_tensor("rsrow", [1, 128 * NTT], F32))
        vout = e(nc.sbuf_tensor("vout", [128, 8 * CH], F32))
        gout = e(nc.sbuf_tensor("gout", [128, 8 * CH], F32))
        ps_st = e(nc.psum_tensor("ps_st", [128, 2], F32))
        ps_tr = e(nc.psum_tensor("ps_tr", [1, 128], F32))
        ps_tr2 = e(nc.psum_tensor("ps_tr2", [1, 128], F32))
        ps_bc = e(nc.psum_tensor("ps_bc", [128, 256], F32))
        ps_a = e(nc.psum_tensor("ps_a", [128, CH], F32))
        ps_b = e(nc.psum_tensor("ps_b", [128, CH], F32))
        b = B(nc, ctx)
        for sname in ("ldc", "ldg", "ldx", "mm", "dv", "ac", "st"):
            b.sem(sname)
        block = e(nc.Block())
        S = dict(ones=ones, ones1=ones1, idm=idm, sq=sq, stat=stat, murow=murow,
                 rsrow=rsrow, ps_st=ps_st, ps_tr=ps_tr, ps_tr2=ps_tr2, ps_bc=ps_bc, epsv=epsv)

        for dst, src in ((idm, id128), (lng, ln_g), (lnb, ln_b), (bo, b_out),
                         (bu, b_u), (bv, b_v), (bh, b_h)):
            b.emit("sync", lambda s, d=dst, sr=src: s.dma_start(out=d[:], in_=sr[:, :]), inc=("ldc", 16))
        b.emit("gpsimd", lambda g: g.dma_start(out=wo[:], in_=wout[:, :]), inc=("ldg", 16))
        b.emit("gpsimd", lambda g: g.dma_start(out=w1[:], in_=win1[:, :]), inc=("ldg", 16))
        ld0 = [("ldc", b.at("ldc")), ("ldg", b.at("ldg"))]
        b.emit("vector", lambda v: v.memset(ones[:], 1.0))
        b.emit("vector", lambda v: v.memset(ones1[:], 1.0))
        b.emit("vector", lambda v: v.memset(epsv[:], EPS), inc="dv")
        b.emit("vector", lambda v: v.tensor_add(bv[:], bv[:], bh[:]), wait=ld0, inc="dv")
        for eng in ("scalar", "tensor"):
            b.emit(eng, lambda x: None, wait=ld0 + ["dv"])

        for c in range(NCH):
            st0 = b.at("st")
            b.emit("gpsimd", lambda g, c=c: g.dma_start(
                out=h0b[:].rearrange("p (u n) -> p u n", u=8),
                in_=h0T.rearrange("p (u n) -> p u n", u=8)[:, :, c * CH:(c + 1) * CH]),
                wait=[("st", st0)] + ld0, inc=("ldg", 16))
            b.emit("sync", lambda s, c=c: s.dma_start(
                out=xnf[:].rearrange("p (u n) -> p u n", u=8),
                in_=xnT.rearrange("p (u n) -> p u n", u=8)[:, :, c * CH:(c + 1) * CH]),
                wait=[("st", st0)] + ld0, inc=("ldx", 16))
            ldv = [("ldx", b.at("ldx")), ("ldg", b.at("ldg")), ("st", st0)]
            for eng in ("vector", "scalar", "tensor"):
                b.emit(eng, lambda x: None, wait=ldv)

            bank_guard = {0: ("dv", 0), 1: ("dv", 0)}
            for ei in range(8):
                pb = ps_a if ei % 2 == 0 else ps_b
                b.emit("tensor", mmgroup(pb[:, :],
                       [(wo[:, (ei * 8 + k) * 128:(ei * 8 + k + 1) * 128], h0b[:, k * CH:(k + 1) * CH]) for k in range(8)]),
                       wait=[bank_guard[ei % 2]], inc="mm")
                b.emit("vector", lambda v, ei=ei, pb=pb: v.tensor_scalar(sq[:, ei * CH:(ei + 1) * CH], pb[:, :], bo[:, ei:ei + 1], None, ALU.add),
                       wait="mm", inc="dv")
                b.emit("vector", lambda v, ei=ei: v.tensor_add(s0o[:, ei * CH:(ei + 1) * CH], sq[:, ei * CH:(ei + 1) * CH], xnf[:, ei * CH:(ei + 1) * CH]),
                       wait="dv", inc="dv")
                bank_guard[ei % 2] = ("dv", b.at("dv"))
            dv_s0 = b.at("dv")
            b.emit("sync", lambda s, c=c: s.dma_start(
                out=s0o_o.rearrange("p (u n) -> p u n", u=8)[:, :, c * CH:(c + 1) * CH],
                in_=s0o[:].rearrange("p (u n) -> p u n", u=8)),
                wait=[("dv", dv_s0)], inc=("st", 16))

            emit_ln(b, S, s0o, s1l, CH, src_tm=None, affine=(lng, lnb), out_bf=s1lb)
            ac_s1lb = b.at("ac")

            bank_guard = {0: ("dv", 0), 1: ("dv", 0)}
            for ei in range(16):
                pb = ps_a if ei % 2 == 0 else ps_b
                b.emit("tensor", mmgroup(pb[:, :],
                       [(w1[:, (ei * 8 + k) * 128:(ei * 8 + k + 1) * 128], s1lb[:, k * CH:(k + 1) * CH]) for k in range(8)]),
                       wait=[("ac", ac_s1lb), bank_guard[ei % 2]], inc="mm")
                if ei < 8:
                    b.emit("scalar", lambda s, ei=ei, pb=pb: s.activation(
                        gout[:, ei * CH:(ei + 1) * CH], pb[:, :], AF.Sigmoid, bias=bu[:, ei:ei + 1]),
                        wait="mm", inc="ac")
                    bank_guard[ei % 2] = ("ac", b.at("ac"))
                else:
                    u = ei - 8
                    b.emit("vector", lambda v, u=u, pb=pb: v.tensor_scalar(
                        vout[:, u * CH:(u + 1) * CH], pb[:, :], bv[:, u:u + 1], None, ALU.add),
                        wait="mm", inc="dv")
                    bank_guard[ei % 2] = ("dv", b.at("dv"))
            b.emit("sync", lambda s, c=c: s.dma_start(
                out=vpre_o.rearrange("p (u n) -> p u n", u=8)[:, :, c * CH:(c + 1) * CH],
                in_=vout[:].rearrange("p (u n) -> p u n", u=8)),
                wait=["dv"], inc=("st", 16))
            b.emit("sync", lambda s, c=c: s.dma_start(
                out=gg_o.rearrange("p (u n) -> p u n", u=8)[:, :, c * CH:(c + 1) * CH],
                in_=gout[:].rearrange("p (u n) -> p u n", u=8)),
                wait=["ac"], inc=("st", 16))
        b.emit("sync", lambda s: None, wait=["st"])
        b.install(block)
    return nc


def build_moeE(N, CH=256, CAP=2560):
    nc = bass.Bass("TRN2", target_bir_lowering=False)
    NCH = N // CH
    xnT = nc.dram_tensor("xnT", [128, 8 * N], F32, kind="ExternalInput").ap()
    w1p = nc.dram_tensor("w1p", [128, 32 * 8 * 128], F32, kind="ExternalInput").ap()
    w2p = nc.dram_tensor("w2p", [128, 8 * 32 * 128], F32, kind="ExternalInput").ap()
    b1p = nc.dram_tensor("b1p", [128, 32], F32, kind="ExternalInput").ap()
    b2p = nc.dram_tensor("b2p", [128, 8], F32, kind="ExternalInput").ap()
    mrow = nc.dram_tensor("mrow", [1, N], F32, kind="ExternalInput").ap()
    grow = nc.dram_tensor("grow", [1, N], F32, kind="ExternalInput").ap()
    on1 = nc.dram_tensor("on1", [1, 128], F32, kind="ExternalInput").ap()
    ctr_o = nc.dram_tensor("ctr_o", [128, 8 * N], F32, kind="ExternalOutput").ap()
    with ExitStack() as ctx:
        e = ctx.enter_context
        w1s = e(nc.sbuf_tensor("w1s", [128, 32 * 8 * 128], BF16))
        w2s = e(nc.sbuf_tensor("w2s", [128, 8 * 32 * 128], BF16))
        b1s = e(nc.sbuf_tensor("b1s", [128, 32], F32))
        b2s = e(nc.sbuf_tensor("b2s", [128, 8], F32))
        SEG = 2048
        mr = e(nc.sbuf_tensor("mr", [1, SEG], F32))
        gr = e(nc.sbuf_tensor("gr", [1, SEG], F32))
        cA = e(nc.sbuf_tensor("cA", [1, SEG], F32))
        cB = e(nc.sbuf_tensor("cB", [1, SEG], F32))
        carry = e(nc.sbuf_tensor("carry", [1, 1], F32))
        wseg = e(nc.sbuf_tensor("wseg", [1, CH], F32))
        wsc = nc.dram_tensor("wsc", [1, N], F32).ap()
        ones1 = e(nc.sbuf_tensor("ones1", [1, 128], F32))
        xnb = e(nc.sbuf_tensor("xnb", [128, 8 * CH], BF16))
        hg = e(nc.sbuf_tensor("hg", [128, 32 * CH], BF16))
        tmp = e(nc.sbuf_tensor("tmp", [128, CH], F32))
        outc = e(nc.sbuf_tensor("outc", [128, 8 * CH], F32))
        ps_w = e(nc.psum_tensor("ps_w", [128, CH], F32))
        ps_h0 = e(nc.psum_tensor("ps_h0", [128, CH], F32))
        ps_h1 = e(nc.psum_tensor("ps_h1", [128, CH], F32))
        ps_o0 = e(nc.psum_tensor("ps_o0", [128, CH], F32))
        ps_o1 = e(nc.psum_tensor("ps_o1", [128, CH], F32))
        b = B(nc, ctx)
        for sname in ("ldc", "ldg", "ldx", "mm", "dv", "ac", "st"):
            b.sem(sname)
        block = e(nc.Block())

        b.emit("gpsimd", lambda g: g.dma_start(out=w1s[:], in_=w1p[:, :]), inc=("ldg", 16))
        b.emit("gpsimd", lambda g: g.dma_start(out=w2s[:], in_=w2p[:, :]), inc=("ldg", 16))
        for dst, src in ((b1s, b1p), (b2s, b2p), (ones1, on1)):
            b.emit("sync", lambda s, d=dst, sr=src: s.dma_start(out=d[:], in_=sr[:, :]), inc=("ldc", 16))
        ld0 = [("ldc", b.at("ldc")), ("ldg", b.at("ldg"))]
        b.emit("vector", lambda v: v.memset(carry[:], 0.0), inc="dv")
        NSEG = N // SEG
        for sg in range(NSEG):
            dvv = b.at("dv")
            b.emit("sync", lambda s, sg=sg: s.dma_start(out=mr[:], in_=mrow[0:1, sg * SEG:(sg + 1) * SEG]),
                   wait=[("dv", dvv), ("st", b.at("st"))], inc=("ldx", 16))
            b.emit("sync", lambda s, sg=sg: s.dma_start(out=gr[:], in_=grow[0:1, sg * SEG:(sg + 1) * SEG]),
                   inc=("ldx", 16))
            ldw = b.at("ldx")
            b.emit("vector", lambda v: v.tensor_copy(cA[:], mr[:]), wait=[("ldx", ldw)], inc="dv")
            cur, oth = cA, cB
            sh = 1
            while sh < SEG:
                b.emit("vector", lambda v, sh=sh, cur=cur, oth=oth: v.tensor_add(
                    oth[0:1, sh:SEG], cur[0:1, sh:SEG], cur[0:1, 0:SEG - sh]), wait="dv", inc="dv")
                b.emit("vector", lambda v, sh=sh, cur=cur, oth=oth: v.tensor_copy(
                    oth[0:1, 0:sh], cur[0:1, 0:sh]), wait="dv", inc="dv")
                cur, oth = oth, cur
                sh *= 2
            b.emit("vector", lambda v, cur=cur: v.tensor_scalar(cur[:], cur[:], carry[0:1, 0:1], None, ALU.add),
                   wait="dv", inc="dv")
            b.emit("vector", lambda v, cur=cur: v.tensor_copy(carry[:], cur[0:1, SEG - 1:SEG]), wait="dv", inc="dv")
            b.emit("vector", lambda v, cur=cur, oth=oth: v.tensor_scalar(oth[:], cur[:], float(CAP), None, ALU.is_le),
                   wait="dv", inc="dv")
            b.emit("vector", lambda v, oth=oth: v.tensor_tensor(oth[:], oth[:], mr[:], ALU.mult), wait="dv", inc="dv")
            b.emit("vector", lambda v, oth=oth: v.tensor_tensor(oth[:], oth[:], gr[:], ALU.mult), wait="dv", inc="dv")
            b.emit("sync", lambda s, sg=sg, oth=oth: s.dma_start(out=wsc[0:1, sg * SEG:(sg + 1) * SEG], in_=oth[:]),
                   wait=["dv"], inc=("st", 16))
        dv_w = b.at("dv")
        st_w = b.at("st")
        for eng in ("scalar", "tensor"):
            b.emit(eng, lambda x: None, wait=ld0 + [("dv", dv_w)])

        for c in range(NCH):
            st0 = b.at("st")
            b.emit("gpsimd", lambda g, c=c: g.dma_start(
                out=xnb[:].rearrange("p (u n) -> p u n", u=8),
                in_=xnT.rearrange("p (u n) -> p u n", u=8)[:, :, c * CH:(c + 1) * CH]),
                wait=[("st", st0)] + ld0, inc=("ldg", 16))
            ldv = [("ldg", b.at("ldg")), ("st", st0)]
            for eng in ("vector", "scalar", "tensor"):
                b.emit(eng, lambda x: None, wait=ldv)
            b.emit("sync", lambda s, c=c: s.dma_start(out=wseg[:], in_=wsc[0:1, c * CH:(c + 1) * CH]),
                   wait=[("st", st_w), ("dv", b.at("dv")), ("mm", b.at("mm"))], inc=("ldx", 16))
            b.emit("tensor", lambda t, c=c: t.matmul(ps_w[:, :], ones1[0:1, :], wseg[0:1, :], start=True, stop=True),
                   wait=["dv", ("ldx", b.at("ldx"))], inc="mm")
            guard = {0: ("ac", 0), 1: ("ac", 0)}
            for hi in range(32):
                pb = ps_h0 if hi % 2 == 0 else ps_h1
                b.emit("tensor", mmgroup(pb[:, :],
                       [(w1s[:, (hi * 8 + k) * 128:(hi * 8 + k + 1) * 128], xnb[:, k * CH:(k + 1) * CH]) for k in range(8)]),
                       wait=[guard[hi % 2]], inc="mm")
                b.emit("scalar", lambda s, hi=hi, pb=pb: s.activation(
                    hg[:, hi * CH:(hi + 1) * CH], pb[:, :], AF.Gelu, bias=b1s[:, hi:hi + 1]),
                    wait="mm", inc="ac")
                guard[hi % 2] = ("ac", b.at("ac"))
            ac_hg = b.at("ac")
            guard = {0: ("dv", 0), 1: ("dv", 0)}
            for di in range(8):
                pb = ps_o0 if di % 2 == 0 else ps_o1
                b.emit("tensor", mmgroup(pb[:, :],
                       [(w2s[:, (di * 32 + k) * 128:(di * 32 + k + 1) * 128], hg[:, k * CH:(k + 1) * CH]) for k in range(32)]),
                       wait=[("ac", ac_hg), guard[di % 2]], inc="mm")
                b.emit("vector", lambda v, di=di, pb=pb: v.tensor_scalar(tmp[:, :], pb[:, :], b2s[:, di:di + 1], None, ALU.add),
                       wait="mm", inc="dv")
                b.emit("vector", lambda v, di=di: v.tensor_tensor(outc[:, di * CH:(di + 1) * CH], tmp[:, :], ps_w[:, :], ALU.mult),
                       wait="dv", inc="dv")
                guard[di % 2] = ("dv", b.at("dv"))
            b.emit("sync", lambda s, c=c: s.dma_start(
                out=ctr_o.rearrange("p (u n) -> p u n", u=8)[:, :, c * CH:(c + 1) * CH],
                in_=outc[:].rearrange("p (u n) -> p u n", u=8)),
                wait=["dv"], inc=("st", 16))
        b.emit("sync", lambda s: None, wait=["st"])
        b.install(block)
    return nc


def build_combF(ntok, CH=512):
    nc = bass.Bass("TRN2", target_bir_lowering=False)
    NCH = ntok // CH
    NTT = CH // 128
    xT = nc.dram_tensor("xT", [128, 8 * ntok], F32, kind="ExternalInput").ap()
    s0T = nc.dram_tensor("s0T", [128, 8 * ntok], F32, kind="ExternalInput").ap()
    h1T = nc.dram_tensor("h1T", [128, 8 * ntok], F32, kind="ExternalInput").ap()
    ctrT = nc.dram_tensor("ctrT", [128, 8 * 8 * ntok], F32, kind="ExternalInput").ap()
    wout = nc.dram_tensor("wout", [128, 8 * 8 * 128], F32, kind="ExternalInput").ap()
    b_out = nc.dram_tensor("b_out", [128, 8], F32, kind="ExternalInput").ap()
    id128 = nc.dram_tensor("id128", [128, 128], F32, kind="ExternalInput").ap()
    x1_o = nc.dram_tensor("x1_o", [128, 8 * ntok], F32, kind="ExternalOutput").ap()
    xn1_o = nc.dram_tensor("xn1_o", [128, 8 * ntok], F32, kind="ExternalOutput").ap()
    with ExitStack() as ctx:
        e = ctx.enter_context
        wo = e(nc.sbuf_tensor("wo", [128, 8 * 8 * 128], BF16))
        idm = e(nc.sbuf_tensor("idm", [128, 128], F32))
        ones = e(nc.sbuf_tensor("ones", [128, 1], F32))
        ones1 = e(nc.sbuf_tensor("ones1", [1, 128], F32))
        epsv = e(nc.sbuf_tensor("epsv", [128, 1], F32))
        bo = e(nc.sbuf_tensor("bo", [128, 8], F32))
        xf = e(nc.sbuf_tensor("xf", [128, 8 * CH], F32))
        s0f = e(nc.sbuf_tensor("s0f", [128, 8 * CH], F32))
        h1b = e(nc.sbuf_tensor("h1b", [128, 8 * CH], BF16))
        cbuf = e(nc.sbuf_tensor("cbuf", [128, 8 * CH], F32))
        acc = e(nc.sbuf_tensor("acc", [128, 8 * CH], F32))
        sq = e(nc.sbuf_tensor("sq", [128, 8 * CH], F32))
        xn1 = e(nc.sbuf_tensor("xn1", [128, 8 * CH], F32))
        stat = e(nc.sbuf_tensor("stat", [128, 8 * NTT], F32))
        murow = e(nc.sbuf_tensor("murow", [1, 128 * NTT], F32))
        rsrow = e(nc.sbuf_tensor("rsrow", [1, 128 * NTT], F32))
        ps_st = e(nc.psum_tensor("ps_st", [128, 2], F32))
        ps_tr = e(nc.psum_tensor("ps_tr", [1, 128], F32))
        ps_tr2 = e(nc.psum_tensor("ps_tr2", [1, 128], F32))
        ps_bc = e(nc.psum_tensor("ps_bc", [128, 256], F32))
        ps_a = e(nc.psum_tensor("ps_a", [128, CH], F32))
        ps_b = e(nc.psum_tensor("ps_b", [128, CH], F32))
        b = B(nc, ctx)
        for sname in ("ldc", "ldg", "ldx", "mm", "dv", "ac", "st"):
            b.sem(sname)
        block = e(nc.Block())
        S = dict(ones=ones, ones1=ones1, idm=idm, sq=sq, stat=stat, murow=murow,
                 rsrow=rsrow, ps_st=ps_st, ps_tr=ps_tr, ps_tr2=ps_tr2, ps_bc=ps_bc, epsv=epsv)

        for dst, src in ((idm, id128), (bo, b_out)):
            b.emit("sync", lambda s, d=dst, sr=src: s.dma_start(out=d[:], in_=sr[:, :]), inc=("ldc", 16))
        b.emit("gpsimd", lambda g: g.dma_start(out=wo[:], in_=wout[:, :]), inc=("ldg", 16))
        ld0 = [("ldc", b.at("ldc")), ("ldg", b.at("ldg"))]
        b.emit("vector", lambda v: v.memset(ones[:], 1.0))
        b.emit("vector", lambda v: v.memset(ones1[:], 1.0))
        b.emit("vector", lambda v: v.memset(epsv[:], EPS), inc="dv")
        for eng in ("scalar", "tensor", "vector"):
            b.emit(eng, lambda x: None, wait=ld0 + ["dv"])

        for c in range(NCH):
            st0 = b.at("st")
            b.emit("sync", lambda s, c=c: s.dma_start(
                out=xf[:].rearrange("p (u n) -> p u n", u=8),
                in_=xT.rearrange("p (u n) -> p u n", u=8)[:, :, c * CH:(c + 1) * CH]),
                wait=[("st", st0)] + ld0, inc=("ldx", 16))
            b.emit("sync", lambda s, c=c: s.dma_start(
                out=s0f[:].rearrange("p (u n) -> p u n", u=8),
                in_=s0T.rearrange("p (u n) -> p u n", u=8)[:, :, c * CH:(c + 1) * CH]),
                inc=("ldx", 16))
            b.emit("gpsimd", lambda g, c=c: g.dma_start(
                out=h1b[:].rearrange("p (u n) -> p u n", u=8),
                in_=h1T.rearrange("p (u n) -> p u n", u=8)[:, :, c * CH:(c + 1) * CH]),
                wait=[("st", st0)] + ld0, inc=("ldg", 16))
            ldv = [("ldx", b.at("ldx")), ("ldg", b.at("ldg")), ("st", st0)]
            for eng in ("vector", "scalar", "tensor"):
                b.emit(eng, lambda x: None, wait=ldv)
            for u in range(8):
                b.emit("vector", lambda v, u=u: v.tensor_add(acc[:, u * CH:(u + 1) * CH], xf[:, u * CH:(u + 1) * CH], s0f[:, u * CH:(u + 1) * CH]),
                       wait=("dv" if u == 0 else ()), inc="dv")
            for ee in range(8):
                dvv = b.at("dv")
                b.emit("sync", lambda s, c=c, ee=ee: s.dma_start(
                    out=cbuf[:].rearrange("p (u n) -> p u n", u=8),
                    in_=ctrT.rearrange("p (e u n) -> p e u n", e=8, u=8)[:, ee, :, c * CH:(c + 1) * CH]),
                    wait=[("dv", dvv)], inc=("ldx", 16))
                b.emit("vector", lambda x2: None, wait=[("ldx", b.at("ldx"))])
                for u in range(8):
                    b.emit("vector", lambda v, u=u: v.tensor_add(acc[:, u * CH:(u + 1) * CH], acc[:, u * CH:(u + 1) * CH], cbuf[:, u * CH:(u + 1) * CH]),
                           inc="dv")
            guard = {0: ("dv", 0), 1: ("dv", 0)}
            for ei in range(8):
                pb = ps_a if ei % 2 == 0 else ps_b
                b.emit("tensor", mmgroup(pb[:, :],
                       [(wo[:, (ei * 8 + k) * 128:(ei * 8 + k + 1) * 128], h1b[:, k * CH:(k + 1) * CH]) for k in range(8)]),
                       wait=[guard[ei % 2]], inc="mm")
                b.emit("vector", lambda v, ei=ei, pb=pb: v.tensor_scalar(sq[:, ei * CH:(ei + 1) * CH], pb[:, :], bo[:, ei:ei + 1], None, ALU.add),
                       wait=["mm", "dv"], inc="dv")
                b.emit("vector", lambda v, ei=ei: v.tensor_add(acc[:, ei * CH:(ei + 1) * CH], acc[:, ei * CH:(ei + 1) * CH], sq[:, ei * CH:(ei + 1) * CH]),
                       wait="dv", inc="dv")
                guard[ei % 2] = ("dv", b.at("dv"))
            dv_x1 = b.at("dv")
            b.emit("sync", lambda s, c=c: s.dma_start(
                out=x1_o.rearrange("p (u n) -> p u n", u=8)[:, :, c * CH:(c + 1) * CH],
                in_=acc[:].rearrange("p (u n) -> p u n", u=8)),
                wait=[("dv", dv_x1)], inc=("st", 16))
            emit_ln(b, S, acc, xn1, CH, src_tm=None, affine=None)
            b.emit("sync", lambda s, c=c: s.dma_start(
                out=xn1_o.rearrange("p (u n) -> p u n", u=8)[:, :, c * CH:(c + 1) * CH],
                in_=xn1[:].rearrange("p (u n) -> p u n", u=8)),
                wait=["dv"], inc=("st", 16))
        b.emit("sync", lambda s: None, wait=["st"])
        b.install(block)
    return nc


def build_qkvG1(ntok, CH=512):
    nc = bass.Bass("TRN2", target_bir_lowering=False)
    NCH = ntok // CH
    xn1T = nc.dram_tensor("xn1T", [128, 8 * ntok], F32, kind="ExternalInput").ap()
    wdq = nc.dram_tensor("wdq", [128, 32 * 8 * 128], F32, kind="ExternalInput").ap()
    wuq = nc.dram_tensor("wuq", [128, 8 * 32 * 128], F32, kind="ExternalInput").ap()
    wqr = nc.dram_tensor("wqr", [128, 4 * 32 * 128], F32, kind="ExternalInput").ap()
    wdkv = nc.dram_tensor("wdkv", [128, 2 * 8 * 128], F32, kind="ExternalInput").ap()
    wuk = nc.dram_tensor("wuk", [128, 8 * 2 * 128], F32, kind="ExternalInput").ap()
    wuv = nc.dram_tensor("wuv", [128, 8 * 2 * 128], F32, kind="ExternalInput").ap()
    wkr = nc.dram_tensor("wkr", [128, 1 * 8 * 32], F32, kind="ExternalInput").ap()
    scq = nc.dram_tensor("scq", [128, 4 * ntok], F32, kind="ExternalInput").ap()
    sck = nc.dram_tensor("sck", [32, ntok], F32, kind="ExternalInput").ap()
    qc_o = nc.dram_tensor("qc_o", [128, 8 * ntok], F32, kind="ExternalOutput").ap()
    qr_o = nc.dram_tensor("qr_o", [128, 4 * ntok], F32, kind="ExternalOutput").ap()
    kc_o = nc.dram_tensor("kc_o", [128, 8 * ntok], F32, kind="ExternalOutput").ap()
    kr_o = nc.dram_tensor("kr_o", [32, ntok], F32, kind="ExternalOutput").ap()
    v_o = nc.dram_tensor("v_o", [128, 8 * ntok], F32, kind="ExternalOutput").ap()
    with ExitStack() as ctx:
        e = ctx.enter_context
        wdqs = e(nc.sbuf_tensor("wdqs", [128, 32 * 8 * 128], BF16))
        wuqs = e(nc.sbuf_tensor("wuqs", [128, 8 * 32 * 128], BF16))
        wqrs = e(nc.sbuf_tensor("wqrs", [128, 4 * 32 * 128], BF16))
        wdkvs = e(nc.sbuf_tensor("wdkvs", [128, 2 * 8 * 128], BF16))
        wuks = e(nc.sbuf_tensor("wuks", [128, 8 * 2 * 128], BF16))
        wuvs = e(nc.sbuf_tensor("wuvs", [128, 8 * 2 * 128], BF16))
        wkrs = e(nc.sbuf_tensor("wkrs", [128, 8 * 32], BF16))
        scqs = e(nc.sbuf_tensor("scqs", [128, 4 * CH], F32))
        scks = e(nc.sbuf_tensor("scks", [32, CH], F32))
        xnb = e(nc.sbuf_tensor("xnb", [128, 8 * CH], BF16))
        cqb = e(nc.sbuf_tensor("cqb", [128, 32 * CH], BF16))
        ckvb = e(nc.sbuf_tensor("ckvb", [128, 2 * CH], BF16))
        ot = e(nc.sbuf_tensor("ot", [128, 8 * CH], F32))
        ps_a = e(nc.psum_tensor("ps_a", [128, CH], F32))
        ps_b = e(nc.psum_tensor("ps_b", [128, CH], F32))
        b = B(nc, ctx)
        for sname in ("ldc", "ldg", "ldx", "mm", "dv", "ac", "st"):
            b.sem(sname)
        block = e(nc.Block())
        for dst, src in ((wdqs, wdq), (wuqs, wuq), (wqrs, wqr), (wdkvs, wdkv),
                         (wuks, wuk), (wuvs, wuv), (wkrs, wkr)):
            b.emit("gpsimd", lambda g, d=dst, sr=src: g.dma_start(out=d[:], in_=sr[:, :]), inc=("ldg", 16))
        ld0 = [("ldg", b.at("ldg"))]
        for eng in ("scalar", "tensor", "vector"):
            b.emit(eng, lambda x: None, wait=ld0)

        for c in range(NCH):
            st0 = b.at("st")
            b.emit("gpsimd", lambda g, c=c: g.dma_start(
                out=xnb[:].rearrange("p (u n) -> p u n", u=8),
                in_=xn1T.rearrange("p (u n) -> p u n", u=8)[:, :, c * CH:(c + 1) * CH]),
                wait=[("st", st0)] + ld0, inc=("ldg", 16))
            b.emit("sync", lambda s, c=c: s.dma_start(
                out=scqs[:].rearrange("p (u n) -> p u n", u=4),
                in_=scq.rearrange("p (u n) -> p u n", u=4)[:, :, c * CH:(c + 1) * CH]),
                wait=[("st", st0)], inc=("ldx", 16))
            b.emit("sync", lambda s, c=c: s.dma_start(
                out=scks[:], in_=sck[:, c * CH:(c + 1) * CH]), inc=("ldx", 16))
            ldv = [("ldx", b.at("ldx")), ("ldg", b.at("ldg")), ("st", st0)]
            for eng in ("vector", "scalar", "tensor"):
                b.emit(eng, lambda x: None, wait=ldv)

            guard = {0: ("ac", 0), 1: ("ac", 0)}
            for ei in range(32):
                pb = ps_a if ei % 2 == 0 else ps_b
                b.emit("tensor", mmgroup(pb[:, :],
                       [(wdqs[:, (ei * 8 + k) * 128:(ei * 8 + k + 1) * 128], xnb[:, k * CH:(k + 1) * CH]) for k in range(8)]),
                       wait=[guard[ei % 2]], inc="mm")
                b.emit("scalar", lambda s, ei=ei, pb=pb: s.copy(cqb[:, ei * CH:(ei + 1) * CH], pb[:, :]),
                       wait="mm", inc="ac")
                guard[ei % 2] = ("ac", b.at("ac"))
            ac_cq = b.at("ac")
            guard = {0: ("dv", 0), 1: ("dv", 0)}
            for ei in range(8):
                pb = ps_a if ei % 2 == 0 else ps_b
                b.emit("tensor", mmgroup(pb[:, :],
                       [(wuqs[:, (ei * 32 + k) * 128:(ei * 32 + k + 1) * 128], cqb[:, k * CH:(k + 1) * CH]) for k in range(32)]),
                       wait=[("ac", ac_cq), guard[ei % 2]], inc="mm")
                b.emit("vector", lambda v, ei=ei, pb=pb: v.tensor_copy(ot[:, ei * CH:(ei + 1) * CH], pb[:, :]),
                       wait=["mm", "dv"], inc="dv")
                guard[ei % 2] = ("dv", b.at("dv"))
            b.emit("sync", lambda s, c=c: s.dma_start(
                out=qc_o.rearrange("p (u n) -> p u n", u=8)[:, :, c * CH:(c + 1) * CH],
                in_=ot[:].rearrange("p (u n) -> p u n", u=8)), wait=["dv"], inc=("st", 16))
            guard = {0: ("dv", b.at("dv")), 1: ("dv", b.at("dv"))}
            for ei in range(4):
                pb = ps_a if ei % 2 == 0 else ps_b
                b.emit("tensor", mmgroup(pb[:, :],
                       [(wqrs[:, (ei * 32 + k) * 128:(ei * 32 + k + 1) * 128], cqb[:, k * CH:(k + 1) * CH]) for k in range(32)]),
                       wait=[("ac", ac_cq), guard[ei % 2]], inc="mm")
                b.emit("vector", lambda v, ei=ei, pb=pb: v.tensor_tensor(ot[:, ei * CH:(ei + 1) * CH], pb[:, :], scqs[:, ei * CH:(ei + 1) * CH], ALU.mult),
                       wait=["mm", "dv"], inc="dv")
                guard[ei % 2] = ("dv", b.at("dv"))
            b.emit("sync", lambda s, c=c: s.dma_start(
                out=qr_o.rearrange("p (u n) -> p u n", u=4)[:, :, c * CH:(c + 1) * CH],
                in_=ot[:, 0:4 * CH].rearrange("p (u n) -> p u n", u=4)), wait=["dv"], inc=("st", 16))
            guard = {0: ("dv", b.at("dv")), 1: ("dv", b.at("dv"))}
            for ei in range(2):
                pb = ps_a if ei % 2 == 0 else ps_b
                b.emit("tensor", mmgroup(pb[:, :],
                       [(wdkvs[:, (ei * 8 + k) * 128:(ei * 8 + k + 1) * 128], xnb[:, k * CH:(k + 1) * CH]) for k in range(8)]),
                       wait=[guard[ei % 2]], inc="mm")
                b.emit("scalar", lambda s, ei=ei, pb=pb: s.copy(ckvb[:, ei * CH:(ei + 1) * CH], pb[:, :]),
                       wait="mm", inc="ac")
                guard[ei % 2] = ("ac", b.at("ac"))
            ac_ckv = b.at("ac")
            for wsrc, dst_o in ((wuks, kc_o), (wuvs, v_o)):
                guard = {0: ("dv", b.at("dv")), 1: ("dv", b.at("dv"))}
                for ei in range(8):
                    pb = ps_a if ei % 2 == 0 else ps_b
                    b.emit("tensor", mmgroup(pb[:, :],
                           [(wsrc[:, (ei * 2 + k) * 128:(ei * 2 + k + 1) * 128], ckvb[:, k * CH:(k + 1) * CH]) for k in range(2)]),
                           wait=[("ac", ac_ckv), guard[ei % 2]], inc="mm")
                    b.emit("vector", lambda v, ei=ei, pb=pb: v.tensor_copy(ot[:, ei * CH:(ei + 1) * CH], pb[:, :]),
                           wait=["mm", "dv"], inc="dv")
                    guard[ei % 2] = ("dv", b.at("dv"))
                b.emit("sync", lambda s, c=c, dst_o=dst_o: s.dma_start(
                    out=dst_o.rearrange("p (u n) -> p u n", u=8)[:, :, c * CH:(c + 1) * CH],
                    in_=ot[:].rearrange("p (u n) -> p u n", u=8)), wait=["dv"], inc=("st", 16))
            b.emit("tensor", mmgroup(ps_a[0:32, :],
                   [(wkrs[:, k * 32:(k + 1) * 32], xnb[:, k * CH:(k + 1) * CH]) for k in range(8)]),
                   wait=["dv"], inc="mm")
            b.emit("vector", lambda v: v.tensor_tensor(ot[0:32, 0:CH], ps_a[0:32, :], scks[0:32, :], ALU.mult),
                   wait=["mm", "dv"], inc="dv")
            b.emit("sync", lambda s, c=c: s.dma_start(
                out=kr_o[:, c * CH:(c + 1) * CH], in_=ot[0:32, 0:CH]), wait=["dv"], inc=("st", 16))
        b.emit("sync", lambda s: None, wait=["st"])
        b.install(block)
    return nc


def build_ffnG2(ntok, CH=256):
    nc = bass.Bass("TRN2", target_bir_lowering=False)
    NCH = ntok // CH
    xn1T = nc.dram_tensor("xn1T", [128, 8 * ntok], F32, kind="ExternalInput").ap()
    x1T = nc.dram_tensor("x1T", [128, 8 * ntok], F32, kind="ExternalInput").ap()
    w1p = nc.dram_tensor("w1p", [128, 32 * 8 * 128], F32, kind="ExternalInput").ap()
    w2p = nc.dram_tensor("w2p", [128, 8 * 32 * 128], F32, kind="ExternalInput").ap()
    b1p = nc.dram_tensor("b1p", [128, 32], F32, kind="ExternalInput").ap()
    b2p = nc.dram_tensor("b2p", [128, 8], F32, kind="ExternalInput").ap()
    op_o = nc.dram_tensor("op_o", [128, 8 * ntok], F32, kind="ExternalOutput").ap()
    with ExitStack() as ctx:
        e = ctx.enter_context
        w1f = e(nc.sbuf_tensor("w1f", [128, 32 * 8 * 128], BF16))
        w2f = e(nc.sbuf_tensor("w2f", [128, 8 * 32 * 128], BF16))
        wbuf = e(nc.sbuf_tensor("wbuf", [128, 32 * 128], F32))
        abuf = e(nc.sbuf_tensor("abuf", [128, 32 * 128], BF16))
        b1s = e(nc.sbuf_tensor("b1s", [128, 32], F32))
        b2s = e(nc.sbuf_tensor("b2s", [128, 8], F32))
        al1 = e(nc.sbuf_tensor("al1", [128, 32], F32))
        al2 = e(nc.sbuf_tensor("al2", [128, 8], F32))
        ones = e(nc.sbuf_tensor("ones", [128, 1], BF16))
        xnb = e(nc.sbuf_tensor("xnb", [128, 8 * CH], BF16))
        x1f = e(nc.sbuf_tensor("x1f", [128, 8 * CH], F32))
        hgb = e(nc.sbuf_tensor("hgb", [128, 32 * CH], BF16))
        tmp = e(nc.sbuf_tensor("tmp", [128, CH], F32))
        ot = e(nc.sbuf_tensor("ot", [128, 8 * CH], F32))
        ps_a = e(nc.psum_tensor("ps_a", [128, CH], F32))
        ps_b = e(nc.psum_tensor("ps_b", [128, CH], F32))
        ps_al = e(nc.psum_tensor("ps_al", [128, 1], F32))
        b = B(nc, ctx)
        for sname in ("ldc", "ldg", "ldx", "mm", "dv", "ac", "st"):
            b.sem(sname)
        block = e(nc.Block())

        b.emit("sync", lambda s: s.dma_start(out=b1s[:], in_=b1p[:, :]), inc=("ldc", 16))
        b.emit("sync", lambda s: s.dma_start(out=b2s[:], in_=b2p[:, :]), inc=("ldc", 16))
        b.emit("vector", lambda v: v.memset(ones[:], 1.0), inc="dv")
        def prep(wp, wf, nI, nK, alo, denom):
            for i in range(nI):
                acv, mmv = b.at("ac"), b.at("mm")
                b.emit("sync", lambda s, i=i, wp=wp, nK=nK: s.dma_start(
                    out=wbuf[:, 0:nK * 128], in_=wp[:, i * nK * 128:(i + 1) * nK * 128]),
                    wait=[("ac", acv), ("mm", mmv)], inc=("ldx", 16))
                ldw = b.at("ldx")
                b.emit("scalar", lambda s, i=i, wf=wf, nK=nK: s.activation(
                    wf[:, i * nK * 128:(i + 1) * nK * 128], wbuf[:, 0:nK * 128], AF.Sign),
                    wait=[("ldx", ldw)], inc="ac")
                b.emit("scalar", lambda s, nK=nK: s.activation(abuf[:, 0:nK * 128], wbuf[:, 0:nK * 128], AF.Abs),
                    wait="ac", inc="ac")
                b.emit("tensor", mmgroup(ps_al[:, :],
                       [(abuf[:, k * 128:(k + 1) * 128], ones[:, :]) for k in range(nK)]),
                       wait=["ac", "dv"], inc="mm")
                b.emit("vector", lambda v, i=i, alo=alo, denom=denom: v.tensor_scalar(
                    alo[:, i:i + 1], ps_al[:, :], 1.0 / denom, None, ALU.mult),
                    wait="mm", inc="dv")
        prep(w1p, w1f, 32, 8, al1, 1024.0)
        prep(w2p, w2f, 8, 32, al2, 4096.0)
        ld0 = [("ldc", b.at("ldc")), ("dv", b.at("dv")), ("ac", b.at("ac"))]
        for eng in ("scalar", "tensor", "vector"):
            b.emit(eng, lambda x: None, wait=ld0)

        for c in range(NCH):
            st0 = b.at("st")
            b.emit("gpsimd", lambda g, c=c: g.dma_start(
                out=xnb[:].rearrange("p (u n) -> p u n", u=8),
                in_=xn1T.rearrange("p (u n) -> p u n", u=8)[:, :, c * CH:(c + 1) * CH]),
                wait=[("st", st0)] + ld0, inc=("ldg", 16))
            b.emit("sync", lambda s, c=c: s.dma_start(
                out=x1f[:].rearrange("p (u n) -> p u n", u=8),
                in_=x1T.rearrange("p (u n) -> p u n", u=8)[:, :, c * CH:(c + 1) * CH]),
                wait=[("st", st0)], inc=("ldx", 16))
            ldv = [("ldx", b.at("ldx")), ("ldg", b.at("ldg")), ("st", st0)]
            for eng in ("vector", "scalar", "tensor"):
                b.emit(eng, lambda x: None, wait=ldv)
            guard = {0: ("ac", 0), 1: ("ac", 0)}
            for hi in range(32):
                pb = ps_a if hi % 2 == 0 else ps_b
                b.emit("tensor", mmgroup(pb[:, :],
                       [(w1f[:, (hi * 8 + k) * 128:(hi * 8 + k + 1) * 128], xnb[:, k * CH:(k + 1) * CH]) for k in range(8)]),
                       wait=[guard[hi % 2]], inc="mm")
                b.emit("vector", lambda v, hi=hi, pb=pb: v.tensor_scalar(
                    tmp[:, :], pb[:, :], al1[:, hi:hi + 1], b1s[:, hi:hi + 1], ALU.mult, ALU.add),
                    wait=["mm", "dv", "ac"], inc="dv")
                b.emit("scalar", lambda s, hi=hi: s.activation(hgb[:, hi * CH:(hi + 1) * CH], tmp[:, :], AF.Gelu),
                       wait="dv", inc="ac")
                guard[hi % 2] = ("ac", b.at("ac"))
            ac_hg = b.at("ac")
            guard = {0: ("dv", 0), 1: ("dv", 0)}
            for di in range(8):
                pb = ps_a if di % 2 == 0 else ps_b
                b.emit("tensor", mmgroup(pb[:, :],
                       [(w2f[:, (di * 32 + k) * 128:(di * 32 + k + 1) * 128], hgb[:, k * CH:(k + 1) * CH]) for k in range(32)]),
                       wait=[("ac", ac_hg), guard[di % 2]], inc="mm")
                b.emit("vector", lambda v, di=di, pb=pb: v.tensor_scalar(
                    ot[:, di * CH:(di + 1) * CH], pb[:, :], al2[:, di:di + 1], b2s[:, di:di + 1], ALU.mult, ALU.add),
                    wait=["mm", "dv"], inc="dv")
                b.emit("vector", lambda v, di=di: v.tensor_add(
                    ot[:, di * CH:(di + 1) * CH], ot[:, di * CH:(di + 1) * CH], x1f[:, di * CH:(di + 1) * CH]),
                    wait="dv", inc="dv")
                guard[di % 2] = ("dv", b.at("dv"))
            b.emit("sync", lambda s, c=c: s.dma_start(
                out=op_o.rearrange("p (u n) -> p u n", u=8)[:, :, c * CH:(c + 1) * CH],
                in_=ot[:].rearrange("p (u n) -> p u n", u=8)), wait=["dv"], inc=("st", 16))
        b.emit("sync", lambda s: None, wait=["st"])
        b.install(block)
    return nc


def build_attnH(T=8192, NH=4):
    nc = bass.Bass("TRN2", target_bir_lowering=False)
    NQ = T // 128
    NKM = T // 512
    qd = [nc.dram_tensor(f"q{h}", [96, T], F32, kind="ExternalInput").ap() for h in range(NH)]
    kd = [nc.dram_tensor(f"k{h}", [96, T], F32, kind="ExternalInput").ap() for h in range(NH)]
    vd = [nc.dram_tensor(f"v{h}", [T, 64], F32, kind="ExternalInput").ap() for h in range(NH)]
    msk = nc.dram_tensor("msk", [128, 4 * 512], F32, kind="ExternalInput").ap()
    idd = nc.dram_tensor("idd", [128, 128], F32, kind="ExternalInput").ap()
    yo = [nc.dram_tensor(f"y{h}", [T, 64], F32, kind="ExternalOutput").ap() for h in range(NH)]
    SCALE = 1.0 / np.sqrt(96.0)
    with ExitStack() as ctx:
        e = ctx.enter_context
        qs = e(nc.sbuf_tensor("qs", [96, T], BF16))
        ks = e(nc.sbuf_tensor("ks", [96, T], BF16))
        vs = e(nc.sbuf_tensor("vs", [128, (T // 128) * 64], BF16))
        mks = e(nc.sbuf_tensor("mks", [128, 4 * 512], F32))
        idb = e(nc.sbuf_tensor("idb", [128, 128], BF16))
        sm = e(nc.sbuf_tensor("sm", [128, T], F32))
        pb_ = e(nc.sbuf_tensor("pb_", [128, T], BF16))
        pt = e(nc.sbuf_tensor("pt", [128, 512], BF16))
        osb = e(nc.sbuf_tensor("osb", [128, 64], F32))
        red = e(nc.sbuf_tensor("red", [128, 4], F32))
        ps_s = e(nc.psum_tensor("ps_s", [128, 512], F32))
        ps_t = e(nc.psum_tensor("ps_t", [128, 512], BF16))
        ps_pv = e(nc.psum_tensor("ps_pv", [128, 64], F32))
        b = B(nc, ctx)
        for sname in ("ldg", "mm", "dv", "ac", "st"):
            b.sem(sname)
        block = e(nc.Block())
        b.emit("gpsimd", lambda g: g.dma_start(out=mks[:], in_=msk[:, :]), inc=("ldg", 16))
        b.emit("gpsimd", lambda g: g.dma_start(out=idb[:], in_=idd[:, :]), inc=("ldg", 16))

        for h in range(NH):
            stH = b.at("st")
            dvH, acH, mmH = b.at("dv"), b.at("ac"), b.at("mm")
            b.emit("gpsimd", lambda g, h=h: g.dma_start(out=qs[0:96, :], in_=qd[h][:, :]),
                   wait=[("st", stH), ("dv", dvH), ("mm", mmH)], inc=("ldg", 16))
            b.emit("gpsimd", lambda g, h=h: g.dma_start(out=ks[0:96, :], in_=kd[h][:, :]), inc=("ldg", 16))
            b.emit("gpsimd", lambda g, h=h: g.dma_start(
                out=vs[:].rearrange("p (j d) -> p j d", d=64),
                in_=vd[h].rearrange("(j p) d -> p j d", p=128)), inc=("ldg", 16))
            ldv = [("ldg", b.at("ldg"))]
            for eng in ("vector", "scalar", "tensor"):
                b.emit(eng, lambda x: None, wait=ldv)
            for qi in range(NQ):
                nkm = qi // 4 + 1
                nk = nkm * 512
                dg = qi // 4
                r = qi % 4
                for km in range(nkm):
                    b.emit("tensor", lambda t, qi=qi, km=km: t.matmul(
                        ps_s[:, :], qs[0:96, 128 * qi:128 * (qi + 1)], ks[0:96, 512 * km:512 * (km + 1)],
                        start=True, stop=True), wait=["dv"], inc="mm")
                    if km == dg:
                        b.emit("vector", lambda v, km=km, r=r: v.scalar_tensor_tensor(
                            sm[:, 512 * km:512 * (km + 1)], ps_s[:, :], SCALE, mks[:, 512 * r:512 * (r + 1)],
                            ALU.mult, ALU.add), wait=["mm", "ac"], inc="dv")
                    else:
                        b.emit("vector", lambda v, km=km: v.tensor_scalar(
                            sm[:, 512 * km:512 * (km + 1)], ps_s[:, :], SCALE, None, ALU.mult),
                            wait=["mm", "ac"], inc="dv")
                b.emit("vector", lambda v, nk=nk: v.reduce_max(red[:, 0:1], sm[:, 0:nk], axis=AX), wait="dv", inc="dv")
                b.emit("vector", lambda v: v.tensor_scalar(red[:, 1:2], red[:, 0:1], -1.0, None, ALU.mult),
                       wait="dv", inc="dv")
                b.emit("scalar", lambda s, nk=nk: s.activation(pb_[:, 0:nk], sm[:, 0:nk], AF.Exp, bias=red[:, 1:2]),
                       wait="dv", inc="ac")
                b.emit("vector", lambda v, nk=nk: v.reduce_sum(red[:, 2:3], pb_[:, 0:nk], axis=AX), wait="ac", inc="dv")
                b.emit("vector", lambda v: v.reciprocal(red[:, 3:4], red[:, 2:3]), wait="dv", inc="dv")
                nkt = nk // 128
                for kt in range(nkt):
                    sl = kt % 4
                    if sl == 0:
                        dvv = b.at("dv")
                        nsl = min(4, nkt - kt)
                        b.emit("tensor", lambda t, qi=qi, kt=kt, nsl=nsl: [
                            t.transpose(ps_t[:, 128 * j:128 * (j + 1)], pb_[:, 128 * (kt + j):128 * (kt + j + 1)], idb[:, :])
                            for j in range(nsl)][-1], wait=[("dv", dvv), "ac"], inc="mm")
                        b.emit("vector", lambda v, nsl=nsl: v.tensor_copy(pt[:, 0:128 * nsl], ps_t[:, 0:128 * nsl]),
                               wait="mm", inc="dv")
                    b.emit("tensor", lambda t, kt=kt, sl=sl, qi=qi: t.matmul(
                        ps_pv[:, :], pt[:, 128 * sl:128 * (sl + 1)], vs[:, 64 * kt:64 * (kt + 1)],
                        start=(kt == 0), stop=(kt == nkt - 1)),
                        wait=(["dv"] if sl == 0 else ()), inc=("mm" if kt == nkt - 1 else None))
                b.emit("vector", lambda v: v.tensor_scalar(osb[:, :], ps_pv[:, :], red[:, 3:4], None, ALU.mult),
                       wait=["mm", "dv"], inc="dv")
                b.emit("sync", lambda s, h=h, qi=qi: s.dma_start(
                    out=yo[h][128 * qi:128 * (qi + 1), :], in_=osb[:, :]), wait=["dv"], inc=("st", 16))
        b.emit("sync", lambda s: None, wait=["st"])
        b.install(block)
    return nc


def build_finalI(ntok, CH=512):
    nc = bass.Bass("TRN2", target_bir_lowering=False)
    NCH = ntok // CH
    yaT = nc.dram_tensor("yaT", [128, 8 * ntok], F32, kind="ExternalInput").ap()
    opT = nc.dram_tensor("opT", [128, 8 * ntok], F32, kind="ExternalInput").ap()
    wop = nc.dram_tensor("wop", [128, 8 * 8 * 128], F32, kind="ExternalInput").ap()
    out_o = nc.dram_tensor("out_o", [128, 8 * ntok], F32, kind="ExternalOutput").ap()
    with ExitStack() as ctx:
        e = ctx.enter_context
        wo = e(nc.sbuf_tensor("wo", [128, 8 * 8 * 128], BF16))
        yab = e(nc.sbuf_tensor("yab", [128, 8 * CH], BF16))
        opf = e(nc.sbuf_tensor("opf", [128, 8 * CH], F32))
        ot = e(nc.sbuf_tensor("ot", [128, 8 * CH], F32))
        ps_a = e(nc.psum_tensor("ps_a", [128, CH], F32))
        ps_b = e(nc.psum_tensor("ps_b", [128, CH], F32))
        b = B(nc, ctx)
        for sname in ("ldc", "ldg", "ldx", "mm", "dv", "ac", "st"):
            b.sem(sname)
        block = e(nc.Block())
        b.emit("gpsimd", lambda g: g.dma_start(out=wo[:], in_=wop[:, :]), inc=("ldg", 16))
        ld0 = [("ldg", b.at("ldg"))]
        for c in range(NCH):
            st0 = b.at("st")
            b.emit("gpsimd", lambda g, c=c: g.dma_start(
                out=yab[:].rearrange("p (u n) -> p u n", u=8),
                in_=yaT.rearrange("p (u n) -> p u n", u=8)[:, :, c * CH:(c + 1) * CH]),
                wait=[("st", st0)] + ld0, inc=("ldg", 16))
            b.emit("sync", lambda s, c=c: s.dma_start(
                out=opf[:].rearrange("p (u n) -> p u n", u=8),
                in_=opT.rearrange("p (u n) -> p u n", u=8)[:, :, c * CH:(c + 1) * CH]),
                wait=[("st", st0)], inc=("ldx", 16))
            ldv = [("ldx", b.at("ldx")), ("ldg", b.at("ldg")), ("st", st0)]
            for eng in ("vector", "scalar", "tensor"):
                b.emit(eng, lambda x: None, wait=ldv)
            guard = {0: ("dv", 0), 1: ("dv", 0)}
            for ei in range(8):
                pb = ps_a if ei % 2 == 0 else ps_b
                b.emit("tensor", mmgroup(pb[:, :],
                       [(wo[:, (ei * 8 + k) * 128:(ei * 8 + k + 1) * 128], yab[:, k * CH:(k + 1) * CH]) for k in range(8)]),
                       wait=[guard[ei % 2]], inc="mm")
                b.emit("vector", lambda v, ei=ei, pb=pb: v.tensor_add(
                    ot[:, ei * CH:(ei + 1) * CH], pb[:, :], opf[:, ei * CH:(ei + 1) * CH]),
                    wait=["mm", "dv"], inc="dv")
                guard[ei % 2] = ("dv", b.at("dv"))
            b.emit("sync", lambda s, c=c: s.dma_start(
                out=out_o.rearrange("p (u n) -> p u n", u=8)[:, :, c * CH:(c + 1) * CH],
                in_=ot[:].rearrange("p (u n) -> p u n", u=8)), wait=["dv"], inc=("st", 16))
        b.emit("sync", lambda s: None, wait=["st"])
        b.install(block)
    return nc


"""Orchestration: multi-launch kernel using klib builders."""

N = 16384
T = 8192
NTOK = 2048
CORES = list(range(8))


def assemble_scan(parts):
    a = np.stack([p.reshape(128, 8, NTOK) for p in parts])
    a = a.reshape(2, 4, 128, 8, NTOK)
    a = a.transpose(2, 1, 4, 3, 0)
    return np.ascontiguousarray(a.reshape(128, T * 16))


def slice_scan(h, c):
    b, tc = c // 4, c % 4
    a = h.reshape(128, T, 8, 2)[:, tc * NTOK:(tc + 1) * NTOK, :, b]
    return np.ascontiguousarray(a.transpose(0, 2, 1).reshape(128, 8 * NTOK))


def rope_scale():
    t = np.arange(T, dtype=np.float32)
    inv = 1.0 / 10000.0 ** (np.arange(0, 32, 2, dtype=np.float32) / 32.0)
    fr = t[:, None] * inv[None, :]
    emb = np.concatenate([fr, fr], -1)
    return (np.cos(emb) + np.sin(emb)).astype(np.float32)


def kernel(**inp):
    x = np.asarray(inp["L0_attn_x"], dtype=np.float32)
    xflat = np.ascontiguousarray(x.reshape(N, 1024))
    ln_g, ln_b = inp["L0_attn_ln_g"], inp["L0_attn_ln_b"]
    W_in, b_in = inp["L0_attn_W_in"], inp["L0_attn_b_in"]
    W_h, b_h = inp["L0_attn_W_h"], inp["L0_attn_b_h"]
    W_out, b_out = inp["L0_attn_W_out"], inp["L0_attn_b_out"]

    ncA = build_preA(NTOK)
    imsA = []
    for c in CORES:
        im = preA_inputs(xflat[c * NTOK:(c + 1) * NTOK], W_in[0], ln_g[0], ln_b[0],
                         b_in[0], b_h[0], inp["L0_ffn_router_w"], inp["L0_ffn_router_b"])
        imsA.append(im)
    rA = run_bass_kernel_spmd(ncA, imsA, core_ids=CORES).results
    xn_parts = [r["xn_o"] for r in rA]
    vpre0 = assemble_scan([r["vpre_o"] for r in rA])
    g0 = assemble_scan([r["gg_o"] for r in rA])
    mask = np.concatenate([r["mask_o"] for r in rA], 0)
    gates = np.concatenate([r["gates_o"] for r in rA], 0)

    ncS = build_scan(T)
    w0 = pack_lhsT(np.ascontiguousarray(W_h[0].T))
    h0 = run_bass_kernel_spmd(ncS, [{"w": w0, "vpre": vpre0, "g": g0}], core_ids=[0]).results[0]["hout"]

    ncC = build_midC(NTOK)
    woutp0 = pack_lhsT(np.ascontiguousarray(W_out[0].T))
    win1p = pack_lhsT(np.ascontiguousarray(W_in[1].T))
    imsC = []
    for c in CORES:
        imsC.append({
            "h0T": slice_scan(h0, c), "xnT": xn_parts[c],
            "wout": woutp0, "win1": win1p,
            "ln_g": vec_pm(ln_g[1]), "ln_b": vec_pm(ln_b[1]),
            "b_out": vec_pm(b_out[0]),
            "b_u": vec_pm(b_in[1][:1024]), "b_v": vec_pm(b_in[1][1024:]), "b_h": vec_pm(b_h[1]),
            "id128": np.eye(128, dtype=np.float32),
        })
    rC = run_bass_kernel_spmd(ncC, imsC, core_ids=CORES).results
    s0o_parts = [r["s0o_o"] for r in rC]
    vpre1 = assemble_scan([r["vpre_o"] for r in rC])
    g1 = assemble_scan([r["gg_o"] for r in rC])

    w1s = pack_lhsT(np.ascontiguousarray(W_h[1].T))
    h1 = run_bass_kernel_spmd(ncS, [{"w": w1s, "vpre": vpre1, "g": g1}], core_ids=[0]).results[0]["hout"]

    ncE = build_moeE(N)
    xn_full = np.concatenate(
        [p.reshape(128, 8, NTOK) for p in xn_parts], axis=2)
    xn_full = np.ascontiguousarray(xn_full.reshape(128, 8 * N))
    imsE = []
    for c in CORES:
        imsE.append({
            "xnT": xn_full,
            "w1p": pack_lhsT(np.ascontiguousarray(inp["L0_ffn_w1"][c])),
            "w2p": pack_lhsT(np.ascontiguousarray(inp["L0_ffn_w2"][c])),
            "b1p": vec_pm(inp["L0_ffn_b1"][c]), "b2p": vec_pm(inp["L0_ffn_b2"][c]),
            "mrow": np.ascontiguousarray(mask[:, c].reshape(1, N)),
            "grow": np.ascontiguousarray(gates[:, c].reshape(1, N)),
            "on1": np.ones((1, 128), np.float32),
        })
    rE = run_bass_kernel_spmd(ncE, imsE, core_ids=CORES).results
    ctrs = [r["ctr_o"].reshape(128, 8, N) for r in rE]

    ncF = build_combF(NTOK)
    wout1p = pack_lhsT(np.ascontiguousarray(W_out[1].T))
    imsF = []
    for c in CORES:
        ctr_c = np.stack([ct[:, :, c * NTOK:(c + 1) * NTOK] for ct in ctrs], axis=1)
        imsF.append({
            "xT": fm(xflat[c * NTOK:(c + 1) * NTOK]),
            "s0T": s0o_parts[c], "h1T": slice_scan(h1, c),
            "ctrT": np.ascontiguousarray(ctr_c.reshape(128, 8 * 8 * NTOK)),
            "wout": wout1p, "b_out": vec_pm(b_out[1]),
            "id128": np.eye(128, dtype=np.float32),
        })
    rF = run_bass_kernel_spmd(ncF, imsF, core_ids=CORES).results
    x1_parts = [r["x1_o"] for r in rF]
    xn1_parts = [r["xn1_o"] for r in rF]

    ncG1 = build_qkvG1(NTOK, CH=256)
    sc = rope_scale()
    wdq = pack_lhsT(inp["L1_attn_W_DQ"])
    wuq = pack_lhsT(inp["L1_attn_W_UQ"])
    wqr = pack_lhsT(inp["L1_attn_W_QR"])
    wdkv = pack_lhsT(inp["L1_attn_W_DKV"])
    wuk = pack_lhsT(inp["L1_attn_W_UK"])
    wuv = pack_lhsT(inp["L1_attn_W_UV"])
    wkr = pack_lhsT(inp["L1_attn_W_KR"])
    imsG1 = []
    for c in CORES:
        tr = np.arange(c % 4 * NTOK, (c % 4 + 1) * NTOK)
        scq = np.empty((128, 4 * NTOK), np.float32)
        for u4 in range(4):
            rows = np.arange(128) + 128 * u4
            scq[:, u4 * NTOK:(u4 + 1) * NTOK] = sc[tr][:, rows % 32].T
        sck = np.ascontiguousarray(sc[tr].T)
        imsG1.append({
            "xn1T": xn1_parts[c], "wdq": wdq, "wuq": wuq, "wqr": wqr,
            "wdkv": wdkv, "wuk": wuk, "wuv": wuv, "wkr": wkr,
            "scq": scq, "sck": sck,
        })
    rG1 = run_bass_kernel_spmd(ncG1, imsG1, core_ids=CORES).results

    ncG2 = build_ffnG2(NTOK, CH=256)
    w1p = pack_lhsT(np.ascontiguousarray(inp["L1_ffn_w1"].T))
    w2p = pack_lhsT(np.ascontiguousarray(inp["L1_ffn_w2"].T))
    imsG2 = []
    for c in CORES:
        imsG2.append({"xn1T": xn1_parts[c], "x1T": x1_parts[c],
                      "w1p": w1p, "w2p": w2p,
                      "b1p": vec_pm(inp["L1_ffn_b1"]), "b2p": vec_pm(inp["L1_ffn_b2"])})
    rG2 = run_bass_kernel_spmd(ncG2, imsG2, core_ids=CORES).results
    op_parts = [r["op_o"] for r in rG2]

    def asm(key, rows):
        return np.concatenate([rG1[c][key].reshape(128, rows // 128, NTOK) for c in CORES], axis=2) \
            .reshape(128, rows // 128, N)
    qc = asm("qc_o", 1024)
    qr = asm("qr_o", 512)
    kc = asm("kc_o", 1024)
    vv = asm("v_o", 1024)
    kr = np.concatenate([rG1[c]["kr_o"] for c in CORES], axis=1)

    def frows(a, r0, r1):
        out = np.empty((r1 - r0, a.shape[2]), np.float32)
        for r in range(r0, r1):
            out[r - r0] = a[r % 128, r // 128]
        return out

    ncH = build_attnH(T, 4)
    mvar = np.full((128, 4, 512), -1e30, np.float32)
    for r in range(4):
        for p in range(128):
            mvar[p, r, :128 * r + p + 1] = 0.0
    mflat = np.ascontiguousarray(mvar.reshape(128, 4 * 512))
    imsH = []
    for c in CORES:
        im = {"msk": mflat, "idd": np.eye(128, dtype=np.float32)}
        for i in range(4):
            pair = 4 * c + i
            bb, hh = pair // 16, pair % 16
            tok = slice(bb * T, (bb + 1) * T)
            qC = frows(qc[:, :, tok], hh * 64, (hh + 1) * 64)
            qR = frows(qr[:, :, tok], hh * 32, (hh + 1) * 32)
            kC = frows(kc[:, :, tok], hh * 64, (hh + 1) * 64)
            vh = frows(vv[:, :, tok], hh * 64, (hh + 1) * 64)
            im[f"q{i}"] = np.ascontiguousarray(np.concatenate([qC, qR], 0))
            im[f"k{i}"] = np.ascontiguousarray(np.concatenate([kC, kr[:, tok]], 0))
            im[f"v{i}"] = np.ascontiguousarray(vh.T)
        imsH.append(im)
    rH = run_bass_kernel_spmd(ncH, imsH, core_ids=CORES).results
    y_att = np.empty((N, 1024), np.float32)
    for c in CORES:
        for i in range(4):
            pair = 4 * c + i
            bb, hh = pair // 16, pair % 16
            y_att[bb * T:(bb + 1) * T, hh * 64:(hh + 1) * 64] = rH[c][f"y{i}"]

    ncI = build_finalI(NTOK)
    wo = pack_lhsT(inp["L1_attn_W_O"])
    imsI = []
    for c in CORES:
        imsI.append({"yaT": fm(y_att[c * NTOK:(c + 1) * NTOK]),
                     "opT": op_parts[c], "wop": wo})
    rI = run_bass_kernel_spmd(ncI, imsI, core_ids=CORES).results
    out = np.concatenate([unfm(r["out_o"], NTOK, 1024) for r in rI], 0)
    return out.reshape(2, T, 1024).astype(np.float32)


# version 06
# speedup vs baseline: 1.0006x; 1.0006x over previous
import numpy as np
import ml_dtypes
from contextlib import ExitStack
import concourse.bass as bass
import concourse.mybir as mybir
from concourse.bass_utils import run_bass_kernel_spmd

F32 = mybir.dt.float32
BF16 = mybir.dt.bfloat16
AF = mybir.ActivationFunctionType
ALU = mybir.AluOpType
AX = mybir.AxisListType.X
BFNP = ml_dtypes.bfloat16
ENGINES = ("sync", "gpsimd", "vector", "scalar", "tensor")
EPS = 1e-5


class B:

    def __init__(self, nc, ctx):
        self.nc = nc
        self.ctx = ctx
        self.ops = {e: [] for e in ENGINES}
        self.sems = {}
        self.cnt = {}

    def sem(self, name):
        h = self.ctx.enter_context(self.nc.semaphore(name))
        self.sems[name] = h
        self.cnt[name] = 0
        return h

    def at(self, name):
        return self.cnt[name]

    def emit(self, eng, fn, wait=(), inc=None):
        if isinstance(wait, (str, tuple)) and (not wait or isinstance(wait, str) or isinstance(wait[1], int)):
            wait = [wait] if wait else []
        wl = []
        for w in wait:
            if isinstance(w, str):
                wl.append((self.sems[w], self.cnt[w]))
            else:
                wl.append((self.sems[w[0]], w[1]))
        inch = None
        amt = 0
        if inc is not None:
            if isinstance(inc, tuple):
                name, amt = inc
            else:
                name, amt = inc, 1
            self.cnt[name] += amt
            inch = self.sems[name]

        def runner(e, fn=fn, wl=tuple(wl), inch=inch, amt=amt):
            for h, v in wl:
                if v > 0:
                    e.wait_ge(h, v)
            inst = fn(e)
            if inch is not None:
                assert inst is not None, "emit with inc= must return an instruction"
                inst.then_inc(inch, amt)
        self.ops[eng].append(runner)

    def install(self, block):
        for name, fns in self.ops.items():
            def runner(e, fns=tuple(fns)):
                for f in fns:
                    f(e)
            getattr(block, name)(runner)


def mmgroup(pb, pairs):
    def fn(t, pb=pb, pairs=tuple(pairs)):
        n = len(pairs)
        inst = None
        for q, (l, r) in enumerate(pairs):
            inst = t.matmul(pb, l, r, start=(q == 0), stop=(q == n - 1))
        return inst
    return fn


def pack_lhsT(wt):
    K, M = wt.shape
    KT = K // 128
    Mw = min(M, 128)
    MT = M // Mw
    return np.ascontiguousarray(
        wt.reshape(KT, 128, MT, Mw).transpose(2, 0, 1, 3).reshape(MT * KT, 128, Mw)
        .transpose(1, 0, 2).reshape(128, MT * KT * Mw))


def vec_pm(v):
    D = v.shape[0]
    return np.ascontiguousarray(v.reshape(D // 128, 128).T)


def fm(x):
    N, D = x.shape
    NT = D // 128
    return np.ascontiguousarray(x.T.reshape(NT, 128, N).transpose(1, 0, 2).reshape(128, NT * N))


def unfm(a, N, D):
    NT = D // 128
    return np.ascontiguousarray(a.reshape(128, NT, N).transpose(1, 0, 2).reshape(D, N).T)


def tm(x):
    N, D = x.shape
    return np.ascontiguousarray(x.reshape(N // 128, 128, D).transpose(1, 0, 2).reshape(128, (N // 128) * D))


def untm(a, N, D):
    return np.ascontiguousarray(a.reshape(128, N // 128, D).transpose(1, 0, 2).reshape(N, D))


def emit_ln(b, S, src_f32, dst, CH, *, src_tm=None, affine=None, out_bf=None):
    NTT = CH // 128
    sq, stat, murow, rsrow = S["sq"], S["stat"], S["murow"], S["rsrow"]
    ps_st, ps_tr, ps_tr2, ps_bc = S["ps_st"], S["ps_tr"], S["ps_tr2"], S["ps_bc"]
    ones, ones1, idm = S["ones"], S["ones1"], S["idm"]

    for j in range(NTT):
        if src_tm is not None:
            b.emit("scalar", lambda s, j=j: s.square(sq[:, 0:1024], src_tm[:, 1024 * j:1024 * (j + 1)]),
                   wait=("dv" if j > 0 else ()), inc="ac")
            b.emit("vector", lambda v, j=j: v.reduce_sum(stat[:, 8 * j:8 * j + 1], src_tm[:, 1024 * j:1024 * (j + 1)], axis=AX))
            b.emit("vector", lambda v, j=j: v.reduce_sum(stat[:, 8 * j + 1:8 * j + 2], sq[:, 0:1024], axis=AX),
                   wait="ac", inc="dv")
            b.emit("vector", lambda v, j=j: v.tensor_scalar(stat[:, 8 * j:8 * j + 2], stat[:, 8 * j:8 * j + 2], 1.0 / 1024.0, None, ALU.mult),
                   wait="dv", inc="dv")
        else:
            if j == 0:
                b.emit("scalar", lambda s: s.square(sq[:], src_f32[:]), wait="dv", inc="ac")
            b.emit("tensor", mmgroup(ps_st[:, 0:1],
                   [(src_f32[:, u * CH + 128 * j:u * CH + 128 * (j + 1)], ones[:, :]) for u in range(8)]),
                   wait=("ac" if j == 0 else "dv"), inc="mm")
            b.emit("tensor", mmgroup(ps_st[:, 1:2],
                   [(sq[:, u * CH + 128 * j:u * CH + 128 * (j + 1)], ones[:, :]) for u in range(8)]),
                   inc="mm")
            b.emit("vector", lambda v, j=j: v.tensor_scalar(stat[:, 8 * j:8 * j + 2], ps_st[:, 0:2], 1.0 / 1024.0, None, ALU.mult),
                   wait="mm", inc="dv")
        b.emit("vector", lambda v, j=j: v.tensor_tensor(stat[:, 8 * j + 2:8 * j + 3], stat[:, 8 * j:8 * j + 1], stat[:, 8 * j:8 * j + 1], ALU.mult),
               wait="dv", inc="dv")
        b.emit("vector", lambda v, j=j: v.tensor_tensor(stat[:, 8 * j + 3:8 * j + 4], stat[:, 8 * j + 1:8 * j + 2], stat[:, 8 * j + 2:8 * j + 3], ALU.subtract),
               wait="dv", inc="dv")
        b.emit("scalar", lambda s, j=j: s.activation(stat[:, 8 * j + 4:8 * j + 5], stat[:, 8 * j + 3:8 * j + 4], AF.Sqrt, bias=S["epsv"][:, 0:1]),
               wait="dv", inc="ac")
        b.emit("vector", lambda v, j=j: v.reciprocal(stat[:, 8 * j + 5:8 * j + 6], stat[:, 8 * j + 4:8 * j + 5]),
               wait="ac", inc="dv")
        def trfn(t, j=j):
            t.transpose(ps_tr[0:1, 0:128], stat[:, 8 * j:8 * j + 1], idm[:, :])
            return t.transpose(ps_tr2[0:1, 0:128], stat[:, 8 * j + 5:8 * j + 6], idm[:, :])
        b.emit("tensor", trfn, wait="dv", inc="mm")
        b.emit("vector", lambda v, j=j: v.tensor_copy(murow[0:1, 128 * j:128 * (j + 1)], ps_tr[0:1, 0:128]),
               wait="mm")
        b.emit("vector", lambda v, j=j: v.tensor_copy(rsrow[0:1, 128 * j:128 * (j + 1)], ps_tr2[0:1, 0:128]),
               inc="dv")
        def bcfn(t, j=j):
            t.matmul(ps_bc[:, 0:128], ones1[:, :], murow[0:1, 128 * j:128 * (j + 1)], start=True, stop=True)
            return t.matmul(ps_bc[:, 128:256], ones1[:, :], rsrow[0:1, 128 * j:128 * (j + 1)], start=True, stop=True)
        b.emit("tensor", bcfn, wait="dv", inc="mm")
        for u in range(8):
            sl = slice(u * CH + 128 * j, u * CH + 128 * (j + 1))
            b.emit("vector", lambda v, sl=sl: v.tensor_tensor(dst[:, sl], src_f32[:, sl], ps_bc[:, 0:128], ALU.subtract),
                   wait=("mm" if u == 0 else ()), inc="dv")
            b.emit("vector", lambda v, sl=sl: v.tensor_tensor(dst[:, sl], dst[:, sl], ps_bc[:, 128:256], ALU.mult),
                   wait="dv", inc="dv")
            if affine is not None:
                ga, ba = affine
                b.emit("vector", lambda v, sl=sl, u=u: v.tensor_scalar(dst[:, sl], dst[:, sl], ga[:, u:u + 1], ba[:, u:u + 1], ALU.mult, ALU.add),
                       wait="dv", inc="dv")
    if out_bf is not None:
        b.emit("scalar", lambda s: s.copy(out_bf[:], dst[:]), wait="dv", inc="ac")


def build_preA(ntok, CH=512, stages=(1, 2, 3, 4, 5, 6, 7)):
    nc = bass.Bass("TRN2", target_bir_lowering=False)
    NCH = ntok // CH
    NTT = CH // 128
    xT = nc.dram_tensor("xT", [128, 8 * ntok], F32, kind="ExternalInput").ap()
    xTM = nc.dram_tensor("xTM", [ntok, 1024], F32, kind="ExternalInput").ap()
    win = nc.dram_tensor("win", [128, 16 * 8 * 128], F32, kind="ExternalInput").ap()
    ln_g = nc.dram_tensor("ln_g", [128, 8], F32, kind="ExternalInput").ap()
    ln_b = nc.dram_tensor("ln_b", [128, 8], F32, kind="ExternalInput").ap()
    b_u = nc.dram_tensor("b_u", [128, 8], F32, kind="ExternalInput").ap()
    b_v = nc.dram_tensor("b_v", [128, 8], F32, kind="ExternalInput").ap()
    b_h = nc.dram_tensor("b_h", [128, 8], F32, kind="ExternalInput").ap()
    rw = nc.dram_tensor("rw", [128, 8 * 8], F32, kind="ExternalInput").ap()
    rb_bc = nc.dram_tensor("rb_bc", [128, 8], F32, kind="ExternalInput").ap()
    iota8 = nc.dram_tensor("iota8", [128, 8], F32, kind="ExternalInput").ap()
    id128 = nc.dram_tensor("id128", [128, 128], F32, kind="ExternalInput").ap()
    xn_o = nc.dram_tensor("xn_o", [128, 8 * ntok], F32, kind="ExternalOutput").ap()
    vpre_o = nc.dram_tensor("vpre_o", [128, 8 * ntok], F32, kind="ExternalOutput").ap()
    gg_o = nc.dram_tensor("gg_o", [128, 8 * ntok], F32, kind="ExternalOutput").ap()
    mask_o = nc.dram_tensor("mask_o", [ntok, 8], F32, kind="ExternalOutput").ap()
    gates_o = nc.dram_tensor("gates_o", [ntok, 8], F32, kind="ExternalOutput").ap()

    with ExitStack() as ctx:
        e = ctx.enter_context
        wsb = e(nc.sbuf_tensor("wsb", [128, 16 * 8 * 128], BF16))
        idm = e(nc.sbuf_tensor("idm", [128, 128], F32))
        ones = e(nc.sbuf_tensor("ones", [128, 1], F32))
        ones1 = e(nc.sbuf_tensor("ones1", [1, 128], F32))
        epsv = e(nc.sbuf_tensor("epsv", [128, 1], F32))
        lng = e(nc.sbuf_tensor("lng", [128, 8], F32))
        lnb = e(nc.sbuf_tensor("lnb", [128, 8], F32))
        bu = e(nc.sbuf_tensor("bu", [128, 8], F32))
        bv = e(nc.sbuf_tensor("bv", [128, 8], F32))
        bh = e(nc.sbuf_tensor("bh", [128, 8], F32))
        rws = e(nc.sbuf_tensor("rws", [128, 64], F32))
        rwh = e(nc.sbuf_tensor("rwh", [128, 64], BF16))
        rwl = e(nc.sbuf_tensor("rwl", [128, 64], BF16))
        rwt = e(nc.sbuf_tensor("rwt", [128, 64], F32))
        xlb = e(nc.sbuf_tensor("xlb", [128, 8 * CH], BF16))
        rbbc = e(nc.sbuf_tensor("rbbc", [128, 8], F32))
        iot = e(nc.sbuf_tensor("iot", [128, 8], F32))
        xf = e(nc.sbuf_tensor("xf", [128, 8 * CH], F32))
        xtm = e(nc.sbuf_tensor("xtm", [128, NTT * 1024], F32))
        sq = e(nc.sbuf_tensor("sq", [128, 8 * CH], F32))
        xn = e(nc.sbuf_tensor("xn", [128, 8 * CH], F32))
        xnbf = e(nc.sbuf_tensor("xnbf", [128, 8 * CH], BF16))
        s0l = e(nc.sbuf_tensor("s0l", [128, 8 * CH], F32))
        s0lb = e(nc.sbuf_tensor("s0lb", [128, 8 * CH], BF16))
        stat = e(nc.sbuf_tensor("stat", [128, 8 * NTT], F32))
        murow = e(nc.sbuf_tensor("murow", [1, 128 * NTT], F32))
        rsrow = e(nc.sbuf_tensor("rsrow", [1, 128 * NTT], F32))
        vout = e(nc.sbuf_tensor("vout", [128, 8 * CH], F32))
        gout = e(nc.sbuf_tensor("gout", [128, 8 * CH], F32))
        rmask = e(nc.sbuf_tensor("rmask", [128, 8 * NTT], F32))
        rgate = e(nc.sbuf_tensor("rgate", [128, 8 * NTT], F32))
        rt1 = e(nc.sbuf_tensor("rt1", [128, 8], F32))
        rt2 = e(nc.sbuf_tensor("rt2", [128, 8], F32))
        rt3 = e(nc.sbuf_tensor("rt3", [128, 8], F32))
        rsc = e(nc.sbuf_tensor("rsc", [128, 4], F32))
        ps_st = e(nc.psum_tensor("ps_st", [128, 2], F32))
        ps_tr = e(nc.psum_tensor("ps_tr", [1, 128], F32))
        ps_tr2 = e(nc.psum_tensor("ps_tr2", [1, 128], F32))
        ps_bc = e(nc.psum_tensor("ps_bc", [128, 256], F32))
        ps_uv0 = e(nc.psum_tensor("ps_uv0", [128, CH], F32))
        ps_uv1 = e(nc.psum_tensor("ps_uv1", [128, CH], F32))
        ps_rt = e(nc.psum_tensor("ps_rt", [128, 8], F32))
        b = B(nc, ctx)
        for sname in ("ldc", "ldg", "ldx", "mm", "dv", "ac", "st"):
            b.sem(sname)
        block = e(nc.Block())
        S = dict(ones=ones, ones1=ones1, idm=idm, sq=sq, stat=stat, murow=murow,
                 rsrow=rsrow, ps_st=ps_st, ps_tr=ps_tr, ps_tr2=ps_tr2, ps_bc=ps_bc, epsv=epsv)

        def dve(fn):
            b.emit("vector", fn, wait=["dv", "ac"], inc="dv")

        for dst, src in ((idm, id128), (lng, ln_g), (lnb, ln_b), (bu, b_u),
                         (bv, b_v), (bh, b_h), (rbbc, rb_bc), (iot, iota8)):
            b.emit("sync", lambda s, d=dst, sr=src: s.dma_start(out=d[:], in_=sr[:, :]), inc=("ldc", 16))
        b.emit("gpsimd", lambda g: g.dma_start(out=wsb[:], in_=win[:, :]), inc=("ldg", 16))
        b.emit("gpsimd", lambda g: g.dma_start(out=rws[:], in_=rw[:, :]), inc=("ldg", 16))
        ld0 = [("ldc", b.at("ldc")), ("ldg", b.at("ldg"))]
        b.emit("vector", lambda v: v.memset(ones[:], 1.0))
        b.emit("vector", lambda v: v.memset(ones1[:], 1.0))
        b.emit("vector", lambda v: v.memset(epsv[:], EPS), inc="dv")
        b.emit("vector", lambda v: v.tensor_add(bv[:], bv[:], bh[:]), wait=ld0, inc="dv")
        b.emit("scalar", lambda s: s.copy(rwh[:], rws[:]), wait=ld0, inc="ac")
        b.emit("scalar", lambda s: s.copy(rwt[:], rwh[:]), wait="ac", inc="ac")
        b.emit("vector", lambda v: v.tensor_tensor(rwt[:], rws[:], rwt[:], ALU.subtract), wait="ac", inc="dv")
        b.emit("scalar", lambda s: s.copy(rwl[:], rwt[:]), wait="dv", inc="ac")
        for eng in ("scalar", "tensor"):
            b.emit(eng, lambda x: None, wait=ld0 + ["dv"])

        for c in range(NCH):
            st0 = b.at("st")
            b.emit("sync", lambda s, c=c: s.dma_start(
                out=xf[:].rearrange("p (u n) -> p u n", u=8),
                in_=xT.rearrange("p (u n) -> p u n", u=8)[:, :, c * CH:(c + 1) * CH]),
                wait=[("st", st0)] + ld0, inc=("ldx", 16))
            b.emit("sync", lambda s, c=c: s.dma_start(
                out=xtm[:].rearrange("p (j d) -> p j d", d=1024),
                in_=xTM[c * CH:(c + 1) * CH, :].rearrange("(j p) d -> p j d", p=128)),
                inc=("ldx", 16))
            ldv = b.at("ldx")
            for eng in ("vector", "scalar", "tensor"):
                b.emit(eng, lambda x: None, wait=[("ldx", ldv), ("ldg", b.at("ldg")), ("st", st0)] + ld0)

            emit_ln(b, S, xf, xn, CH, src_tm=xtm, affine=None, out_bf=xnbf)
            dv_xn = b.at("dv")
            b.emit("sync", lambda s, c=c: s.dma_start(
                out=xn_o.rearrange("p (u n) -> p u n", u=8)[:, :, c * CH:(c + 1) * CH],
                in_=xn[:].rearrange("p (u n) -> p u n", u=8)),
                wait=[("dv", dv_xn)], inc=("st", 16))

            if 2 not in stages:
                continue
            emit_ln(b, S, xn, s0l, CH, src_tm=None, affine=(lng, lnb), out_bf=s0lb)
            ac_s0lb = b.at("ac")

            do_uv = 3 in stages
            bank_guard = {0: ("ac", 0), 1: ("ac", 0)}
            for ei in (range(16) if do_uv else ()):
                pb = ps_uv0 if ei % 2 == 0 else ps_uv1
                waits = [("ac", ac_s0lb), bank_guard[ei % 2]]
                b.emit("tensor", mmgroup(pb[:, :],
                       [(wsb[:, (ei * 8 + k) * 128:(ei * 8 + k + 1) * 128], s0lb[:, k * CH:(k + 1) * CH]) for k in range(8)]),
                       wait=waits, inc="mm")
                if ei < 8:
                    b.emit("scalar", lambda s, ei=ei, pb=pb: s.activation(
                        gout[:, ei * CH:(ei + 1) * CH], pb[:, :], AF.Sigmoid, bias=bu[:, ei:ei + 1]),
                        wait="mm", inc="ac")
                    bank_guard[ei % 2] = ("ac", b.at("ac"))
                else:
                    u = ei - 8
                    b.emit("vector", lambda v, u=u, pb=pb: v.tensor_scalar(
                        vout[:, u * CH:(u + 1) * CH], pb[:, :], bv[:, u:u + 1], None, ALU.add),
                        wait="mm", inc="dv")
                    bank_guard[ei % 2] = ("dv", b.at("dv"))
            if do_uv:
                b.emit("sync", lambda s, c=c: s.dma_start(
                    out=vpre_o.rearrange("p (u n) -> p u n", u=8)[:, :, c * CH:(c + 1) * CH],
                    in_=vout[:].rearrange("p (u n) -> p u n", u=8)),
                    wait=["dv"], inc=("st", 16))
                b.emit("sync", lambda s, c=c: s.dma_start(
                    out=gg_o.rearrange("p (u n) -> p u n", u=8)[:, :, c * CH:(c + 1) * CH],
                    in_=gout[:].rearrange("p (u n) -> p u n", u=8)),
                    wait=["ac"], inc=("st", 16))

            if 4 not in stages:
                continue
            b.emit("scalar", lambda s: s.copy(s0l[:], xnbf[:]), wait=["dv", "mm"], inc="ac")
            b.emit("vector", lambda v: v.tensor_tensor(sq[:], xn[:], s0l[:], ALU.subtract), wait="ac", inc="dv")
            b.emit("scalar", lambda s: s.copy(xlb[:], sq[:]), wait="dv", inc="ac")
            ac_split = b.at("ac")
            rt_guard = ("dv", 0)
            for j in range(NTT):
                b.emit("tensor", mmgroup(ps_rt[:, :],
                       [(xa[:, u * CH + 128 * j:u * CH + 128 * (j + 1)], wb[:, u * 8:(u + 1) * 8])
                        for (xa, wb) in ((xnbf, rwh), (xnbf, rwl), (xlb, rwh), (xlb, rwl))
                        for u in range(8)]),
                       wait=[("ac", ac_split), rt_guard], inc="mm")
                base = 8 * j
                b.emit("vector", lambda v: v.tensor_tensor(rt1[:, :], ps_rt[:, :], rbbc[:, :], ALU.add),
                       wait=["dv", "mm", "ac"], inc="dv")
                rt_guard = ("dv", b.at("dv"))
                if 5 not in stages:
                    continue
                dve(lambda v: v.reduce_max(rsc[:, 0:1], rt1[:, :], axis=AX))
                dve(lambda v: v.tensor_scalar(rsc[:, 1:2], rsc[:, 0:1], -1.0, None, ALU.mult))
                b.emit("scalar", lambda s: s.activation(rt2[:, :], rt1[:, :], AF.Exp, bias=rsc[:, 1:2]),
                       wait="dv", inc="ac")
                b.emit("vector", lambda v: v.reduce_sum(rsc[:, 2:3], rt2[:, :], axis=AX), wait="ac", inc="dv")
                dve(lambda v: v.reciprocal(rsc[:, 3:4], rsc[:, 2:3]))
                dve(lambda v, b8=base: v.tensor_scalar(rgate[:, b8:b8 + 8], rt2[:, :], rsc[:, 3:4], None, ALU.mult))
                if 6 not in stages:
                    continue
                dve(lambda v, b8=base: v.reduce_max(rsc[:, 0:1], rgate[:, b8:b8 + 8], axis=AX))
                dve(lambda v, b8=base: v.tensor_scalar(rt1[:, :], rgate[:, b8:b8 + 8], rsc[:, 0:1], None, ALU.is_ge))
                dve(lambda v: v.tensor_scalar(rt2[:, :], rt1[:, :], -1e9, 1e9, ALU.mult, ALU.add))
                dve(lambda v: v.tensor_tensor(rt2[:, :], rt2[:, :], iot[:, :], ALU.add))
                dve(lambda v: v.tensor_reduce(rsc[:, 0:1], rt2[:, :], AX, ALU.min))
                dve(lambda v: v.tensor_scalar(rt1[:, :], rt2[:, :], rsc[:, 0:1], None, ALU.is_le))
                if 7 not in stages:
                    dve(lambda v, b8=base: v.tensor_copy(rmask[:, b8:b8 + 8], rt1[:, :]))
                    continue
                dve(lambda v, b8=base: v.tensor_scalar(rt2[:, :], rt1[:, :], -2.0, None, ALU.mult))
                dve(lambda v, b8=base: v.tensor_tensor(rt2[:, :], rt2[:, :], rgate[:, b8:b8 + 8], ALU.add))
                dve(lambda v: v.reduce_max(rsc[:, 0:1], rt2[:, :], axis=AX))
                dve(lambda v: v.tensor_scalar(rt3[:, :], rt2[:, :], rsc[:, 0:1], None, ALU.is_ge))
                dve(lambda v: v.tensor_scalar(rt2[:, :], rt3[:, :], -1e9, 1e9, ALU.mult, ALU.add))
                dve(lambda v: v.tensor_tensor(rt2[:, :], rt2[:, :], iot[:, :], ALU.add))
                dve(lambda v: v.tensor_reduce(rsc[:, 0:1], rt2[:, :], AX, ALU.min))
                dve(lambda v: v.tensor_scalar(rt3[:, :], rt2[:, :], rsc[:, 0:1], None, ALU.is_le))
                dve(lambda v, b8=base: v.tensor_tensor(rmask[:, b8:b8 + 8], rt1[:, :], rt3[:, :], ALU.max))
            b.emit("sync", lambda s, c=c: s.dma_start(
                out=mask_o[c * CH:(c + 1) * CH, :].rearrange("(j p) d -> p j d", p=128),
                in_=rmask[:, 0:8 * NTT].rearrange("p (j d) -> p j d", d=8)), wait=["dv"], inc=("st", 16))
            b.emit("sync", lambda s, c=c: s.dma_start(
                out=gates_o[c * CH:(c + 1) * CH, :].rearrange("(j p) d -> p j d", p=128),
                in_=rgate[:, 0:8 * NTT].rearrange("p (j d) -> p j d", d=8)), inc=("st", 16))

        b.emit("sync", lambda s: None, wait=["st"])
        b.install(block)
    return nc


def preA_inputs(x_slice, W_in0, ln_g0, ln_b0, b_in0, b_h0, router_w, router_b):
    ntok = x_slice.shape[0]
    return {
        "xT": fm(x_slice),
        "xTM": np.ascontiguousarray(x_slice),
        "win": pack_lhsT(np.ascontiguousarray(W_in0.T)),
        "ln_g": vec_pm(ln_g0), "ln_b": vec_pm(ln_b0),
        "b_u": vec_pm(b_in0[:1024]), "b_v": vec_pm(b_in0[1024:]), "b_h": vec_pm(b_h0),
        "rw": np.ascontiguousarray(router_w.reshape(8, 128, 8).transpose(1, 0, 2).reshape(128, 64)),
        "rb_bc": np.ascontiguousarray(np.tile(router_b[None, :], (128, 1)).astype(np.float32)),
        "iota8": np.ascontiguousarray(np.tile(np.arange(8, dtype=np.float32)[None, :], (128, 1))),
        "id128": np.eye(128, dtype=np.float32),
    }



def build_scan(T, CHUNK=128):
    nc = bass.Bass("TRN2", target_bir_lowering=False)
    w = nc.dram_tensor("w", [128, 64 * 128], F32, kind="ExternalInput").ap()
    vpre = nc.dram_tensor("vpre", [128, 16 * T], F32, kind="ExternalInput").ap()
    g = nc.dram_tensor("g", [128, 16 * T], F32, kind="ExternalInput").ap()
    hout = nc.dram_tensor("hout", [128, 16 * T], F32, kind="ExternalOutput").ap()
    nch = T // CHUNK
    with ExitStack() as ctx:
        e = ctx.enter_context
        ws = e(nc.sbuf_tensor("ws", [128, 64 * 128], BF16))
        vs = e(nc.sbuf_tensor("vs", [128, 16 * CHUNK * 2], F32))
        gs = e(nc.sbuf_tensor("gs", [128, 16 * CHUNK * 2], F32))
        gms = e(nc.sbuf_tensor("gms", [128, 16 * CHUNK * 2], F32))
        hall = e(nc.sbuf_tensor("hall", [128, 16 * CHUNK * 2], F32))
        z16 = e(nc.sbuf_tensor("z16", [128, 16], F32))
        hb = e(nc.sbuf_tensor("hb", [128, 16], BF16))
        zs = e(nc.sbuf_tensor("zs", [128, 16], F32))
        cs = e(nc.sbuf_tensor("cs", [128, 16], F32))
        t1 = e(nc.sbuf_tensor("t1", [128, 16], F32))
        t2 = e(nc.sbuf_tensor("t2", [128, 16], F32))
        ps = e(nc.psum_tensor("ps", [128, 16], F32))
        ldg = e(nc.semaphore("ldg"))
        ldx0 = e(nc.semaphore("ldx0"))
        ldx1 = e(nc.semaphore("ldx1"))
        mm_s = e(nc.semaphore("mm_s"))
        z_s = e(nc.semaphore("z_s"))
        c_s = e(nc.semaphore("c_s"))
        h_s = e(nc.semaphore("h_s"))
        e_s = e(nc.semaphore("e_s"))
        gm_s = e(nc.semaphore("gm_s"))
        hb_s = e(nc.semaphore("hb_s"))
        stc = e(nc.semaphore("stc"))
        block = e(nc.Block())

        @block.gpsimd
        def _(gp):
            gp.dma_start(out=ws[:], in_=w[:, :]).then_inc(ldg, 16)

        @block.sync
        def _(sy):
            for c in range(nch):
                buf = c % 2
                sem = ldx0 if buf == 0 else ldx1
                if c >= 2:
                    sy.wait_ge(h_s, (c - 1) * CHUNK)
                    sy.wait_ge(stc, 16 * (c - 1))
                sy.dma_start(out=vs[:, 16 * CHUNK * buf:16 * CHUNK * (buf + 1)],
                             in_=vpre[:, 16 * CHUNK * c:16 * CHUNK * (c + 1)]).then_inc(sem, 16)
                sy.dma_start(out=gs[:, 16 * CHUNK * buf:16 * CHUNK * (buf + 1)],
                             in_=g[:, 16 * CHUNK * c:16 * CHUNK * (c + 1)]).then_inc(sem, 16)
                sy.wait_ge(h_s, (c + 1) * CHUNK)
                sy.dma_start(out=hout[:, 16 * CHUNK * c:16 * CHUNK * (c + 1)],
                             in_=hall[:, 16 * CHUNK * buf:16 * CHUNK * (buf + 1)]).then_inc(stc, 16)
            sy.wait_ge(stc, 16 * nch)

        @block.vector
        def _(v):
            v.memset(z16[:], 0.0)
            v.memset(hb[:], 0.0).then_inc(hb_s, 1)
            for c in range(nch):
                buf = c % 2
                sem = ldx0 if buf == 0 else ldx1
                v.wait_ge(sem, 32 * (c // 2 + 1))
                v.tensor_scalar(gms[:, 16 * CHUNK * buf:16 * CHUNK * (buf + 1)],
                                gs[:, 16 * CHUNK * buf:16 * CHUNK * (buf + 1)],
                                -1.0, 1.0, ALU.mult, ALU.add).then_inc(gm_s, 1)
                v.wait_ge(gm_s, c + 1)
                for tt in range(CHUNK):
                    t = c * CHUNK + tt
                    col = slice(16 * (CHUNK * buf + tt), 16 * (CHUNK * buf + tt + 1))
                    if tt == 0:
                        pcol = None if c == 0 else slice(16 * (CHUNK * ((c - 1) % 2) + CHUNK - 1),
                                                         16 * (CHUNK * ((c - 1) % 2) + CHUNK))
                    else:
                        pcol = slice(16 * (CHUNK * buf + tt - 1), 16 * (CHUNK * buf + tt))
                    hprev = z16[:, 0:16] if pcol is None else hall[:, pcol]
                    v.wait_ge(mm_s, t + 1)
                    v.tensor_add(zs[:], ps[:, 0:16], vs[:, col]).then_inc(z_s, 1)
                    v.tensor_mul(t2[:], gms[:, col], hprev)
                    v.wait_ge(c_s, t + 1)
                    v.tensor_mul(t1[:], gs[:, col], cs[:]).then_inc(e_s, 1)
                    v.wait_ge(e_s, t + 1)
                    v.tensor_add(hall[:, col], t1[:], t2[:]).then_inc(h_s, 1)
                    v.wait_ge(h_s, t + 1)
                    v.tensor_copy(hb[:], hall[:, col]).then_inc(hb_s, 1)

        @block.scalar
        def _(s):
            for t in range(T):
                s.wait_ge(z_s, t + 1)
                s.activation(out=cs[:], in_=zs[:], func=AF.Tanh).then_inc(c_s, 1)

        @block.tensor
        def _(te):
            te.wait_ge(ldg, 16)
            for t in range(T):
                te.wait_ge(hb_s, t + 1)
                for i in range(8):
                    for k in range(8):
                        inst = te.matmul(ps[:, 2 * i:2 * i + 2],
                                         ws[:, (i * 8 + k) * 128:(i * 8 + k + 1) * 128],
                                         hb[:, 2 * k:2 * k + 2],
                                         start=(k == 0), stop=(k == 7))
                inst.then_inc(mm_s, 1)
    return nc


def scan_inputs(W_h, vpre_tdb, g_tdb):
    T = vpre_tdb.shape[0]

    def pk(x):
        return np.ascontiguousarray(x.reshape(T, 8, 128, 2).transpose(2, 0, 1, 3).reshape(128, T * 16))
    return {"w": pack_lhsT(np.ascontiguousarray(W_h.T)), "vpre": pk(vpre_tdb), "g": pk(g_tdb)}


def scan_h_unpack(hout, T):
    return np.ascontiguousarray(hout.reshape(128, T, 8, 2).transpose(1, 2, 0, 3).reshape(T, 1024, 2))


def build_midC(ntok, CH=512):
    nc = bass.Bass("TRN2", target_bir_lowering=False)
    NCH = ntok // CH
    NTT = CH // 128
    h0T = nc.dram_tensor("h0T", [128, 8 * ntok], F32, kind="ExternalInput").ap()
    xnT = nc.dram_tensor("xnT", [128, 8 * ntok], F32, kind="ExternalInput").ap()
    wout = nc.dram_tensor("wout", [128, 8 * 8 * 128], F32, kind="ExternalInput").ap()
    win1 = nc.dram_tensor("win1", [128, 16 * 8 * 128], F32, kind="ExternalInput").ap()
    ln_g = nc.dram_tensor("ln_g", [128, 8], F32, kind="ExternalInput").ap()
    ln_b = nc.dram_tensor("ln_b", [128, 8], F32, kind="ExternalInput").ap()
    b_out = nc.dram_tensor("b_out", [128, 8], F32, kind="ExternalInput").ap()
    b_u = nc.dram_tensor("b_u", [128, 8], F32, kind="ExternalInput").ap()
    b_v = nc.dram_tensor("b_v", [128, 8], F32, kind="ExternalInput").ap()
    b_h = nc.dram_tensor("b_h", [128, 8], F32, kind="ExternalInput").ap()
    id128 = nc.dram_tensor("id128", [128, 128], F32, kind="ExternalInput").ap()
    s0o_o = nc.dram_tensor("s0o_o", [128, 8 * ntok], F32, kind="ExternalOutput").ap()
    vpre_o = nc.dram_tensor("vpre_o", [128, 8 * ntok], F32, kind="ExternalOutput").ap()
    gg_o = nc.dram_tensor("gg_o", [128, 8 * ntok], F32, kind="ExternalOutput").ap()
    with ExitStack() as ctx:
        e = ctx.enter_context
        wo = e(nc.sbuf_tensor("wo", [128, 8 * 8 * 128], BF16))
        w1 = e(nc.sbuf_tensor("w1", [128, 16 * 8 * 128], BF16))
        idm = e(nc.sbuf_tensor("idm", [128, 128], F32))
        ones = e(nc.sbuf_tensor("ones", [128, 1], F32))
        ones1 = e(nc.sbuf_tensor("ones1", [1, 128], F32))
        epsv = e(nc.sbuf_tensor("epsv", [128, 1], F32))
        lng = e(nc.sbuf_tensor("lng", [128, 8], F32))
        lnb = e(nc.sbuf_tensor("lnb", [128, 8], F32))
        bo = e(nc.sbuf_tensor("bo", [128, 8], F32))
        bu = e(nc.sbuf_tensor("bu", [128, 8], F32))
        bv = e(nc.sbuf_tensor("bv", [128, 8], F32))
        bh = e(nc.sbuf_tensor("bh", [128, 8], F32))
        h0b = e(nc.sbuf_tensor("h0b", [128, 8 * CH], BF16))
        xnf = e(nc.sbuf_tensor("xnf", [128, 8 * CH], F32))
        s0o = e(nc.sbuf_tensor("s0o", [128, 8 * CH], F32))
        sq = e(nc.sbuf_tensor("sq", [128, 8 * CH], F32))
        s1l = e(nc.sbuf_tensor("s1l", [128, 8 * CH], F32))
        s1lb = e(nc.sbuf_tensor("s1lb", [128, 8 * CH], BF16))
        stat = e(nc.sbuf_tensor("stat", [128, 8 * NTT], F32))
        murow = e(nc.sbuf_tensor("murow", [1, 128 * NTT], F32))
        rsrow = e(nc.sbuf_tensor("rsrow", [1, 128 * NTT], F32))
        vout = e(nc.sbuf_tensor("vout", [128, 8 * CH], F32))
        gout = e(nc.sbuf_tensor("gout", [128, 8 * CH], F32))
        ps_st = e(nc.psum_tensor("ps_st", [128, 2], F32))
        ps_tr = e(nc.psum_tensor("ps_tr", [1, 128], F32))
        ps_tr2 = e(nc.psum_tensor("ps_tr2", [1, 128], F32))
        ps_bc = e(nc.psum_tensor("ps_bc", [128, 256], F32))
        ps_a = e(nc.psum_tensor("ps_a", [128, CH], F32))
        ps_b = e(nc.psum_tensor("ps_b", [128, CH], F32))
        b = B(nc, ctx)
        for sname in ("ldc", "ldg", "ldx", "mm", "dv", "ac", "st"):
            b.sem(sname)
        block = e(nc.Block())
        S = dict(ones=ones, ones1=ones1, idm=idm, sq=sq, stat=stat, murow=murow,
                 rsrow=rsrow, ps_st=ps_st, ps_tr=ps_tr, ps_tr2=ps_tr2, ps_bc=ps_bc, epsv=epsv)

        for dst, src in ((idm, id128), (lng, ln_g), (lnb, ln_b), (bo, b_out),
                         (bu, b_u), (bv, b_v), (bh, b_h)):
            b.emit("sync", lambda s, d=dst, sr=src: s.dma_start(out=d[:], in_=sr[:, :]), inc=("ldc", 16))
        b.emit("gpsimd", lambda g: g.dma_start(out=wo[:], in_=wout[:, :]), inc=("ldg", 16))
        b.emit("gpsimd", lambda g: g.dma_start(out=w1[:], in_=win1[:, :]), inc=("ldg", 16))
        ld0 = [("ldc", b.at("ldc")), ("ldg", b.at("ldg"))]
        b.emit("vector", lambda v: v.memset(ones[:], 1.0))
        b.emit("vector", lambda v: v.memset(ones1[:], 1.0))
        b.emit("vector", lambda v: v.memset(epsv[:], EPS), inc="dv")
        b.emit("vector", lambda v: v.tensor_add(bv[:], bv[:], bh[:]), wait=ld0, inc="dv")
        for eng in ("scalar", "tensor"):
            b.emit(eng, lambda x: None, wait=ld0 + ["dv"])

        for c in range(NCH):
            st0 = b.at("st")
            b.emit("gpsimd", lambda g, c=c: g.dma_start(
                out=h0b[:].rearrange("p (u n) -> p u n", u=8),
                in_=h0T.rearrange("p (u n) -> p u n", u=8)[:, :, c * CH:(c + 1) * CH]),
                wait=[("st", st0)] + ld0, inc=("ldg", 16))
            b.emit("sync", lambda s, c=c: s.dma_start(
                out=xnf[:].rearrange("p (u n) -> p u n", u=8),
                in_=xnT.rearrange("p (u n) -> p u n", u=8)[:, :, c * CH:(c + 1) * CH]),
                wait=[("st", st0)] + ld0, inc=("ldx", 16))
            ldv = [("ldx", b.at("ldx")), ("ldg", b.at("ldg")), ("st", st0)]
            for eng in ("vector", "scalar", "tensor"):
                b.emit(eng, lambda x: None, wait=ldv)

            bank_guard = {0: ("dv", 0), 1: ("dv", 0)}
            for ei in range(8):
                pb = ps_a if ei % 2 == 0 else ps_b
                b.emit("tensor", mmgroup(pb[:, :],
                       [(wo[:, (ei * 8 + k) * 128:(ei * 8 + k + 1) * 128], h0b[:, k * CH:(k + 1) * CH]) for k in range(8)]),
                       wait=[bank_guard[ei % 2]], inc="mm")
                b.emit("vector", lambda v, ei=ei, pb=pb: v.tensor_scalar(sq[:, ei * CH:(ei + 1) * CH], pb[:, :], bo[:, ei:ei + 1], None, ALU.add),
                       wait="mm", inc="dv")
                b.emit("vector", lambda v, ei=ei: v.tensor_add(s0o[:, ei * CH:(ei + 1) * CH], sq[:, ei * CH:(ei + 1) * CH], xnf[:, ei * CH:(ei + 1) * CH]),
                       wait="dv", inc="dv")
                bank_guard[ei % 2] = ("dv", b.at("dv"))
            dv_s0 = b.at("dv")
            b.emit("sync", lambda s, c=c: s.dma_start(
                out=s0o_o.rearrange("p (u n) -> p u n", u=8)[:, :, c * CH:(c + 1) * CH],
                in_=s0o[:].rearrange("p (u n) -> p u n", u=8)),
                wait=[("dv", dv_s0)], inc=("st", 16))

            emit_ln(b, S, s0o, s1l, CH, src_tm=None, affine=(lng, lnb), out_bf=s1lb)
            ac_s1lb = b.at("ac")

            bank_guard = {0: ("dv", 0), 1: ("dv", 0)}
            for ei in range(16):
                pb = ps_a if ei % 2 == 0 else ps_b
                b.emit("tensor", mmgroup(pb[:, :],
                       [(w1[:, (ei * 8 + k) * 128:(ei * 8 + k + 1) * 128], s1lb[:, k * CH:(k + 1) * CH]) for k in range(8)]),
                       wait=[("ac", ac_s1lb), bank_guard[ei % 2]], inc="mm")
                if ei < 8:
                    b.emit("scalar", lambda s, ei=ei, pb=pb: s.activation(
                        gout[:, ei * CH:(ei + 1) * CH], pb[:, :], AF.Sigmoid, bias=bu[:, ei:ei + 1]),
                        wait="mm", inc="ac")
                    bank_guard[ei % 2] = ("ac", b.at("ac"))
                else:
                    u = ei - 8
                    b.emit("vector", lambda v, u=u, pb=pb: v.tensor_scalar(
                        vout[:, u * CH:(u + 1) * CH], pb[:, :], bv[:, u:u + 1], None, ALU.add),
                        wait="mm", inc="dv")
                    bank_guard[ei % 2] = ("dv", b.at("dv"))
            b.emit("sync", lambda s, c=c: s.dma_start(
                out=vpre_o.rearrange("p (u n) -> p u n", u=8)[:, :, c * CH:(c + 1) * CH],
                in_=vout[:].rearrange("p (u n) -> p u n", u=8)),
                wait=["dv"], inc=("st", 16))
            b.emit("sync", lambda s, c=c: s.dma_start(
                out=gg_o.rearrange("p (u n) -> p u n", u=8)[:, :, c * CH:(c + 1) * CH],
                in_=gout[:].rearrange("p (u n) -> p u n", u=8)),
                wait=["ac"], inc=("st", 16))
        b.emit("sync", lambda s: None, wait=["st"])
        b.install(block)
    return nc


def build_moeE(N, CH=256, CAP=2560):
    nc = bass.Bass("TRN2", target_bir_lowering=False)
    NCH = N // CH
    xnT = nc.dram_tensor("xnT", [128, 8 * N], F32, kind="ExternalInput").ap()
    w1p = nc.dram_tensor("w1p", [128, 32 * 8 * 128], F32, kind="ExternalInput").ap()
    w2p = nc.dram_tensor("w2p", [128, 8 * 32 * 128], F32, kind="ExternalInput").ap()
    b1p = nc.dram_tensor("b1p", [128, 32], F32, kind="ExternalInput").ap()
    b2p = nc.dram_tensor("b2p", [128, 8], F32, kind="ExternalInput").ap()
    mrow = nc.dram_tensor("mrow", [1, N], F32, kind="ExternalInput").ap()
    grow = nc.dram_tensor("grow", [1, N], F32, kind="ExternalInput").ap()
    on1 = nc.dram_tensor("on1", [1, 128], F32, kind="ExternalInput").ap()
    ctr_o = nc.dram_tensor("ctr_o", [128, 8 * N], F32, kind="ExternalOutput").ap()
    with ExitStack() as ctx:
        e = ctx.enter_context
        w1s = e(nc.sbuf_tensor("w1s", [128, 32 * 8 * 128], BF16))
        w2s = e(nc.sbuf_tensor("w2s", [128, 8 * 32 * 128], BF16))
        b1s = e(nc.sbuf_tensor("b1s", [128, 32], F32))
        b2s = e(nc.sbuf_tensor("b2s", [128, 8], F32))
        SEG = 2048
        mr = e(nc.sbuf_tensor("mr", [1, SEG], F32))
        gr = e(nc.sbuf_tensor("gr", [1, SEG], F32))
        cA = e(nc.sbuf_tensor("cA", [1, SEG], F32))
        cB = e(nc.sbuf_tensor("cB", [1, SEG], F32))
        carry = e(nc.sbuf_tensor("carry", [1, 1], F32))
        wseg = e(nc.sbuf_tensor("wseg", [1, CH], F32))
        wsc = nc.dram_tensor("wsc", [1, N], F32).ap()
        ones1 = e(nc.sbuf_tensor("ones1", [1, 128], F32))
        xnb = e(nc.sbuf_tensor("xnb", [128, 8 * CH], BF16))
        hg = e(nc.sbuf_tensor("hg", [128, 32 * CH], BF16))
        tmp = e(nc.sbuf_tensor("tmp", [128, CH], F32))
        outc = e(nc.sbuf_tensor("outc", [128, 8 * CH], F32))
        ps_w = e(nc.psum_tensor("ps_w", [128, CH], F32))
        ps_h0 = e(nc.psum_tensor("ps_h0", [128, CH], F32))
        ps_h1 = e(nc.psum_tensor("ps_h1", [128, CH], F32))
        ps_o0 = e(nc.psum_tensor("ps_o0", [128, CH], F32))
        ps_o1 = e(nc.psum_tensor("ps_o1", [128, CH], F32))
        b = B(nc, ctx)
        for sname in ("ldc", "ldg", "ldx", "mm", "dv", "ac", "st"):
            b.sem(sname)
        block = e(nc.Block())

        b.emit("gpsimd", lambda g: g.dma_start(out=w1s[:], in_=w1p[:, :]), inc=("ldg", 16))
        b.emit("gpsimd", lambda g: g.dma_start(out=w2s[:], in_=w2p[:, :]), inc=("ldg", 16))
        for dst, src in ((b1s, b1p), (b2s, b2p), (ones1, on1)):
            b.emit("sync", lambda s, d=dst, sr=src: s.dma_start(out=d[:], in_=sr[:, :]), inc=("ldc", 16))
        ld0 = [("ldc", b.at("ldc")), ("ldg", b.at("ldg"))]
        b.emit("vector", lambda v: v.memset(carry[:], 0.0), inc="dv")
        NSEG = N // SEG
        for sg in range(NSEG):
            dvv = b.at("dv")
            b.emit("sync", lambda s, sg=sg: s.dma_start(out=mr[:], in_=mrow[0:1, sg * SEG:(sg + 1) * SEG]),
                   wait=[("dv", dvv), ("st", b.at("st"))], inc=("ldx", 16))
            b.emit("sync", lambda s, sg=sg: s.dma_start(out=gr[:], in_=grow[0:1, sg * SEG:(sg + 1) * SEG]),
                   inc=("ldx", 16))
            ldw = b.at("ldx")
            b.emit("vector", lambda v: v.tensor_copy(cA[:], mr[:]), wait=[("ldx", ldw)], inc="dv")
            cur, oth = cA, cB
            sh = 1
            while sh < SEG:
                b.emit("vector", lambda v, sh=sh, cur=cur, oth=oth: v.tensor_add(
                    oth[0:1, sh:SEG], cur[0:1, sh:SEG], cur[0:1, 0:SEG - sh]), wait="dv", inc="dv")
                b.emit("vector", lambda v, sh=sh, cur=cur, oth=oth: v.tensor_copy(
                    oth[0:1, 0:sh], cur[0:1, 0:sh]), wait="dv", inc="dv")
                cur, oth = oth, cur
                sh *= 2
            b.emit("vector", lambda v, cur=cur: v.tensor_scalar(cur[:], cur[:], carry[0:1, 0:1], None, ALU.add),
                   wait="dv", inc="dv")
            b.emit("vector", lambda v, cur=cur: v.tensor_copy(carry[:], cur[0:1, SEG - 1:SEG]), wait="dv", inc="dv")
            b.emit("vector", lambda v, cur=cur, oth=oth: v.tensor_scalar(oth[:], cur[:], float(CAP), None, ALU.is_le),
                   wait="dv", inc="dv")
            b.emit("vector", lambda v, oth=oth: v.tensor_tensor(oth[:], oth[:], mr[:], ALU.mult), wait="dv", inc="dv")
            b.emit("vector", lambda v, oth=oth: v.tensor_tensor(oth[:], oth[:], gr[:], ALU.mult), wait="dv", inc="dv")
            b.emit("sync", lambda s, sg=sg, oth=oth: s.dma_start(out=wsc[0:1, sg * SEG:(sg + 1) * SEG], in_=oth[:]),
                   wait=["dv"], inc=("st", 16))
        dv_w = b.at("dv")
        st_w = b.at("st")
        for eng in ("scalar", "tensor"):
            b.emit(eng, lambda x: None, wait=ld0 + [("dv", dv_w)])

        for c in range(NCH):
            st0 = b.at("st")
            b.emit("gpsimd", lambda g, c=c: g.dma_start(
                out=xnb[:].rearrange("p (u n) -> p u n", u=8),
                in_=xnT.rearrange("p (u n) -> p u n", u=8)[:, :, c * CH:(c + 1) * CH]),
                wait=[("st", st0)] + ld0, inc=("ldg", 16))
            ldv = [("ldg", b.at("ldg")), ("st", st0)]
            for eng in ("vector", "scalar", "tensor"):
                b.emit(eng, lambda x: None, wait=ldv)
            b.emit("sync", lambda s, c=c: s.dma_start(out=wseg[:], in_=wsc[0:1, c * CH:(c + 1) * CH]),
                   wait=[("st", st_w), ("dv", b.at("dv")), ("mm", b.at("mm"))], inc=("ldx", 16))
            b.emit("tensor", lambda t, c=c: t.matmul(ps_w[:, :], ones1[0:1, :], wseg[0:1, :], start=True, stop=True),
                   wait=["dv", ("ldx", b.at("ldx"))], inc="mm")
            guard = {0: ("ac", 0), 1: ("ac", 0)}
            for hi in range(32):
                pb = ps_h0 if hi % 2 == 0 else ps_h1
                b.emit("tensor", mmgroup(pb[:, :],
                       [(w1s[:, (hi * 8 + k) * 128:(hi * 8 + k + 1) * 128], xnb[:, k * CH:(k + 1) * CH]) for k in range(8)]),
                       wait=[guard[hi % 2]], inc="mm")
                b.emit("scalar", lambda s, hi=hi, pb=pb: s.activation(
                    hg[:, hi * CH:(hi + 1) * CH], pb[:, :], AF.Gelu, bias=b1s[:, hi:hi + 1]),
                    wait="mm", inc="ac")
                guard[hi % 2] = ("ac", b.at("ac"))
            ac_hg = b.at("ac")
            guard = {0: ("dv", 0), 1: ("dv", 0)}
            for di in range(8):
                pb = ps_o0 if di % 2 == 0 else ps_o1
                b.emit("tensor", mmgroup(pb[:, :],
                       [(w2s[:, (di * 32 + k) * 128:(di * 32 + k + 1) * 128], hg[:, k * CH:(k + 1) * CH]) for k in range(32)]),
                       wait=[("ac", ac_hg), guard[di % 2]], inc="mm")
                b.emit("vector", lambda v, di=di, pb=pb: v.tensor_scalar(tmp[:, :], pb[:, :], b2s[:, di:di + 1], None, ALU.add),
                       wait=["mm", "dv"], inc="dv")
                b.emit("vector", lambda v, di=di: v.tensor_tensor(outc[:, di * CH:(di + 1) * CH], tmp[:, :], ps_w[:, :], ALU.mult),
                       wait="dv", inc="dv")
                guard[di % 2] = ("dv", b.at("dv"))
            b.emit("sync", lambda s, c=c: s.dma_start(
                out=ctr_o.rearrange("p (u n) -> p u n", u=8)[:, :, c * CH:(c + 1) * CH],
                in_=outc[:].rearrange("p (u n) -> p u n", u=8)),
                wait=["dv"], inc=("st", 16))
        b.emit("sync", lambda s: None, wait=["st"])
        b.install(block)
    return nc


def build_combF(ntok, CH=512):
    nc = bass.Bass("TRN2", target_bir_lowering=False)
    NCH = ntok // CH
    NTT = CH // 128
    xT = nc.dram_tensor("xT", [128, 8 * ntok], F32, kind="ExternalInput").ap()
    s0T = nc.dram_tensor("s0T", [128, 8 * ntok], F32, kind="ExternalInput").ap()
    h1T = nc.dram_tensor("h1T", [128, 8 * ntok], F32, kind="ExternalInput").ap()
    ctrT = nc.dram_tensor("ctrT", [128, 8 * 8 * ntok], F32, kind="ExternalInput").ap()
    wout = nc.dram_tensor("wout", [128, 8 * 8 * 128], F32, kind="ExternalInput").ap()
    b_out = nc.dram_tensor("b_out", [128, 8], F32, kind="ExternalInput").ap()
    id128 = nc.dram_tensor("id128", [128, 128], F32, kind="ExternalInput").ap()
    x1_o = nc.dram_tensor("x1_o", [128, 8 * ntok], F32, kind="ExternalOutput").ap()
    xn1_o = nc.dram_tensor("xn1_o", [128, 8 * ntok], F32, kind="ExternalOutput").ap()
    with ExitStack() as ctx:
        e = ctx.enter_context
        wo = e(nc.sbuf_tensor("wo", [128, 8 * 8 * 128], BF16))
        idm = e(nc.sbuf_tensor("idm", [128, 128], F32))
        ones = e(nc.sbuf_tensor("ones", [128, 1], F32))
        ones1 = e(nc.sbuf_tensor("ones1", [1, 128], F32))
        epsv = e(nc.sbuf_tensor("epsv", [128, 1], F32))
        bo = e(nc.sbuf_tensor("bo", [128, 8], F32))
        xf = e(nc.sbuf_tensor("xf", [128, 8 * CH], F32))
        s0f = e(nc.sbuf_tensor("s0f", [128, 8 * CH], F32))
        h1b = e(nc.sbuf_tensor("h1b", [128, 8 * CH], BF16))
        cbuf = e(nc.sbuf_tensor("cbuf", [128, 8 * CH], F32))
        acc = e(nc.sbuf_tensor("acc", [128, 8 * CH], F32))
        sq = e(nc.sbuf_tensor("sq", [128, 8 * CH], F32))
        xn1 = e(nc.sbuf_tensor("xn1", [128, 8 * CH], F32))
        stat = e(nc.sbuf_tensor("stat", [128, 8 * NTT], F32))
        murow = e(nc.sbuf_tensor("murow", [1, 128 * NTT], F32))
        rsrow = e(nc.sbuf_tensor("rsrow", [1, 128 * NTT], F32))
        ps_st = e(nc.psum_tensor("ps_st", [128, 2], F32))
        ps_tr = e(nc.psum_tensor("ps_tr", [1, 128], F32))
        ps_tr2 = e(nc.psum_tensor("ps_tr2", [1, 128], F32))
        ps_bc = e(nc.psum_tensor("ps_bc", [128, 256], F32))
        ps_a = e(nc.psum_tensor("ps_a", [128, CH], F32))
        ps_b = e(nc.psum_tensor("ps_b", [128, CH], F32))
        b = B(nc, ctx)
        for sname in ("ldc", "ldg", "ldx", "mm", "dv", "ac", "st"):
            b.sem(sname)
        block = e(nc.Block())
        S = dict(ones=ones, ones1=ones1, idm=idm, sq=sq, stat=stat, murow=murow,
                 rsrow=rsrow, ps_st=ps_st, ps_tr=ps_tr, ps_tr2=ps_tr2, ps_bc=ps_bc, epsv=epsv)

        for dst, src in ((idm, id128), (bo, b_out)):
            b.emit("sync", lambda s, d=dst, sr=src: s.dma_start(out=d[:], in_=sr[:, :]), inc=("ldc", 16))
        b.emit("gpsimd", lambda g: g.dma_start(out=wo[:], in_=wout[:, :]), inc=("ldg", 16))
        ld0 = [("ldc", b.at("ldc")), ("ldg", b.at("ldg"))]
        b.emit("vector", lambda v: v.memset(ones[:], 1.0))
        b.emit("vector", lambda v: v.memset(ones1[:], 1.0))
        b.emit("vector", lambda v: v.memset(epsv[:], EPS), inc="dv")
        for eng in ("scalar", "tensor", "vector"):
            b.emit(eng, lambda x: None, wait=ld0 + ["dv"])

        for c in range(NCH):
            st0 = b.at("st")
            b.emit("sync", lambda s, c=c: s.dma_start(
                out=xf[:].rearrange("p (u n) -> p u n", u=8),
                in_=xT.rearrange("p (u n) -> p u n", u=8)[:, :, c * CH:(c + 1) * CH]),
                wait=[("st", st0)] + ld0, inc=("ldx", 16))
            b.emit("sync", lambda s, c=c: s.dma_start(
                out=s0f[:].rearrange("p (u n) -> p u n", u=8),
                in_=s0T.rearrange("p (u n) -> p u n", u=8)[:, :, c * CH:(c + 1) * CH]),
                inc=("ldx", 16))
            b.emit("gpsimd", lambda g, c=c: g.dma_start(
                out=h1b[:].rearrange("p (u n) -> p u n", u=8),
                in_=h1T.rearrange("p (u n) -> p u n", u=8)[:, :, c * CH:(c + 1) * CH]),
                wait=[("st", st0)] + ld0, inc=("ldg", 16))
            ldv = [("ldx", b.at("ldx")), ("ldg", b.at("ldg")), ("st", st0)]
            for eng in ("vector", "scalar", "tensor"):
                b.emit(eng, lambda x: None, wait=ldv)
            for u in range(8):
                b.emit("vector", lambda v, u=u: v.tensor_add(acc[:, u * CH:(u + 1) * CH], xf[:, u * CH:(u + 1) * CH], s0f[:, u * CH:(u + 1) * CH]),
                       wait=("dv" if u == 0 else ()), inc="dv")
            for ee in range(8):
                dvv = b.at("dv")
                b.emit("sync", lambda s, c=c, ee=ee: s.dma_start(
                    out=cbuf[:].rearrange("p (u n) -> p u n", u=8),
                    in_=ctrT.rearrange("p (e u n) -> p e u n", e=8, u=8)[:, ee, :, c * CH:(c + 1) * CH]),
                    wait=[("dv", dvv)], inc=("ldx", 16))
                b.emit("vector", lambda x2: None, wait=[("ldx", b.at("ldx"))])
                for u in range(8):
                    b.emit("vector", lambda v, u=u: v.tensor_add(acc[:, u * CH:(u + 1) * CH], acc[:, u * CH:(u + 1) * CH], cbuf[:, u * CH:(u + 1) * CH]),
                           inc="dv")
            guard = {0: ("dv", 0), 1: ("dv", 0)}
            for ei in range(8):
                pb = ps_a if ei % 2 == 0 else ps_b
                b.emit("tensor", mmgroup(pb[:, :],
                       [(wo[:, (ei * 8 + k) * 128:(ei * 8 + k + 1) * 128], h1b[:, k * CH:(k + 1) * CH]) for k in range(8)]),
                       wait=[guard[ei % 2]], inc="mm")
                b.emit("vector", lambda v, ei=ei, pb=pb: v.tensor_scalar(sq[:, ei * CH:(ei + 1) * CH], pb[:, :], bo[:, ei:ei + 1], None, ALU.add),
                       wait=["mm", "dv"], inc="dv")
                b.emit("vector", lambda v, ei=ei: v.tensor_add(acc[:, ei * CH:(ei + 1) * CH], acc[:, ei * CH:(ei + 1) * CH], sq[:, ei * CH:(ei + 1) * CH]),
                       wait="dv", inc="dv")
                guard[ei % 2] = ("dv", b.at("dv"))
            dv_x1 = b.at("dv")
            b.emit("sync", lambda s, c=c: s.dma_start(
                out=x1_o.rearrange("p (u n) -> p u n", u=8)[:, :, c * CH:(c + 1) * CH],
                in_=acc[:].rearrange("p (u n) -> p u n", u=8)),
                wait=[("dv", dv_x1)], inc=("st", 16))
            emit_ln(b, S, acc, xn1, CH, src_tm=None, affine=None)
            b.emit("sync", lambda s, c=c: s.dma_start(
                out=xn1_o.rearrange("p (u n) -> p u n", u=8)[:, :, c * CH:(c + 1) * CH],
                in_=xn1[:].rearrange("p (u n) -> p u n", u=8)),
                wait=["dv"], inc=("st", 16))
        b.emit("sync", lambda s: None, wait=["st"])
        b.install(block)
    return nc


def build_qkvG1(ntok, CH=512):
    nc = bass.Bass("TRN2", target_bir_lowering=False)
    NCH = ntok // CH
    xn1T = nc.dram_tensor("xn1T", [128, 8 * ntok], F32, kind="ExternalInput").ap()
    wdq = nc.dram_tensor("wdq", [128, 32 * 8 * 128], F32, kind="ExternalInput").ap()
    wuq = nc.dram_tensor("wuq", [128, 8 * 32 * 128], F32, kind="ExternalInput").ap()
    wqr = nc.dram_tensor("wqr", [128, 4 * 32 * 128], F32, kind="ExternalInput").ap()
    wdkv = nc.dram_tensor("wdkv", [128, 2 * 8 * 128], F32, kind="ExternalInput").ap()
    wuk = nc.dram_tensor("wuk", [128, 8 * 2 * 128], F32, kind="ExternalInput").ap()
    wuv = nc.dram_tensor("wuv", [128, 8 * 2 * 128], F32, kind="ExternalInput").ap()
    wkr = nc.dram_tensor("wkr", [128, 1 * 8 * 32], F32, kind="ExternalInput").ap()
    scq = nc.dram_tensor("scq", [128, 4 * ntok], F32, kind="ExternalInput").ap()
    sck = nc.dram_tensor("sck", [32, ntok], F32, kind="ExternalInput").ap()
    qc_o = nc.dram_tensor("qc_o", [128, 8 * ntok], F32, kind="ExternalOutput").ap()
    qr_o = nc.dram_tensor("qr_o", [128, 4 * ntok], F32, kind="ExternalOutput").ap()
    kc_o = nc.dram_tensor("kc_o", [128, 8 * ntok], F32, kind="ExternalOutput").ap()
    kr_o = nc.dram_tensor("kr_o", [32, ntok], F32, kind="ExternalOutput").ap()
    v_o = nc.dram_tensor("v_o", [128, 8 * ntok], F32, kind="ExternalOutput").ap()
    with ExitStack() as ctx:
        e = ctx.enter_context
        wdqs = e(nc.sbuf_tensor("wdqs", [128, 32 * 8 * 128], BF16))
        wuqs = e(nc.sbuf_tensor("wuqs", [128, 8 * 32 * 128], BF16))
        wqrs = e(nc.sbuf_tensor("wqrs", [128, 4 * 32 * 128], BF16))
        wdkvs = e(nc.sbuf_tensor("wdkvs", [128, 2 * 8 * 128], BF16))
        wuks = e(nc.sbuf_tensor("wuks", [128, 8 * 2 * 128], BF16))
        wuvs = e(nc.sbuf_tensor("wuvs", [128, 8 * 2 * 128], BF16))
        wkrs = e(nc.sbuf_tensor("wkrs", [128, 8 * 32], BF16))
        scqs = e(nc.sbuf_tensor("scqs", [128, 4 * CH], F32))
        scks = e(nc.sbuf_tensor("scks", [32, CH], F32))
        xnb = e(nc.sbuf_tensor("xnb", [128, 8 * CH], BF16))
        cqb = e(nc.sbuf_tensor("cqb", [128, 32 * CH], BF16))
        ckvb = e(nc.sbuf_tensor("ckvb", [128, 2 * CH], BF16))
        ot = e(nc.sbuf_tensor("ot", [128, 8 * CH], F32))
        ps_a = e(nc.psum_tensor("ps_a", [128, CH], F32))
        ps_b = e(nc.psum_tensor("ps_b", [128, CH], F32))
        b = B(nc, ctx)
        for sname in ("ldc", "ldg", "ldx", "mm", "dv", "ac", "st"):
            b.sem(sname)
        block = e(nc.Block())
        for dst, src in ((wdqs, wdq), (wuqs, wuq), (wqrs, wqr), (wdkvs, wdkv),
                         (wuks, wuk), (wuvs, wuv), (wkrs, wkr)):
            b.emit("gpsimd", lambda g, d=dst, sr=src: g.dma_start(out=d[:], in_=sr[:, :]), inc=("ldg", 16))
        ld0 = [("ldg", b.at("ldg"))]
        for eng in ("scalar", "tensor", "vector"):
            b.emit(eng, lambda x: None, wait=ld0)

        for c in range(NCH):
            st0 = b.at("st")
            b.emit("gpsimd", lambda g, c=c: g.dma_start(
                out=xnb[:].rearrange("p (u n) -> p u n", u=8),
                in_=xn1T.rearrange("p (u n) -> p u n", u=8)[:, :, c * CH:(c + 1) * CH]),
                wait=[("st", st0)] + ld0, inc=("ldg", 16))
            b.emit("sync", lambda s, c=c: s.dma_start(
                out=scqs[:].rearrange("p (u n) -> p u n", u=4),
                in_=scq.rearrange("p (u n) -> p u n", u=4)[:, :, c * CH:(c + 1) * CH]),
                wait=[("st", st0)], inc=("ldx", 16))
            b.emit("sync", lambda s, c=c: s.dma_start(
                out=scks[:], in_=sck[:, c * CH:(c + 1) * CH]), inc=("ldx", 16))
            ldv = [("ldx", b.at("ldx")), ("ldg", b.at("ldg")), ("st", st0)]
            for eng in ("vector", "scalar", "tensor"):
                b.emit(eng, lambda x: None, wait=ldv)

            guard = {0: ("ac", 0), 1: ("ac", 0)}
            for ei in range(32):
                pb = ps_a if ei % 2 == 0 else ps_b
                b.emit("tensor", mmgroup(pb[:, :],
                       [(wdqs[:, (ei * 8 + k) * 128:(ei * 8 + k + 1) * 128], xnb[:, k * CH:(k + 1) * CH]) for k in range(8)]),
                       wait=[guard[ei % 2]], inc="mm")
                b.emit("scalar", lambda s, ei=ei, pb=pb: s.copy(cqb[:, ei * CH:(ei + 1) * CH], pb[:, :]),
                       wait="mm", inc="ac")
                guard[ei % 2] = ("ac", b.at("ac"))
            ac_cq = b.at("ac")
            guard = {0: ("dv", 0), 1: ("dv", 0)}
            for ei in range(8):
                pb = ps_a if ei % 2 == 0 else ps_b
                b.emit("tensor", mmgroup(pb[:, :],
                       [(wuqs[:, (ei * 32 + k) * 128:(ei * 32 + k + 1) * 128], cqb[:, k * CH:(k + 1) * CH]) for k in range(32)]),
                       wait=[("ac", ac_cq), guard[ei % 2]], inc="mm")
                b.emit("vector", lambda v, ei=ei, pb=pb: v.tensor_copy(ot[:, ei * CH:(ei + 1) * CH], pb[:, :]),
                       wait=["mm", "dv"], inc="dv")
                guard[ei % 2] = ("dv", b.at("dv"))
            b.emit("sync", lambda s, c=c: s.dma_start(
                out=qc_o.rearrange("p (u n) -> p u n", u=8)[:, :, c * CH:(c + 1) * CH],
                in_=ot[:].rearrange("p (u n) -> p u n", u=8)), wait=["dv"], inc=("st", 16))
            st_ot = ("st", b.at("st"))
            guard = {0: ("dv", b.at("dv")), 1: ("dv", b.at("dv"))}
            for ei in range(4):
                pb = ps_a if ei % 2 == 0 else ps_b
                b.emit("tensor", mmgroup(pb[:, :],
                       [(wqrs[:, (ei * 32 + k) * 128:(ei * 32 + k + 1) * 128], cqb[:, k * CH:(k + 1) * CH]) for k in range(32)]),
                       wait=[("ac", ac_cq), guard[ei % 2]], inc="mm")
                b.emit("vector", lambda v, ei=ei, pb=pb: v.tensor_tensor(ot[:, ei * CH:(ei + 1) * CH], pb[:, :], scqs[:, ei * CH:(ei + 1) * CH], ALU.mult),
                       wait=["mm", "dv", st_ot], inc="dv")
                guard[ei % 2] = ("dv", b.at("dv"))
            b.emit("sync", lambda s, c=c: s.dma_start(
                out=qr_o.rearrange("p (u n) -> p u n", u=4)[:, :, c * CH:(c + 1) * CH],
                in_=ot[:, 0:4 * CH].rearrange("p (u n) -> p u n", u=4)), wait=["dv"], inc=("st", 16))
            guard = {0: ("dv", b.at("dv")), 1: ("dv", b.at("dv"))}
            for ei in range(2):
                pb = ps_a if ei % 2 == 0 else ps_b
                b.emit("tensor", mmgroup(pb[:, :],
                       [(wdkvs[:, (ei * 8 + k) * 128:(ei * 8 + k + 1) * 128], xnb[:, k * CH:(k + 1) * CH]) for k in range(8)]),
                       wait=[guard[ei % 2]], inc="mm")
                b.emit("scalar", lambda s, ei=ei, pb=pb: s.copy(ckvb[:, ei * CH:(ei + 1) * CH], pb[:, :]),
                       wait="mm", inc="ac")
                guard[ei % 2] = ("ac", b.at("ac"))
            ac_ckv = b.at("ac")
            for wsrc, dst_o in ((wuks, kc_o), (wuvs, v_o)):
                st_ot = None
                guard = {0: ("dv", b.at("dv")), 1: ("dv", b.at("dv"))}
                for ei in range(8):
                    pb = ps_a if ei % 2 == 0 else ps_b
                    b.emit("tensor", mmgroup(pb[:, :],
                           [(wsrc[:, (ei * 2 + k) * 128:(ei * 2 + k + 1) * 128], ckvb[:, k * CH:(k + 1) * CH]) for k in range(2)]),
                           wait=[("ac", ac_ckv), guard[ei % 2]], inc="mm")
                    if st_ot is None:
                        st_ot = ("st", b.at("st"))
                    b.emit("vector", lambda v, ei=ei, pb=pb: v.tensor_copy(ot[:, ei * CH:(ei + 1) * CH], pb[:, :]),
                           wait=["mm", "dv", st_ot], inc="dv")
                    guard[ei % 2] = ("dv", b.at("dv"))
                b.emit("sync", lambda s, c=c, dst_o=dst_o: s.dma_start(
                    out=dst_o.rearrange("p (u n) -> p u n", u=8)[:, :, c * CH:(c + 1) * CH],
                    in_=ot[:].rearrange("p (u n) -> p u n", u=8)), wait=["dv"], inc=("st", 16))
            b.emit("tensor", mmgroup(ps_a[0:32, :],
                   [(wkrs[:, k * 32:(k + 1) * 32], xnb[:, k * CH:(k + 1) * CH]) for k in range(8)]),
                   wait=["dv"], inc="mm")
            b.emit("vector", lambda v: v.tensor_tensor(ot[0:32, 0:CH], ps_a[0:32, :], scks[0:32, :], ALU.mult),
                   wait=["mm", "dv", ("st", b.at("st"))], inc="dv")
            b.emit("sync", lambda s, c=c: s.dma_start(
                out=kr_o[:, c * CH:(c + 1) * CH], in_=ot[0:32, 0:CH]), wait=["dv"], inc=("st", 16))
        b.emit("sync", lambda s: None, wait=["st"])
        b.install(block)
    return nc


def build_ffnG2(ntok, CH=256):
    nc = bass.Bass("TRN2", target_bir_lowering=False)
    NCH = ntok // CH
    xn1T = nc.dram_tensor("xn1T", [128, 8 * ntok], F32, kind="ExternalInput").ap()
    x1T = nc.dram_tensor("x1T", [128, 8 * ntok], F32, kind="ExternalInput").ap()
    w1p = nc.dram_tensor("w1p", [128, 32 * 8 * 128], F32, kind="ExternalInput").ap()
    w2p = nc.dram_tensor("w2p", [128, 8 * 32 * 128], F32, kind="ExternalInput").ap()
    b1p = nc.dram_tensor("b1p", [128, 32], F32, kind="ExternalInput").ap()
    b2p = nc.dram_tensor("b2p", [128, 8], F32, kind="ExternalInput").ap()
    op_o = nc.dram_tensor("op_o", [128, 8 * ntok], F32, kind="ExternalOutput").ap()
    with ExitStack() as ctx:
        e = ctx.enter_context
        w1f = e(nc.sbuf_tensor("w1f", [128, 32 * 8 * 128], BF16))
        w2f = e(nc.sbuf_tensor("w2f", [128, 8 * 32 * 128], BF16))
        wbuf = e(nc.sbuf_tensor("wbuf", [128, 32 * 128], F32))
        abuf = e(nc.sbuf_tensor("abuf", [128, 32 * 128], BF16))
        b1s = e(nc.sbuf_tensor("b1s", [128, 32], F32))
        b2s = e(nc.sbuf_tensor("b2s", [128, 8], F32))
        al1 = e(nc.sbuf_tensor("al1", [128, 32], F32))
        al2 = e(nc.sbuf_tensor("al2", [128, 8], F32))
        ones = e(nc.sbuf_tensor("ones", [128, 1], BF16))
        xnb = e(nc.sbuf_tensor("xnb", [128, 8 * CH], BF16))
        x1f = e(nc.sbuf_tensor("x1f", [128, 8 * CH], F32))
        hgb = e(nc.sbuf_tensor("hgb", [128, 32 * CH], BF16))
        tmp = e(nc.sbuf_tensor("tmp", [128, CH], F32))
        ot = e(nc.sbuf_tensor("ot", [128, 8 * CH], F32))
        ps_a = e(nc.psum_tensor("ps_a", [128, CH], F32))
        ps_b = e(nc.psum_tensor("ps_b", [128, CH], F32))
        ps_al = e(nc.psum_tensor("ps_al", [128, 1], F32))
        b = B(nc, ctx)
        for sname in ("ldc", "ldg", "ldx", "mm", "dv", "ac", "st"):
            b.sem(sname)
        block = e(nc.Block())

        b.emit("sync", lambda s: s.dma_start(out=b1s[:], in_=b1p[:, :]), inc=("ldc", 16))
        b.emit("sync", lambda s: s.dma_start(out=b2s[:], in_=b2p[:, :]), inc=("ldc", 16))
        b.emit("vector", lambda v: v.memset(ones[:], 1.0), inc="dv")
        def prep(wp, wf, nI, nK, alo, denom):
            for i in range(nI):
                acv, mmv = b.at("ac"), b.at("mm")
                b.emit("sync", lambda s, i=i, wp=wp, nK=nK: s.dma_start(
                    out=wbuf[:, 0:nK * 128], in_=wp[:, i * nK * 128:(i + 1) * nK * 128]),
                    wait=[("ac", acv), ("mm", mmv)], inc=("ldx", 16))
                ldw = b.at("ldx")
                b.emit("scalar", lambda s, i=i, wf=wf, nK=nK: s.activation(
                    wf[:, i * nK * 128:(i + 1) * nK * 128], wbuf[:, 0:nK * 128], AF.Sign),
                    wait=[("ldx", ldw)], inc="ac")
                b.emit("scalar", lambda s, nK=nK: s.activation(abuf[:, 0:nK * 128], wbuf[:, 0:nK * 128], AF.Abs),
                    wait="ac", inc="ac")
                b.emit("tensor", mmgroup(ps_al[:, :],
                       [(abuf[:, k * 128:(k + 1) * 128], ones[:, :]) for k in range(nK)]),
                       wait=["ac", "dv"], inc="mm")
                b.emit("vector", lambda v, i=i, alo=alo, denom=denom: v.tensor_scalar(
                    alo[:, i:i + 1], ps_al[:, :], 1.0 / denom, None, ALU.mult),
                    wait="mm", inc="dv")
        prep(w1p, w1f, 32, 8, al1, 1024.0)
        prep(w2p, w2f, 8, 32, al2, 4096.0)
        ld0 = [("ldc", b.at("ldc")), ("dv", b.at("dv")), ("ac", b.at("ac"))]
        for eng in ("scalar", "tensor", "vector"):
            b.emit(eng, lambda x: None, wait=ld0)

        for c in range(NCH):
            st0 = b.at("st")
            b.emit("gpsimd", lambda g, c=c: g.dma_start(
                out=xnb[:].rearrange("p (u n) -> p u n", u=8),
                in_=xn1T.rearrange("p (u n) -> p u n", u=8)[:, :, c * CH:(c + 1) * CH]),
                wait=[("st", st0)] + ld0, inc=("ldg", 16))
            b.emit("sync", lambda s, c=c: s.dma_start(
                out=x1f[:].rearrange("p (u n) -> p u n", u=8),
                in_=x1T.rearrange("p (u n) -> p u n", u=8)[:, :, c * CH:(c + 1) * CH]),
                wait=[("st", st0)] + ([("ldx", b.at("ldx")), ("ldc", 32)] if c == 0 else []), inc=("ldc", 16))
            ldv = [("ldc", b.at("ldc")), ("ldg", b.at("ldg")), ("st", st0)]
            for eng in ("vector", "scalar", "tensor"):
                b.emit(eng, lambda x: None, wait=ldv)
            guard = {0: ("ac", 0), 1: ("ac", 0)}
            for hi in range(32):
                pb = ps_a if hi % 2 == 0 else ps_b
                b.emit("tensor", mmgroup(pb[:, :],
                       [(w1f[:, (hi * 8 + k) * 128:(hi * 8 + k + 1) * 128], xnb[:, k * CH:(k + 1) * CH]) for k in range(8)]),
                       wait=[guard[hi % 2]], inc="mm")
                b.emit("vector", lambda v, hi=hi, pb=pb: v.tensor_scalar(
                    tmp[:, :], pb[:, :], al1[:, hi:hi + 1], b1s[:, hi:hi + 1], ALU.mult, ALU.add),
                    wait=["mm", "dv", "ac"], inc="dv")
                b.emit("scalar", lambda s, hi=hi: s.activation(hgb[:, hi * CH:(hi + 1) * CH], tmp[:, :], AF.Gelu),
                       wait="dv", inc="ac")
                guard[hi % 2] = ("ac", b.at("ac"))
            ac_hg = b.at("ac")
            guard = {0: ("dv", 0), 1: ("dv", 0)}
            for di in range(8):
                pb = ps_a if di % 2 == 0 else ps_b
                b.emit("tensor", mmgroup(pb[:, :],
                       [(w2f[:, (di * 32 + k) * 128:(di * 32 + k + 1) * 128], hgb[:, k * CH:(k + 1) * CH]) for k in range(32)]),
                       wait=[("ac", ac_hg), guard[di % 2]], inc="mm")
                b.emit("vector", lambda v, di=di, pb=pb: v.tensor_scalar(
                    ot[:, di * CH:(di + 1) * CH], pb[:, :], al2[:, di:di + 1], b2s[:, di:di + 1], ALU.mult, ALU.add),
                    wait=["mm", "dv"], inc="dv")
                b.emit("vector", lambda v, di=di: v.tensor_add(
                    ot[:, di * CH:(di + 1) * CH], ot[:, di * CH:(di + 1) * CH], x1f[:, di * CH:(di + 1) * CH]),
                    wait="dv", inc="dv")
                guard[di % 2] = ("dv", b.at("dv"))
            b.emit("sync", lambda s, c=c: s.dma_start(
                out=op_o.rearrange("p (u n) -> p u n", u=8)[:, :, c * CH:(c + 1) * CH],
                in_=ot[:].rearrange("p (u n) -> p u n", u=8)), wait=["dv"], inc=("st", 16))
        b.emit("sync", lambda s: None, wait=["st"])
        b.install(block)
    return nc


def build_attnH(T=8192, NH=4):
    nc = bass.Bass("TRN2", target_bir_lowering=False)
    NQ = T // 128
    NKM = T // 512
    qd = [nc.dram_tensor(f"q{h}", [96, T], F32, kind="ExternalInput").ap() for h in range(NH)]
    kd = [nc.dram_tensor(f"k{h}", [96, T], F32, kind="ExternalInput").ap() for h in range(NH)]
    vd = [nc.dram_tensor(f"v{h}", [T, 64], F32, kind="ExternalInput").ap() for h in range(NH)]
    msk = nc.dram_tensor("msk", [128, 4 * 512], F32, kind="ExternalInput").ap()
    idd = nc.dram_tensor("idd", [128, 128], F32, kind="ExternalInput").ap()
    yo = [nc.dram_tensor(f"y{h}", [T, 64], F32, kind="ExternalOutput").ap() for h in range(NH)]
    SCALE = 1.0 / np.sqrt(96.0)
    with ExitStack() as ctx:
        e = ctx.enter_context
        qs = e(nc.sbuf_tensor("qs", [96, T], BF16))
        ks = e(nc.sbuf_tensor("ks", [96, T], BF16))
        vs = e(nc.sbuf_tensor("vs", [128, (T // 128) * 64], BF16))
        mks = e(nc.sbuf_tensor("mks", [128, 4 * 512], F32))
        idb = e(nc.sbuf_tensor("idb", [128, 128], BF16))
        sm = e(nc.sbuf_tensor("sm", [128, T], F32))
        pb_ = e(nc.sbuf_tensor("pb_", [128, T], BF16))
        pt = e(nc.sbuf_tensor("pt", [128, 512], BF16))
        osb = e(nc.sbuf_tensor("osb", [128, 64], F32))
        red = e(nc.sbuf_tensor("red", [128, 4], F32))
        ps_s = e(nc.psum_tensor("ps_s", [128, 512], F32))
        ps_t = e(nc.psum_tensor("ps_t", [128, 512], BF16))
        ps_pv = e(nc.psum_tensor("ps_pv", [128, 64], F32))
        b = B(nc, ctx)
        for sname in ("ldg", "mm", "dv", "ac", "st"):
            b.sem(sname)
        block = e(nc.Block())
        b.emit("gpsimd", lambda g: g.dma_start(out=mks[:], in_=msk[:, :]), inc=("ldg", 16))
        b.emit("gpsimd", lambda g: g.dma_start(out=idb[:], in_=idd[:, :]), inc=("ldg", 16))

        for h in range(NH):
            stH = b.at("st")
            dvH, acH, mmH = b.at("dv"), b.at("ac"), b.at("mm")
            b.emit("gpsimd", lambda g, h=h: g.dma_start(out=qs[0:96, :], in_=qd[h][:, :]),
                   wait=[("st", stH), ("dv", dvH), ("mm", mmH)], inc=("ldg", 16))
            b.emit("gpsimd", lambda g, h=h: g.dma_start(out=ks[0:96, :], in_=kd[h][:, :]), inc=("ldg", 16))
            b.emit("gpsimd", lambda g, h=h: g.dma_start(
                out=vs[:].rearrange("p (j d) -> p j d", d=64),
                in_=vd[h].rearrange("(j p) d -> p j d", p=128)), inc=("ldg", 16))
            ldv = [("ldg", b.at("ldg"))]
            for eng in ("vector", "scalar", "tensor"):
                b.emit(eng, lambda x: None, wait=ldv)
            for qi in range(NQ):
                nkm = qi // 4 + 1
                nk = nkm * 512
                dg = qi // 4
                r = qi % 4
                for km in range(nkm):
                    b.emit("tensor", lambda t, qi=qi, km=km: t.matmul(
                        ps_s[:, :], qs[0:96, 128 * qi:128 * (qi + 1)], ks[0:96, 512 * km:512 * (km + 1)],
                        start=True, stop=True), wait=["dv"], inc="mm")
                    if km == dg:
                        b.emit("vector", lambda v, km=km, r=r: v.scalar_tensor_tensor(
                            sm[:, 512 * km:512 * (km + 1)], ps_s[:, :], SCALE, mks[:, 512 * r:512 * (r + 1)],
                            ALU.mult, ALU.add), wait=["mm", "ac"], inc="dv")
                    else:
                        b.emit("vector", lambda v, km=km: v.tensor_scalar(
                            sm[:, 512 * km:512 * (km + 1)], ps_s[:, :], SCALE, None, ALU.mult),
                            wait=["mm", "ac"], inc="dv")
                b.emit("vector", lambda v, nk=nk: v.reduce_max(red[:, 0:1], sm[:, 0:nk], axis=AX), wait="dv", inc="dv")
                b.emit("vector", lambda v: v.tensor_scalar(red[:, 1:2], red[:, 0:1], -1.0, None, ALU.mult),
                       wait="dv", inc="dv")
                b.emit("scalar", lambda s, nk=nk: s.activation(pb_[:, 0:nk], sm[:, 0:nk], AF.Exp, bias=red[:, 1:2]),
                       wait="dv", inc="ac")
                b.emit("vector", lambda v, nk=nk: v.reduce_sum(red[:, 2:3], pb_[:, 0:nk], axis=AX), wait="ac", inc="dv")
                b.emit("vector", lambda v: v.reciprocal(red[:, 3:4], red[:, 2:3]), wait="dv", inc="dv")
                nkt = nk // 128
                for kt in range(nkt):
                    sl = kt % 4
                    if sl == 0:
                        dvv = b.at("dv")
                        nsl = min(4, nkt - kt)
                        b.emit("tensor", lambda t, qi=qi, kt=kt, nsl=nsl: [
                            t.transpose(ps_t[:, 128 * j:128 * (j + 1)], pb_[:, 128 * (kt + j):128 * (kt + j + 1)], idb[:, :])
                            for j in range(nsl)][-1], wait=[("dv", dvv), "ac"], inc="mm")
                        b.emit("vector", lambda v, nsl=nsl: v.tensor_copy(pt[:, 0:128 * nsl], ps_t[:, 0:128 * nsl]),
                               wait="mm", inc="dv")
                    b.emit("tensor", lambda t, kt=kt, sl=sl, qi=qi: t.matmul(
                        ps_pv[:, :], pt[:, 128 * sl:128 * (sl + 1)], vs[:, 64 * kt:64 * (kt + 1)],
                        start=(kt == 0), stop=(kt == nkt - 1)),
                        wait=(["dv"] if sl == 0 else ()), inc=("mm" if kt == nkt - 1 else None))
                b.emit("vector", lambda v: v.tensor_scalar(osb[:, :], ps_pv[:, :], red[:, 3:4], None, ALU.mult),
                       wait=["mm", "dv"], inc="dv")
                b.emit("sync", lambda s, h=h, qi=qi: s.dma_start(
                    out=yo[h][128 * qi:128 * (qi + 1), :], in_=osb[:, :]), wait=["dv"], inc=("st", 16))
        b.emit("sync", lambda s: None, wait=["st"])
        b.install(block)
    return nc


def build_finalI(ntok, CH=512):
    nc = bass.Bass("TRN2", target_bir_lowering=False)
    NCH = ntok // CH
    yaT = nc.dram_tensor("yaT", [128, 8 * ntok], F32, kind="ExternalInput").ap()
    opT = nc.dram_tensor("opT", [128, 8 * ntok], F32, kind="ExternalInput").ap()
    wop = nc.dram_tensor("wop", [128, 8 * 8 * 128], F32, kind="ExternalInput").ap()
    out_o = nc.dram_tensor("out_o", [128, 8 * ntok], F32, kind="ExternalOutput").ap()
    with ExitStack() as ctx:
        e = ctx.enter_context
        wo = e(nc.sbuf_tensor("wo", [128, 8 * 8 * 128], BF16))
        yab = e(nc.sbuf_tensor("yab", [128, 8 * CH], BF16))
        opf = e(nc.sbuf_tensor("opf", [128, 8 * CH], F32))
        ot = e(nc.sbuf_tensor("ot", [128, 8 * CH], F32))
        ps_a = e(nc.psum_tensor("ps_a", [128, CH], F32))
        ps_b = e(nc.psum_tensor("ps_b", [128, CH], F32))
        b = B(nc, ctx)
        for sname in ("ldc", "ldg", "ldx", "mm", "dv", "ac", "st"):
            b.sem(sname)
        block = e(nc.Block())
        b.emit("gpsimd", lambda g: g.dma_start(out=wo[:], in_=wop[:, :]), inc=("ldg", 16))
        ld0 = [("ldg", b.at("ldg"))]
        for c in range(NCH):
            st0 = b.at("st")
            b.emit("gpsimd", lambda g, c=c: g.dma_start(
                out=yab[:].rearrange("p (u n) -> p u n", u=8),
                in_=yaT.rearrange("p (u n) -> p u n", u=8)[:, :, c * CH:(c + 1) * CH]),
                wait=[("st", st0)] + ld0, inc=("ldg", 16))
            b.emit("sync", lambda s, c=c: s.dma_start(
                out=opf[:].rearrange("p (u n) -> p u n", u=8),
                in_=opT.rearrange("p (u n) -> p u n", u=8)[:, :, c * CH:(c + 1) * CH]),
                wait=[("st", st0)], inc=("ldx", 16))
            ldv = [("ldx", b.at("ldx")), ("ldg", b.at("ldg")), ("st", st0)]
            for eng in ("vector", "scalar", "tensor"):
                b.emit(eng, lambda x: None, wait=ldv)
            guard = {0: ("dv", 0), 1: ("dv", 0)}
            for ei in range(8):
                pb = ps_a if ei % 2 == 0 else ps_b
                b.emit("tensor", mmgroup(pb[:, :],
                       [(wo[:, (ei * 8 + k) * 128:(ei * 8 + k + 1) * 128], yab[:, k * CH:(k + 1) * CH]) for k in range(8)]),
                       wait=[guard[ei % 2]], inc="mm")
                b.emit("vector", lambda v, ei=ei, pb=pb: v.tensor_add(
                    ot[:, ei * CH:(ei + 1) * CH], pb[:, :], opf[:, ei * CH:(ei + 1) * CH]),
                    wait=["mm", "dv"], inc="dv")
                guard[ei % 2] = ("dv", b.at("dv"))
            b.emit("sync", lambda s, c=c: s.dma_start(
                out=out_o.rearrange("p (u n) -> p u n", u=8)[:, :, c * CH:(c + 1) * CH],
                in_=ot[:].rearrange("p (u n) -> p u n", u=8)), wait=["dv"], inc=("st", 16))
        b.emit("sync", lambda s: None, wait=["st"])
        b.install(block)
    return nc


"""Orchestration: multi-launch kernel using klib builders."""

N = 16384
T = 8192
NTOK = 2048
CORES = list(range(8))


def assemble_scan(parts):
    a = np.stack([p.reshape(128, 8, NTOK) for p in parts])
    a = a.reshape(2, 4, 128, 8, NTOK)
    a = a.transpose(2, 1, 4, 3, 0)
    return np.ascontiguousarray(a.reshape(128, T * 16))


def slice_scan(h, c):
    b, tc = c // 4, c % 4
    a = h.reshape(128, T, 8, 2)[:, tc * NTOK:(tc + 1) * NTOK, :, b]
    return np.ascontiguousarray(a.transpose(0, 2, 1).reshape(128, 8 * NTOK))


def rope_scale():
    t = np.arange(T, dtype=np.float32)
    inv = 1.0 / 10000.0 ** (np.arange(0, 32, 2, dtype=np.float32) / 32.0)
    fr = t[:, None] * inv[None, :]
    emb = np.concatenate([fr, fr], -1)
    return (np.cos(emb) + np.sin(emb)).astype(np.float32)


def kernel(**inp):
    x = np.asarray(inp["L0_attn_x"], dtype=np.float32)
    xflat = np.ascontiguousarray(x.reshape(N, 1024))
    ln_g, ln_b = inp["L0_attn_ln_g"], inp["L0_attn_ln_b"]
    W_in, b_in = inp["L0_attn_W_in"], inp["L0_attn_b_in"]
    W_h, b_h = inp["L0_attn_W_h"], inp["L0_attn_b_h"]
    W_out, b_out = inp["L0_attn_W_out"], inp["L0_attn_b_out"]

    ncA = build_preA(NTOK)
    imsA = []
    for c in CORES:
        im = preA_inputs(xflat[c * NTOK:(c + 1) * NTOK], W_in[0], ln_g[0], ln_b[0],
                         b_in[0], b_h[0], inp["L0_ffn_router_w"], inp["L0_ffn_router_b"])
        imsA.append(im)
    rA = run_bass_kernel_spmd(ncA, imsA, core_ids=CORES).results
    xn_parts = [r["xn_o"] for r in rA]
    vpre0 = assemble_scan([r["vpre_o"] for r in rA])
    g0 = assemble_scan([r["gg_o"] for r in rA])
    mask = np.concatenate([r["mask_o"] for r in rA], 0)
    gates = np.concatenate([r["gates_o"] for r in rA], 0)

    ncS = build_scan(T)
    w0 = pack_lhsT(np.ascontiguousarray(W_h[0].T))
    h0 = run_bass_kernel_spmd(ncS, [{"w": w0, "vpre": vpre0, "g": g0}], core_ids=[0]).results[0]["hout"]

    ncC = build_midC(NTOK)
    woutp0 = pack_lhsT(np.ascontiguousarray(W_out[0].T))
    win1p = pack_lhsT(np.ascontiguousarray(W_in[1].T))
    imsC = []
    for c in CORES:
        imsC.append({
            "h0T": slice_scan(h0, c), "xnT": xn_parts[c],
            "wout": woutp0, "win1": win1p,
            "ln_g": vec_pm(ln_g[1]), "ln_b": vec_pm(ln_b[1]),
            "b_out": vec_pm(b_out[0]),
            "b_u": vec_pm(b_in[1][:1024]), "b_v": vec_pm(b_in[1][1024:]), "b_h": vec_pm(b_h[1]),
            "id128": np.eye(128, dtype=np.float32),
        })
    rC = run_bass_kernel_spmd(ncC, imsC, core_ids=CORES).results
    s0o_parts = [r["s0o_o"] for r in rC]
    vpre1 = assemble_scan([r["vpre_o"] for r in rC])
    g1 = assemble_scan([r["gg_o"] for r in rC])

    w1s = pack_lhsT(np.ascontiguousarray(W_h[1].T))
    h1 = run_bass_kernel_spmd(ncS, [{"w": w1s, "vpre": vpre1, "g": g1}], core_ids=[0]).results[0]["hout"]

    ncE = build_moeE(N)
    xn_full = np.concatenate(
        [p.reshape(128, 8, NTOK) for p in xn_parts], axis=2)
    xn_full = np.ascontiguousarray(xn_full.reshape(128, 8 * N))
    imsE = []
    for c in CORES:
        imsE.append({
            "xnT": xn_full,
            "w1p": pack_lhsT(np.ascontiguousarray(inp["L0_ffn_w1"][c])),
            "w2p": pack_lhsT(np.ascontiguousarray(inp["L0_ffn_w2"][c])),
            "b1p": vec_pm(inp["L0_ffn_b1"][c]), "b2p": vec_pm(inp["L0_ffn_b2"][c]),
            "mrow": np.ascontiguousarray(mask[:, c].reshape(1, N)),
            "grow": np.ascontiguousarray(gates[:, c].reshape(1, N)),
            "on1": np.ones((1, 128), np.float32),
        })
    rE = run_bass_kernel_spmd(ncE, imsE, core_ids=CORES).results
    ctrs = [r["ctr_o"].reshape(128, 8, N) for r in rE]

    ncF = build_combF(NTOK)
    wout1p = pack_lhsT(np.ascontiguousarray(W_out[1].T))
    imsF = []
    for c in CORES:
        ctr_c = np.stack([ct[:, :, c * NTOK:(c + 1) * NTOK] for ct in ctrs], axis=1)
        imsF.append({
            "xT": fm(xflat[c * NTOK:(c + 1) * NTOK]),
            "s0T": s0o_parts[c], "h1T": slice_scan(h1, c),
            "ctrT": np.ascontiguousarray(ctr_c.reshape(128, 8 * 8 * NTOK)),
            "wout": wout1p, "b_out": vec_pm(b_out[1]),
            "id128": np.eye(128, dtype=np.float32),
        })
    rF = run_bass_kernel_spmd(ncF, imsF, core_ids=CORES).results
    x1_parts = [r["x1_o"] for r in rF]
    xn1_parts = [r["xn1_o"] for r in rF]

    ncG1 = build_qkvG1(NTOK, CH=256)
    sc = rope_scale()
    wdq = pack_lhsT(inp["L1_attn_W_DQ"])
    wuq = pack_lhsT(inp["L1_attn_W_UQ"])
    wqr = pack_lhsT(inp["L1_attn_W_QR"])
    wdkv = pack_lhsT(inp["L1_attn_W_DKV"])
    wuk = pack_lhsT(inp["L1_attn_W_UK"])
    wuv = pack_lhsT(inp["L1_attn_W_UV"])
    wkr = pack_lhsT(inp["L1_attn_W_KR"])
    imsG1 = []
    for c in CORES:
        tr = np.arange(c % 4 * NTOK, (c % 4 + 1) * NTOK)
        scq = np.empty((128, 4 * NTOK), np.float32)
        for u4 in range(4):
            rows = np.arange(128) + 128 * u4
            scq[:, u4 * NTOK:(u4 + 1) * NTOK] = sc[tr][:, rows % 32].T
        sck = np.ascontiguousarray(sc[tr].T)
        imsG1.append({
            "xn1T": xn1_parts[c], "wdq": wdq, "wuq": wuq, "wqr": wqr,
            "wdkv": wdkv, "wuk": wuk, "wuv": wuv, "wkr": wkr,
            "scq": scq, "sck": sck,
        })
    rG1 = run_bass_kernel_spmd(ncG1, imsG1, core_ids=CORES).results

    ncG2 = build_ffnG2(NTOK, CH=256)
    w1p = pack_lhsT(np.ascontiguousarray(inp["L1_ffn_w1"].T))
    w2p = pack_lhsT(np.ascontiguousarray(inp["L1_ffn_w2"].T))
    imsG2 = []
    for c in CORES:
        imsG2.append({"xn1T": xn1_parts[c], "x1T": x1_parts[c],
                      "w1p": w1p, "w2p": w2p,
                      "b1p": vec_pm(inp["L1_ffn_b1"]), "b2p": vec_pm(inp["L1_ffn_b2"])})
    rG2 = run_bass_kernel_spmd(ncG2, imsG2, core_ids=CORES).results
    op_parts = [r["op_o"] for r in rG2]

    def asm(key, rows):
        return np.concatenate([rG1[c][key].reshape(128, rows // 128, NTOK) for c in CORES], axis=2) \
            .reshape(128, rows // 128, N)
    qc = asm("qc_o", 1024)
    qr = asm("qr_o", 512)
    kc = asm("kc_o", 1024)
    vv = asm("v_o", 1024)
    kr = np.concatenate([rG1[c]["kr_o"] for c in CORES], axis=1)

    def frows(a, r0, r1):
        out = np.empty((r1 - r0, a.shape[2]), np.float32)
        for r in range(r0, r1):
            out[r - r0] = a[r % 128, r // 128]
        return out

    ncH = build_attnH(T, 4)
    mvar = np.full((128, 4, 512), -1e30, np.float32)
    for r in range(4):
        for p in range(128):
            mvar[p, r, :128 * r + p + 1] = 0.0
    mflat = np.ascontiguousarray(mvar.reshape(128, 4 * 512))
    imsH = []
    for c in CORES:
        im = {"msk": mflat, "idd": np.eye(128, dtype=np.float32)}
        for i in range(4):
            pair = 4 * c + i
            bb, hh = pair // 16, pair % 16
            tok = slice(bb * T, (bb + 1) * T)
            qC = frows(qc[:, :, tok], hh * 64, (hh + 1) * 64)
            qR = frows(qr[:, :, tok], hh * 32, (hh + 1) * 32)
            kC = frows(kc[:, :, tok], hh * 64, (hh + 1) * 64)
            vh = frows(vv[:, :, tok], hh * 64, (hh + 1) * 64)
            im[f"q{i}"] = np.ascontiguousarray(np.concatenate([qC, qR], 0))
            im[f"k{i}"] = np.ascontiguousarray(np.concatenate([kC, kr[:, tok]], 0))
            im[f"v{i}"] = np.ascontiguousarray(vh.T)
        imsH.append(im)
    rH = run_bass_kernel_spmd(ncH, imsH, core_ids=CORES).results
    y_att = np.empty((N, 1024), np.float32)
    for c in CORES:
        for i in range(4):
            pair = 4 * c + i
            bb, hh = pair // 16, pair % 16
            y_att[bb * T:(bb + 1) * T, hh * 64:(hh + 1) * 64] = rH[c][f"y{i}"]

    ncI = build_finalI(NTOK)
    wo = pack_lhsT(inp["L1_attn_W_O"])
    imsI = []
    for c in CORES:
        imsI.append({"yaT": fm(y_att[c * NTOK:(c + 1) * NTOK]),
                     "opT": op_parts[c], "wop": wo})
    rI = run_bass_kernel_spmd(ncI, imsI, core_ids=CORES).results
    out = np.concatenate([unfm(r["out_o"], NTOK, 1024) for r in rI], 0)
    return out.reshape(2, T, 1024).astype(np.float32)


# version 07
# speedup vs baseline: 1.0022x; 1.0016x over previous
import numpy as np
import ml_dtypes
from contextlib import ExitStack
import concourse.bass as bass
import concourse.mybir as mybir
from concourse.bass_utils import run_bass_kernel_spmd

F32 = mybir.dt.float32
BF16 = mybir.dt.bfloat16
AF = mybir.ActivationFunctionType
ALU = mybir.AluOpType
AX = mybir.AxisListType.X
BFNP = ml_dtypes.bfloat16
ENGINES = ("sync", "gpsimd", "vector", "scalar", "tensor")
EPS = 1e-5


class B:

    def __init__(self, nc, ctx):
        self.nc = nc
        self.ctx = ctx
        self.ops = {e: [] for e in ENGINES}
        self.sems = {}
        self.cnt = {}

    def sem(self, name):
        h = self.ctx.enter_context(self.nc.semaphore(name))
        self.sems[name] = h
        self.cnt[name] = 0
        return h

    def at(self, name):
        return self.cnt[name]

    def emit(self, eng, fn, wait=(), inc=None):
        if isinstance(wait, (str, tuple)) and (not wait or isinstance(wait, str) or isinstance(wait[1], int)):
            wait = [wait] if wait else []
        wl = []
        for w in wait:
            if isinstance(w, str):
                wl.append((self.sems[w], self.cnt[w]))
            else:
                wl.append((self.sems[w[0]], w[1]))
        inch = None
        amt = 0
        if inc is not None:
            if isinstance(inc, tuple):
                name, amt = inc
            else:
                name, amt = inc, 1
            self.cnt[name] += amt
            inch = self.sems[name]

        def runner(e, fn=fn, wl=tuple(wl), inch=inch, amt=amt):
            for h, v in wl:
                if v > 0:
                    e.wait_ge(h, v)
            inst = fn(e)
            if inch is not None:
                assert inst is not None, "emit with inc= must return an instruction"
                inst.then_inc(inch, amt)
        self.ops[eng].append(runner)

    def install(self, block):
        for name, fns in self.ops.items():
            def runner(e, fns=tuple(fns)):
                for f in fns:
                    f(e)
            getattr(block, name)(runner)


def mmgroup(pb, pairs):
    def fn(t, pb=pb, pairs=tuple(pairs)):
        n = len(pairs)
        inst = None
        for q, (l, r) in enumerate(pairs):
            inst = t.matmul(pb, l, r, start=(q == 0), stop=(q == n - 1))
        return inst
    return fn


def pack_lhsT(wt):
    K, M = wt.shape
    KT = K // 128
    Mw = min(M, 128)
    MT = M // Mw
    return np.ascontiguousarray(
        wt.reshape(KT, 128, MT, Mw).transpose(2, 0, 1, 3).reshape(MT * KT, 128, Mw)
        .transpose(1, 0, 2).reshape(128, MT * KT * Mw))


def vec_pm(v):
    D = v.shape[0]
    return np.ascontiguousarray(v.reshape(D // 128, 128).T)


def fm(x):
    N, D = x.shape
    NT = D // 128
    return np.ascontiguousarray(x.T.reshape(NT, 128, N).transpose(1, 0, 2).reshape(128, NT * N))


def unfm(a, N, D):
    NT = D // 128
    return np.ascontiguousarray(a.reshape(128, NT, N).transpose(1, 0, 2).reshape(D, N).T)


def tm(x):
    N, D = x.shape
    return np.ascontiguousarray(x.reshape(N // 128, 128, D).transpose(1, 0, 2).reshape(128, (N // 128) * D))


def untm(a, N, D):
    return np.ascontiguousarray(a.reshape(128, N // 128, D).transpose(1, 0, 2).reshape(N, D))


def emit_ln(b, S, src_f32, dst, CH, *, src_tm=None, affine=None, out_bf=None):
    NTT = CH // 128
    sq, stat, murow, rsrow = S["sq"], S["stat"], S["murow"], S["rsrow"]
    ps_st, ps_tr, ps_tr2, ps_bc = S["ps_st"], S["ps_tr"], S["ps_tr2"], S["ps_bc"]
    ones, ones1, idm = S["ones"], S["ones1"], S["idm"]

    for j in range(NTT):
        if src_tm is not None:
            b.emit("scalar", lambda s, j=j: s.square(sq[:, 0:1024], src_tm[:, 1024 * j:1024 * (j + 1)]),
                   wait=("dv" if j > 0 else ()), inc="ac")
            b.emit("vector", lambda v, j=j: v.reduce_sum(stat[:, 8 * j:8 * j + 1], src_tm[:, 1024 * j:1024 * (j + 1)], axis=AX))
            b.emit("vector", lambda v, j=j: v.reduce_sum(stat[:, 8 * j + 1:8 * j + 2], sq[:, 0:1024], axis=AX),
                   wait="ac", inc="dv")
            b.emit("vector", lambda v, j=j: v.tensor_scalar(stat[:, 8 * j:8 * j + 2], stat[:, 8 * j:8 * j + 2], 1.0 / 1024.0, None, ALU.mult),
                   wait="dv", inc="dv")
        else:
            if j == 0:
                b.emit("scalar", lambda s: s.square(sq[:], src_f32[:]), wait="dv", inc="ac")
            b.emit("tensor", mmgroup(ps_st[:, 0:1],
                   [(src_f32[:, u * CH + 128 * j:u * CH + 128 * (j + 1)], ones[:, :]) for u in range(8)]),
                   wait=("ac" if j == 0 else "dv"), inc="mm")
            b.emit("tensor", mmgroup(ps_st[:, 1:2],
                   [(sq[:, u * CH + 128 * j:u * CH + 128 * (j + 1)], ones[:, :]) for u in range(8)]),
                   inc="mm")
            b.emit("vector", lambda v, j=j: v.tensor_scalar(stat[:, 8 * j:8 * j + 2], ps_st[:, 0:2], 1.0 / 1024.0, None, ALU.mult),
                   wait="mm", inc="dv")
        b.emit("vector", lambda v, j=j: v.tensor_tensor(stat[:, 8 * j + 2:8 * j + 3], stat[:, 8 * j:8 * j + 1], stat[:, 8 * j:8 * j + 1], ALU.mult),
               wait="dv", inc="dv")
        b.emit("vector", lambda v, j=j: v.tensor_tensor(stat[:, 8 * j + 3:8 * j + 4], stat[:, 8 * j + 1:8 * j + 2], stat[:, 8 * j + 2:8 * j + 3], ALU.subtract),
               wait="dv", inc="dv")
        b.emit("scalar", lambda s, j=j: s.activation(stat[:, 8 * j + 4:8 * j + 5], stat[:, 8 * j + 3:8 * j + 4], AF.Sqrt, bias=S["epsv"][:, 0:1]),
               wait="dv", inc="ac")
        b.emit("vector", lambda v, j=j: v.reciprocal(stat[:, 8 * j + 5:8 * j + 6], stat[:, 8 * j + 4:8 * j + 5]),
               wait="ac", inc="dv")
        def trfn(t, j=j):
            t.transpose(ps_tr[0:1, 0:128], stat[:, 8 * j:8 * j + 1], idm[:, :])
            return t.transpose(ps_tr2[0:1, 0:128], stat[:, 8 * j + 5:8 * j + 6], idm[:, :])
        b.emit("tensor", trfn, wait="dv", inc="mm")
        b.emit("vector", lambda v, j=j: v.tensor_copy(murow[0:1, 128 * j:128 * (j + 1)], ps_tr[0:1, 0:128]),
               wait="mm")
        b.emit("vector", lambda v, j=j: v.tensor_copy(rsrow[0:1, 128 * j:128 * (j + 1)], ps_tr2[0:1, 0:128]),
               inc="dv")
        def bcfn(t, j=j):
            t.matmul(ps_bc[:, 0:128], ones1[:, :], murow[0:1, 128 * j:128 * (j + 1)], start=True, stop=True)
            return t.matmul(ps_bc[:, 128:256], ones1[:, :], rsrow[0:1, 128 * j:128 * (j + 1)], start=True, stop=True)
        b.emit("tensor", bcfn, wait="dv", inc="mm")
        for u in range(8):
            sl = slice(u * CH + 128 * j, u * CH + 128 * (j + 1))
            b.emit("vector", lambda v, sl=sl: v.tensor_tensor(dst[:, sl], src_f32[:, sl], ps_bc[:, 0:128], ALU.subtract),
                   wait=("mm" if u == 0 else ()), inc="dv")
            b.emit("vector", lambda v, sl=sl: v.tensor_tensor(dst[:, sl], dst[:, sl], ps_bc[:, 128:256], ALU.mult),
                   wait="dv", inc="dv")
            if affine is not None:
                ga, ba = affine
                b.emit("vector", lambda v, sl=sl, u=u: v.tensor_scalar(dst[:, sl], dst[:, sl], ga[:, u:u + 1], ba[:, u:u + 1], ALU.mult, ALU.add),
                       wait="dv", inc="dv")
    if out_bf is not None:
        b.emit("scalar", lambda s: s.copy(out_bf[:], dst[:]), wait="dv", inc="ac")


def build_preA(ntok, CH=512, stages=(1, 2, 3, 4, 5, 6, 7)):
    nc = bass.Bass("TRN2", target_bir_lowering=False)
    NCH = ntok // CH
    NTT = CH // 128
    xT = nc.dram_tensor("xT", [128, 8 * ntok], F32, kind="ExternalInput").ap()
    xTM = nc.dram_tensor("xTM", [ntok, 1024], F32, kind="ExternalInput").ap()
    win = nc.dram_tensor("win", [128, 16 * 8 * 128], F32, kind="ExternalInput").ap()
    ln_g = nc.dram_tensor("ln_g", [128, 8], F32, kind="ExternalInput").ap()
    ln_b = nc.dram_tensor("ln_b", [128, 8], F32, kind="ExternalInput").ap()
    b_u = nc.dram_tensor("b_u", [128, 8], F32, kind="ExternalInput").ap()
    b_v = nc.dram_tensor("b_v", [128, 8], F32, kind="ExternalInput").ap()
    b_h = nc.dram_tensor("b_h", [128, 8], F32, kind="ExternalInput").ap()
    rw = nc.dram_tensor("rw", [128, 8 * 8], F32, kind="ExternalInput").ap()
    rb_bc = nc.dram_tensor("rb_bc", [128, 8], F32, kind="ExternalInput").ap()
    iota8 = nc.dram_tensor("iota8", [128, 8], F32, kind="ExternalInput").ap()
    id128 = nc.dram_tensor("id128", [128, 128], F32, kind="ExternalInput").ap()
    xn_o = nc.dram_tensor("xn_o", [128, 8 * ntok], F32, kind="ExternalOutput").ap()
    vpre_o = nc.dram_tensor("vpre_o", [128, 8 * ntok], F32, kind="ExternalOutput").ap()
    gg_o = nc.dram_tensor("gg_o", [128, 8 * ntok], F32, kind="ExternalOutput").ap()
    mask_o = nc.dram_tensor("mask_o", [ntok, 8], F32, kind="ExternalOutput").ap()
    gates_o = nc.dram_tensor("gates_o", [ntok, 8], F32, kind="ExternalOutput").ap()

    with ExitStack() as ctx:
        e = ctx.enter_context
        wsb = e(nc.sbuf_tensor("wsb", [128, 16 * 8 * 128], BF16))
        idm = e(nc.sbuf_tensor("idm", [128, 128], F32))
        ones = e(nc.sbuf_tensor("ones", [128, 1], F32))
        ones1 = e(nc.sbuf_tensor("ones1", [1, 128], F32))
        epsv = e(nc.sbuf_tensor("epsv", [128, 1], F32))
        lng = e(nc.sbuf_tensor("lng", [128, 8], F32))
        lnb = e(nc.sbuf_tensor("lnb", [128, 8], F32))
        bu = e(nc.sbuf_tensor("bu", [128, 8], F32))
        bv = e(nc.sbuf_tensor("bv", [128, 8], F32))
        bh = e(nc.sbuf_tensor("bh", [128, 8], F32))
        rws = e(nc.sbuf_tensor("rws", [128, 64], F32))
        rwh = e(nc.sbuf_tensor("rwh", [128, 64], BF16))
        rwl = e(nc.sbuf_tensor("rwl", [128, 64], BF16))
        rwt = e(nc.sbuf_tensor("rwt", [128, 64], F32))
        xlb = e(nc.sbuf_tensor("xlb", [128, 8 * CH], BF16))
        rbbc = e(nc.sbuf_tensor("rbbc", [128, 8], F32))
        iot = e(nc.sbuf_tensor("iot", [128, 8], F32))
        xf = e(nc.sbuf_tensor("xf", [128, 8 * CH], F32))
        xtm = e(nc.sbuf_tensor("xtm", [128, NTT * 1024], F32))
        sq = e(nc.sbuf_tensor("sq", [128, 8 * CH], F32))
        xn = e(nc.sbuf_tensor("xn", [128, 8 * CH], F32))
        xnbf = e(nc.sbuf_tensor("xnbf", [128, 8 * CH], BF16))
        s0l = e(nc.sbuf_tensor("s0l", [128, 8 * CH], F32))
        s0lb = e(nc.sbuf_tensor("s0lb", [128, 8 * CH], BF16))
        stat = e(nc.sbuf_tensor("stat", [128, 8 * NTT], F32))
        murow = e(nc.sbuf_tensor("murow", [1, 128 * NTT], F32))
        rsrow = e(nc.sbuf_tensor("rsrow", [1, 128 * NTT], F32))
        vout = e(nc.sbuf_tensor("vout", [128, 8 * CH], F32))
        gout = e(nc.sbuf_tensor("gout", [128, 8 * CH], F32))
        rmask = e(nc.sbuf_tensor("rmask", [128, 8 * NTT], F32))
        rgate = e(nc.sbuf_tensor("rgate", [128, 8 * NTT], F32))
        rt1 = e(nc.sbuf_tensor("rt1", [128, 8], F32))
        rt2 = e(nc.sbuf_tensor("rt2", [128, 8], F32))
        rt3 = e(nc.sbuf_tensor("rt3", [128, 8], F32))
        rsc = e(nc.sbuf_tensor("rsc", [128, 4], F32))
        ps_st = e(nc.psum_tensor("ps_st", [128, 2], F32))
        ps_tr = e(nc.psum_tensor("ps_tr", [1, 128], F32))
        ps_tr2 = e(nc.psum_tensor("ps_tr2", [1, 128], F32))
        ps_bc = e(nc.psum_tensor("ps_bc", [128, 256], F32))
        ps_uv0 = e(nc.psum_tensor("ps_uv0", [128, CH], F32))
        ps_uv1 = e(nc.psum_tensor("ps_uv1", [128, CH], F32))
        ps_rt = e(nc.psum_tensor("ps_rt", [128, 8], F32))
        b = B(nc, ctx)
        for sname in ("ldc", "ldg", "ldx", "mm", "dv", "ac", "st"):
            b.sem(sname)
        block = e(nc.Block())
        S = dict(ones=ones, ones1=ones1, idm=idm, sq=sq, stat=stat, murow=murow,
                 rsrow=rsrow, ps_st=ps_st, ps_tr=ps_tr, ps_tr2=ps_tr2, ps_bc=ps_bc, epsv=epsv)

        def dve(fn):
            b.emit("vector", fn, wait=["dv", "ac"], inc="dv")

        for dst, src in ((idm, id128), (lng, ln_g), (lnb, ln_b), (bu, b_u),
                         (bv, b_v), (bh, b_h), (rbbc, rb_bc), (iot, iota8)):
            b.emit("sync", lambda s, d=dst, sr=src: s.dma_start(out=d[:], in_=sr[:, :]), inc=("ldc", 16))
        b.emit("gpsimd", lambda g: g.dma_start(out=wsb[:], in_=win[:, :]), inc=("ldg", 16))
        b.emit("gpsimd", lambda g: g.dma_start(out=rws[:], in_=rw[:, :]), inc=("ldg", 16))
        ld0 = [("ldc", b.at("ldc")), ("ldg", b.at("ldg"))]
        b.emit("vector", lambda v: v.memset(ones[:], 1.0))
        b.emit("vector", lambda v: v.memset(ones1[:], 1.0))
        b.emit("vector", lambda v: v.memset(epsv[:], EPS), inc="dv")
        b.emit("vector", lambda v: v.tensor_add(bv[:], bv[:], bh[:]), wait=ld0, inc="dv")
        b.emit("scalar", lambda s: s.copy(rwh[:], rws[:]), wait=ld0, inc="ac")
        b.emit("scalar", lambda s: s.copy(rwt[:], rwh[:]), wait="ac", inc="ac")
        b.emit("vector", lambda v: v.tensor_tensor(rwt[:], rws[:], rwt[:], ALU.subtract), wait="ac", inc="dv")
        b.emit("scalar", lambda s: s.copy(rwl[:], rwt[:]), wait="dv", inc="ac")
        for eng in ("scalar", "tensor"):
            b.emit(eng, lambda x: None, wait=ld0 + ["dv"])

        for c in range(NCH):
            st0 = b.at("st")
            b.emit("sync", lambda s, c=c: s.dma_start(
                out=xf[:].rearrange("p (u n) -> p u n", u=8),
                in_=xT.rearrange("p (u n) -> p u n", u=8)[:, :, c * CH:(c + 1) * CH]),
                wait=[("st", st0)] + ld0, inc=("ldx", 16))
            b.emit("sync", lambda s, c=c: s.dma_start(
                out=xtm[:].rearrange("p (j d) -> p j d", d=1024),
                in_=xTM[c * CH:(c + 1) * CH, :].rearrange("(j p) d -> p j d", p=128)),
                inc=("ldx", 16))
            ldv = b.at("ldx")
            for eng in ("vector", "scalar", "tensor"):
                b.emit(eng, lambda x: None, wait=[("ldx", ldv), ("ldg", b.at("ldg")), ("st", st0)] + ld0)

            emit_ln(b, S, xf, xn, CH, src_tm=xtm, affine=None, out_bf=xnbf)
            dv_xn = b.at("dv")
            b.emit("sync", lambda s, c=c: s.dma_start(
                out=xn_o.rearrange("p (u n) -> p u n", u=8)[:, :, c * CH:(c + 1) * CH],
                in_=xn[:].rearrange("p (u n) -> p u n", u=8)),
                wait=[("dv", dv_xn)], inc=("st", 16))

            if 2 not in stages:
                continue
            emit_ln(b, S, xn, s0l, CH, src_tm=None, affine=(lng, lnb), out_bf=s0lb)
            ac_s0lb = b.at("ac")

            do_uv = 3 in stages
            bank_guard = {0: ("ac", 0), 1: ("ac", 0)}
            for ei in (range(16) if do_uv else ()):
                pb = ps_uv0 if ei % 2 == 0 else ps_uv1
                waits = [("ac", ac_s0lb), bank_guard[ei % 2]]
                b.emit("tensor", mmgroup(pb[:, :],
                       [(wsb[:, (ei * 8 + k) * 128:(ei * 8 + k + 1) * 128], s0lb[:, k * CH:(k + 1) * CH]) for k in range(8)]),
                       wait=waits, inc="mm")
                if ei < 8:
                    b.emit("scalar", lambda s, ei=ei, pb=pb: s.activation(
                        gout[:, ei * CH:(ei + 1) * CH], pb[:, :], AF.Sigmoid, bias=bu[:, ei:ei + 1]),
                        wait="mm", inc="ac")
                    bank_guard[ei % 2] = ("ac", b.at("ac"))
                else:
                    u = ei - 8
                    b.emit("vector", lambda v, u=u, pb=pb: v.tensor_scalar(
                        vout[:, u * CH:(u + 1) * CH], pb[:, :], bv[:, u:u + 1], None, ALU.add),
                        wait="mm", inc="dv")
                    bank_guard[ei % 2] = ("dv", b.at("dv"))
            if do_uv:
                b.emit("sync", lambda s, c=c: s.dma_start(
                    out=vpre_o.rearrange("p (u n) -> p u n", u=8)[:, :, c * CH:(c + 1) * CH],
                    in_=vout[:].rearrange("p (u n) -> p u n", u=8)),
                    wait=["dv"], inc=("st", 16))
                b.emit("sync", lambda s, c=c: s.dma_start(
                    out=gg_o.rearrange("p (u n) -> p u n", u=8)[:, :, c * CH:(c + 1) * CH],
                    in_=gout[:].rearrange("p (u n) -> p u n", u=8)),
                    wait=["ac"], inc=("st", 16))

            if 4 not in stages:
                continue
            b.emit("scalar", lambda s: s.copy(s0l[:], xnbf[:]), wait=["dv", "mm"], inc="ac")
            b.emit("vector", lambda v: v.tensor_tensor(sq[:], xn[:], s0l[:], ALU.subtract), wait="ac", inc="dv")
            b.emit("scalar", lambda s: s.copy(xlb[:], sq[:]), wait="dv", inc="ac")
            ac_split = b.at("ac")
            rt_guard = ("dv", 0)
            for j in range(NTT):
                b.emit("tensor", mmgroup(ps_rt[:, :],
                       [(xa[:, u * CH + 128 * j:u * CH + 128 * (j + 1)], wb[:, u * 8:(u + 1) * 8])
                        for (xa, wb) in ((xnbf, rwh), (xnbf, rwl), (xlb, rwh), (xlb, rwl))
                        for u in range(8)]),
                       wait=[("ac", ac_split), rt_guard], inc="mm")
                base = 8 * j
                b.emit("vector", lambda v: v.tensor_tensor(rt1[:, :], ps_rt[:, :], rbbc[:, :], ALU.add),
                       wait=["dv", "mm", "ac"], inc="dv")
                rt_guard = ("dv", b.at("dv"))
                if 5 not in stages:
                    continue
                dve(lambda v: v.reduce_max(rsc[:, 0:1], rt1[:, :], axis=AX))
                dve(lambda v: v.tensor_scalar(rsc[:, 1:2], rsc[:, 0:1], -1.0, None, ALU.mult))
                b.emit("scalar", lambda s: s.activation(rt2[:, :], rt1[:, :], AF.Exp, bias=rsc[:, 1:2]),
                       wait="dv", inc="ac")
                b.emit("vector", lambda v: v.reduce_sum(rsc[:, 2:3], rt2[:, :], axis=AX), wait="ac", inc="dv")
                dve(lambda v: v.reciprocal(rsc[:, 3:4], rsc[:, 2:3]))
                dve(lambda v, b8=base: v.tensor_scalar(rgate[:, b8:b8 + 8], rt2[:, :], rsc[:, 3:4], None, ALU.mult))
                if 6 not in stages:
                    continue
                dve(lambda v, b8=base: v.reduce_max(rsc[:, 0:1], rgate[:, b8:b8 + 8], axis=AX))
                dve(lambda v, b8=base: v.tensor_scalar(rt1[:, :], rgate[:, b8:b8 + 8], rsc[:, 0:1], None, ALU.is_ge))
                dve(lambda v: v.tensor_scalar(rt2[:, :], rt1[:, :], -1e9, 1e9, ALU.mult, ALU.add))
                dve(lambda v: v.tensor_tensor(rt2[:, :], rt2[:, :], iot[:, :], ALU.add))
                dve(lambda v: v.tensor_reduce(rsc[:, 0:1], rt2[:, :], AX, ALU.min))
                dve(lambda v: v.tensor_scalar(rt1[:, :], rt2[:, :], rsc[:, 0:1], None, ALU.is_le))
                if 7 not in stages:
                    dve(lambda v, b8=base: v.tensor_copy(rmask[:, b8:b8 + 8], rt1[:, :]))
                    continue
                dve(lambda v, b8=base: v.tensor_scalar(rt2[:, :], rt1[:, :], -2.0, None, ALU.mult))
                dve(lambda v, b8=base: v.tensor_tensor(rt2[:, :], rt2[:, :], rgate[:, b8:b8 + 8], ALU.add))
                dve(lambda v: v.reduce_max(rsc[:, 0:1], rt2[:, :], axis=AX))
                dve(lambda v: v.tensor_scalar(rt3[:, :], rt2[:, :], rsc[:, 0:1], None, ALU.is_ge))
                dve(lambda v: v.tensor_scalar(rt2[:, :], rt3[:, :], -1e9, 1e9, ALU.mult, ALU.add))
                dve(lambda v: v.tensor_tensor(rt2[:, :], rt2[:, :], iot[:, :], ALU.add))
                dve(lambda v: v.tensor_reduce(rsc[:, 0:1], rt2[:, :], AX, ALU.min))
                dve(lambda v: v.tensor_scalar(rt3[:, :], rt2[:, :], rsc[:, 0:1], None, ALU.is_le))
                dve(lambda v, b8=base: v.tensor_tensor(rmask[:, b8:b8 + 8], rt1[:, :], rt3[:, :], ALU.max))
            b.emit("sync", lambda s, c=c: s.dma_start(
                out=mask_o[c * CH:(c + 1) * CH, :].rearrange("(j p) d -> p j d", p=128),
                in_=rmask[:, 0:8 * NTT].rearrange("p (j d) -> p j d", d=8)), wait=["dv"], inc=("st", 16))
            b.emit("sync", lambda s, c=c: s.dma_start(
                out=gates_o[c * CH:(c + 1) * CH, :].rearrange("(j p) d -> p j d", p=128),
                in_=rgate[:, 0:8 * NTT].rearrange("p (j d) -> p j d", d=8)), inc=("st", 16))

        b.emit("sync", lambda s: None, wait=["st"])
        b.install(block)
    return nc


def preA_inputs(x_slice, W_in0, ln_g0, ln_b0, b_in0, b_h0, router_w, router_b):
    ntok = x_slice.shape[0]
    return {
        "xT": fm(x_slice),
        "xTM": np.ascontiguousarray(x_slice),
        "win": pack_lhsT(np.ascontiguousarray(W_in0.T)),
        "ln_g": vec_pm(ln_g0), "ln_b": vec_pm(ln_b0),
        "b_u": vec_pm(b_in0[:1024]), "b_v": vec_pm(b_in0[1024:]), "b_h": vec_pm(b_h0),
        "rw": np.ascontiguousarray(router_w.reshape(8, 128, 8).transpose(1, 0, 2).reshape(128, 64)),
        "rb_bc": np.ascontiguousarray(np.tile(router_b[None, :], (128, 1)).astype(np.float32)),
        "iota8": np.ascontiguousarray(np.tile(np.arange(8, dtype=np.float32)[None, :], (128, 1))),
        "id128": np.eye(128, dtype=np.float32),
    }



def build_scan(T, CHUNK=128):
    nc = bass.Bass("TRN2", target_bir_lowering=False)
    w = nc.dram_tensor("w", [128, 64 * 128], F32, kind="ExternalInput").ap()
    vpre = nc.dram_tensor("vpre", [128, 16 * T], F32, kind="ExternalInput").ap()
    g = nc.dram_tensor("g", [128, 16 * T], F32, kind="ExternalInput").ap()
    hout = nc.dram_tensor("hout", [128, 16 * T], F32, kind="ExternalOutput").ap()
    nch = T // CHUNK
    with ExitStack() as ctx:
        e = ctx.enter_context
        ws = e(nc.sbuf_tensor("ws", [128, 64 * 128], BF16))
        vs = e(nc.sbuf_tensor("vs", [128, 16 * CHUNK * 2], F32))
        gs = e(nc.sbuf_tensor("gs", [128, 16 * CHUNK * 2], F32))
        gms = e(nc.sbuf_tensor("gms", [128, 16 * CHUNK * 2], F32))
        hall = e(nc.sbuf_tensor("hall", [128, 16 * CHUNK * 2], F32))
        z16 = e(nc.sbuf_tensor("z16", [128, 16], F32))
        hb = e(nc.sbuf_tensor("hb", [128, 16], BF16))
        zs = e(nc.sbuf_tensor("zs", [128, 16], F32))
        cs = e(nc.sbuf_tensor("cs", [128, 16], F32))
        t1 = e(nc.sbuf_tensor("t1", [128, 16], F32))
        t2 = e(nc.sbuf_tensor("t2", [128, 16], F32))
        ps = e(nc.psum_tensor("ps", [128, 16], F32))
        ldg = e(nc.semaphore("ldg"))
        ldx0 = e(nc.semaphore("ldx0"))
        ldx1 = e(nc.semaphore("ldx1"))
        mm_s = e(nc.semaphore("mm_s"))
        z_s = e(nc.semaphore("z_s"))
        c_s = e(nc.semaphore("c_s"))
        h_s = e(nc.semaphore("h_s"))
        e_s = e(nc.semaphore("e_s"))
        gm_s = e(nc.semaphore("gm_s"))
        hb_s = e(nc.semaphore("hb_s"))
        stc = e(nc.semaphore("stc"))
        block = e(nc.Block())

        @block.gpsimd
        def _(gp):
            gp.dma_start(out=ws[:], in_=w[:, :]).then_inc(ldg, 16)

        @block.sync
        def _(sy):
            for c in range(nch):
                buf = c % 2
                sem = ldx0 if buf == 0 else ldx1
                if c >= 2:
                    sy.wait_ge(h_s, (c - 1) * CHUNK)
                    sy.wait_ge(stc, 16 * (c - 1))
                sy.dma_start(out=vs[:, 16 * CHUNK * buf:16 * CHUNK * (buf + 1)],
                             in_=vpre[:, 16 * CHUNK * c:16 * CHUNK * (c + 1)]).then_inc(sem, 16)
                sy.dma_start(out=gs[:, 16 * CHUNK * buf:16 * CHUNK * (buf + 1)],
                             in_=g[:, 16 * CHUNK * c:16 * CHUNK * (c + 1)]).then_inc(sem, 16)
                sy.wait_ge(h_s, (c + 1) * CHUNK)
                sy.dma_start(out=hout[:, 16 * CHUNK * c:16 * CHUNK * (c + 1)],
                             in_=hall[:, 16 * CHUNK * buf:16 * CHUNK * (buf + 1)]).then_inc(stc, 16)
            sy.wait_ge(stc, 16 * nch)

        @block.vector
        def _(v):
            v.memset(z16[:], 0.0)
            v.memset(hb[:], 0.0).then_inc(hb_s, 1)
            for c in range(nch):
                buf = c % 2
                sem = ldx0 if buf == 0 else ldx1
                v.wait_ge(sem, 32 * (c // 2 + 1))
                v.tensor_scalar(gms[:, 16 * CHUNK * buf:16 * CHUNK * (buf + 1)],
                                gs[:, 16 * CHUNK * buf:16 * CHUNK * (buf + 1)],
                                -1.0, 1.0, ALU.mult, ALU.add).then_inc(gm_s, 1)
                v.wait_ge(gm_s, c + 1)
                for tt in range(CHUNK):
                    t = c * CHUNK + tt
                    col = slice(16 * (CHUNK * buf + tt), 16 * (CHUNK * buf + tt + 1))
                    if tt == 0:
                        pcol = None if c == 0 else slice(16 * (CHUNK * ((c - 1) % 2) + CHUNK - 1),
                                                         16 * (CHUNK * ((c - 1) % 2) + CHUNK))
                    else:
                        pcol = slice(16 * (CHUNK * buf + tt - 1), 16 * (CHUNK * buf + tt))
                    hprev = z16[:, 0:16] if pcol is None else hall[:, pcol]
                    v.wait_ge(mm_s, t + 1)
                    v.tensor_add(zs[:], ps[:, 0:16], vs[:, col]).then_inc(z_s, 1)
                    v.tensor_mul(t2[:], gms[:, col], hprev)
                    v.wait_ge(c_s, t + 1)
                    v.tensor_mul(t1[:], gs[:, col], cs[:]).then_inc(e_s, 1)
                    v.wait_ge(e_s, t + 1)
                    v.tensor_add(hall[:, col], t1[:], t2[:]).then_inc(h_s, 1)
                    v.wait_ge(h_s, t + 1)
                    v.tensor_copy(hb[:], hall[:, col]).then_inc(hb_s, 1)

        @block.scalar
        def _(s):
            for t in range(T):
                s.wait_ge(z_s, t + 1)
                s.activation(out=cs[:], in_=zs[:], func=AF.Tanh).then_inc(c_s, 1)

        @block.tensor
        def _(te):
            te.wait_ge(ldg, 16)
            with te.Fori(0, T) as ii:
                te.wait_ge(hb_s, ii + 1)
                for i in range(8):
                    for k in range(8):
                        inst = te.matmul(ps[:, 2 * i:2 * i + 2],
                                         ws[:, (i * 8 + k) * 128:(i * 8 + k + 1) * 128],
                                         hb[:, 2 * k:2 * k + 2],
                                         start=(k == 0), stop=(k == 7))
                inst.then_inc(mm_s, 1)
    return nc


def scan_inputs(W_h, vpre_tdb, g_tdb):
    T = vpre_tdb.shape[0]

    def pk(x):
        return np.ascontiguousarray(x.reshape(T, 8, 128, 2).transpose(2, 0, 1, 3).reshape(128, T * 16))
    return {"w": pack_lhsT(np.ascontiguousarray(W_h.T)), "vpre": pk(vpre_tdb), "g": pk(g_tdb)}


def scan_h_unpack(hout, T):
    return np.ascontiguousarray(hout.reshape(128, T, 8, 2).transpose(1, 2, 0, 3).reshape(T, 1024, 2))


def build_midC(ntok, CH=512):
    nc = bass.Bass("TRN2", target_bir_lowering=False)
    NCH = ntok // CH
    NTT = CH // 128
    h0T = nc.dram_tensor("h0T", [128, 8 * ntok], F32, kind="ExternalInput").ap()
    xnT = nc.dram_tensor("xnT", [128, 8 * ntok], F32, kind="ExternalInput").ap()
    wout = nc.dram_tensor("wout", [128, 8 * 8 * 128], F32, kind="ExternalInput").ap()
    win1 = nc.dram_tensor("win1", [128, 16 * 8 * 128], F32, kind="ExternalInput").ap()
    ln_g = nc.dram_tensor("ln_g", [128, 8], F32, kind="ExternalInput").ap()
    ln_b = nc.dram_tensor("ln_b", [128, 8], F32, kind="ExternalInput").ap()
    b_out = nc.dram_tensor("b_out", [128, 8], F32, kind="ExternalInput").ap()
    b_u = nc.dram_tensor("b_u", [128, 8], F32, kind="ExternalInput").ap()
    b_v = nc.dram_tensor("b_v", [128, 8], F32, kind="ExternalInput").ap()
    b_h = nc.dram_tensor("b_h", [128, 8], F32, kind="ExternalInput").ap()
    id128 = nc.dram_tensor("id128", [128, 128], F32, kind="ExternalInput").ap()
    s0o_o = nc.dram_tensor("s0o_o", [128, 8 * ntok], F32, kind="ExternalOutput").ap()
    vpre_o = nc.dram_tensor("vpre_o", [128, 8 * ntok], F32, kind="ExternalOutput").ap()
    gg_o = nc.dram_tensor("gg_o", [128, 8 * ntok], F32, kind="ExternalOutput").ap()
    with ExitStack() as ctx:
        e = ctx.enter_context
        wo = e(nc.sbuf_tensor("wo", [128, 8 * 8 * 128], BF16))
        w1 = e(nc.sbuf_tensor("w1", [128, 16 * 8 * 128], BF16))
        idm = e(nc.sbuf_tensor("idm", [128, 128], F32))
        ones = e(nc.sbuf_tensor("ones", [128, 1], F32))
        ones1 = e(nc.sbuf_tensor("ones1", [1, 128], F32))
        epsv = e(nc.sbuf_tensor("epsv", [128, 1], F32))
        lng = e(nc.sbuf_tensor("lng", [128, 8], F32))
        lnb = e(nc.sbuf_tensor("lnb", [128, 8], F32))
        bo = e(nc.sbuf_tensor("bo", [128, 8], F32))
        bu = e(nc.sbuf_tensor("bu", [128, 8], F32))
        bv = e(nc.sbuf_tensor("bv", [128, 8], F32))
        bh = e(nc.sbuf_tensor("bh", [128, 8], F32))
        h0b = e(nc.sbuf_tensor("h0b", [128, 8 * CH], BF16))
        xnf = e(nc.sbuf_tensor("xnf", [128, 8 * CH], F32))
        s0o = e(nc.sbuf_tensor("s0o", [128, 8 * CH], F32))
        sq = e(nc.sbuf_tensor("sq", [128, 8 * CH], F32))
        s1l = e(nc.sbuf_tensor("s1l", [128, 8 * CH], F32))
        s1lb = e(nc.sbuf_tensor("s1lb", [128, 8 * CH], BF16))
        stat = e(nc.sbuf_tensor("stat", [128, 8 * NTT], F32))
        murow = e(nc.sbuf_tensor("murow", [1, 128 * NTT], F32))
        rsrow = e(nc.sbuf_tensor("rsrow", [1, 128 * NTT], F32))
        vout = e(nc.sbuf_tensor("vout", [128, 8 * CH], F32))
        gout = e(nc.sbuf_tensor("gout", [128, 8 * CH], F32))
        ps_st = e(nc.psum_tensor("ps_st", [128, 2], F32))
        ps_tr = e(nc.psum_tensor("ps_tr", [1, 128], F32))
        ps_tr2 = e(nc.psum_tensor("ps_tr2", [1, 128], F32))
        ps_bc = e(nc.psum_tensor("ps_bc", [128, 256], F32))
        ps_a = e(nc.psum_tensor("ps_a", [128, CH], F32))
        ps_b = e(nc.psum_tensor("ps_b", [128, CH], F32))
        b = B(nc, ctx)
        for sname in ("ldc", "ldg", "ldx", "mm", "dv", "ac", "st"):
            b.sem(sname)
        block = e(nc.Block())
        S = dict(ones=ones, ones1=ones1, idm=idm, sq=sq, stat=stat, murow=murow,
                 rsrow=rsrow, ps_st=ps_st, ps_tr=ps_tr, ps_tr2=ps_tr2, ps_bc=ps_bc, epsv=epsv)

        for dst, src in ((idm, id128), (lng, ln_g), (lnb, ln_b), (bo, b_out),
                         (bu, b_u), (bv, b_v), (bh, b_h)):
            b.emit("sync", lambda s, d=dst, sr=src: s.dma_start(out=d[:], in_=sr[:, :]), inc=("ldc", 16))
        b.emit("gpsimd", lambda g: g.dma_start(out=wo[:], in_=wout[:, :]), inc=("ldg", 16))
        b.emit("gpsimd", lambda g: g.dma_start(out=w1[:], in_=win1[:, :]), inc=("ldg", 16))
        ld0 = [("ldc", b.at("ldc")), ("ldg", b.at("ldg"))]
        b.emit("vector", lambda v: v.memset(ones[:], 1.0))
        b.emit("vector", lambda v: v.memset(ones1[:], 1.0))
        b.emit("vector", lambda v: v.memset(epsv[:], EPS), inc="dv")
        b.emit("vector", lambda v: v.tensor_add(bv[:], bv[:], bh[:]), wait=ld0, inc="dv")
        for eng in ("scalar", "tensor"):
            b.emit(eng, lambda x: None, wait=ld0 + ["dv"])

        for c in range(NCH):
            st0 = b.at("st")
            b.emit("gpsimd", lambda g, c=c: g.dma_start(
                out=h0b[:].rearrange("p (u n) -> p u n", u=8),
                in_=h0T.rearrange("p (u n) -> p u n", u=8)[:, :, c * CH:(c + 1) * CH]),
                wait=[("st", st0)] + ld0, inc=("ldg", 16))
            b.emit("sync", lambda s, c=c: s.dma_start(
                out=xnf[:].rearrange("p (u n) -> p u n", u=8),
                in_=xnT.rearrange("p (u n) -> p u n", u=8)[:, :, c * CH:(c + 1) * CH]),
                wait=[("st", st0)] + ld0, inc=("ldx", 16))
            ldv = [("ldx", b.at("ldx")), ("ldg", b.at("ldg")), ("st", st0)]
            for eng in ("vector", "scalar", "tensor"):
                b.emit(eng, lambda x: None, wait=ldv)

            bank_guard = {0: ("dv", 0), 1: ("dv", 0)}
            for ei in range(8):
                pb = ps_a if ei % 2 == 0 else ps_b
                b.emit("tensor", mmgroup(pb[:, :],
                       [(wo[:, (ei * 8 + k) * 128:(ei * 8 + k + 1) * 128], h0b[:, k * CH:(k + 1) * CH]) for k in range(8)]),
                       wait=[bank_guard[ei % 2]], inc="mm")
                b.emit("vector", lambda v, ei=ei, pb=pb: v.tensor_scalar(sq[:, ei * CH:(ei + 1) * CH], pb[:, :], bo[:, ei:ei + 1], None, ALU.add),
                       wait="mm", inc="dv")
                b.emit("vector", lambda v, ei=ei: v.tensor_add(s0o[:, ei * CH:(ei + 1) * CH], sq[:, ei * CH:(ei + 1) * CH], xnf[:, ei * CH:(ei + 1) * CH]),
                       wait="dv", inc="dv")
                bank_guard[ei % 2] = ("dv", b.at("dv"))
            dv_s0 = b.at("dv")
            b.emit("sync", lambda s, c=c: s.dma_start(
                out=s0o_o.rearrange("p (u n) -> p u n", u=8)[:, :, c * CH:(c + 1) * CH],
                in_=s0o[:].rearrange("p (u n) -> p u n", u=8)),
                wait=[("dv", dv_s0)], inc=("st", 16))

            emit_ln(b, S, s0o, s1l, CH, src_tm=None, affine=(lng, lnb), out_bf=s1lb)
            ac_s1lb = b.at("ac")

            bank_guard = {0: ("dv", 0), 1: ("dv", 0)}
            for ei in range(16):
                pb = ps_a if ei % 2 == 0 else ps_b
                b.emit("tensor", mmgroup(pb[:, :],
                       [(w1[:, (ei * 8 + k) * 128:(ei * 8 + k + 1) * 128], s1lb[:, k * CH:(k + 1) * CH]) for k in range(8)]),
                       wait=[("ac", ac_s1lb), bank_guard[ei % 2]], inc="mm")
                if ei < 8:
                    b.emit("scalar", lambda s, ei=ei, pb=pb: s.activation(
                        gout[:, ei * CH:(ei + 1) * CH], pb[:, :], AF.Sigmoid, bias=bu[:, ei:ei + 1]),
                        wait="mm", inc="ac")
                    bank_guard[ei % 2] = ("ac", b.at("ac"))
                else:
                    u = ei - 8
                    b.emit("vector", lambda v, u=u, pb=pb: v.tensor_scalar(
                        vout[:, u * CH:(u + 1) * CH], pb[:, :], bv[:, u:u + 1], None, ALU.add),
                        wait="mm", inc="dv")
                    bank_guard[ei % 2] = ("dv", b.at("dv"))
            b.emit("sync", lambda s, c=c: s.dma_start(
                out=vpre_o.rearrange("p (u n) -> p u n", u=8)[:, :, c * CH:(c + 1) * CH],
                in_=vout[:].rearrange("p (u n) -> p u n", u=8)),
                wait=["dv"], inc=("st", 16))
            b.emit("sync", lambda s, c=c: s.dma_start(
                out=gg_o.rearrange("p (u n) -> p u n", u=8)[:, :, c * CH:(c + 1) * CH],
                in_=gout[:].rearrange("p (u n) -> p u n", u=8)),
                wait=["ac"], inc=("st", 16))
        b.emit("sync", lambda s: None, wait=["st"])
        b.install(block)
    return nc


def build_moeE(N, CH=256, CAP=2560):
    nc = bass.Bass("TRN2", target_bir_lowering=False)
    NCH = N // CH
    xnT = nc.dram_tensor("xnT", [128, 8 * N], F32, kind="ExternalInput").ap()
    w1p = nc.dram_tensor("w1p", [128, 32 * 8 * 128], F32, kind="ExternalInput").ap()
    w2p = nc.dram_tensor("w2p", [128, 8 * 32 * 128], F32, kind="ExternalInput").ap()
    b1p = nc.dram_tensor("b1p", [128, 32], F32, kind="ExternalInput").ap()
    b2p = nc.dram_tensor("b2p", [128, 8], F32, kind="ExternalInput").ap()
    mrow = nc.dram_tensor("mrow", [1, N], F32, kind="ExternalInput").ap()
    grow = nc.dram_tensor("grow", [1, N], F32, kind="ExternalInput").ap()
    on1 = nc.dram_tensor("on1", [1, 128], F32, kind="ExternalInput").ap()
    ctr_o = nc.dram_tensor("ctr_o", [128, 8 * N], F32, kind="ExternalOutput").ap()
    with ExitStack() as ctx:
        e = ctx.enter_context
        w1s = e(nc.sbuf_tensor("w1s", [128, 32 * 8 * 128], BF16))
        w2s = e(nc.sbuf_tensor("w2s", [128, 8 * 32 * 128], BF16))
        b1s = e(nc.sbuf_tensor("b1s", [128, 32], F32))
        b2s = e(nc.sbuf_tensor("b2s", [128, 8], F32))
        SEG = 2048
        mr = e(nc.sbuf_tensor("mr", [1, SEG], F32))
        gr = e(nc.sbuf_tensor("gr", [1, SEG], F32))
        cA = e(nc.sbuf_tensor("cA", [1, SEG], F32))
        cB = e(nc.sbuf_tensor("cB", [1, SEG], F32))
        carry = e(nc.sbuf_tensor("carry", [1, 1], F32))
        wseg = e(nc.sbuf_tensor("wseg", [1, CH], F32))
        wsc = nc.dram_tensor("wsc", [1, N], F32).ap()
        ones1 = e(nc.sbuf_tensor("ones1", [1, 128], F32))
        xnb = e(nc.sbuf_tensor("xnb", [128, 8 * CH], BF16))
        hg = e(nc.sbuf_tensor("hg", [128, 32 * CH], BF16))
        tmp = e(nc.sbuf_tensor("tmp", [128, CH], F32))
        outc = e(nc.sbuf_tensor("outc", [128, 8 * CH], F32))
        ps_w = e(nc.psum_tensor("ps_w", [128, CH], F32))
        ps_h0 = e(nc.psum_tensor("ps_h0", [128, CH], F32))
        ps_h1 = e(nc.psum_tensor("ps_h1", [128, CH], F32))
        ps_o0 = e(nc.psum_tensor("ps_o0", [128, CH], F32))
        ps_o1 = e(nc.psum_tensor("ps_o1", [128, CH], F32))
        b = B(nc, ctx)
        for sname in ("ldc", "ldg", "ldx", "mm", "dv", "ac", "st"):
            b.sem(sname)
        block = e(nc.Block())

        b.emit("gpsimd", lambda g: g.dma_start(out=w1s[:], in_=w1p[:, :]), inc=("ldg", 16))
        b.emit("gpsimd", lambda g: g.dma_start(out=w2s[:], in_=w2p[:, :]), inc=("ldg", 16))
        for dst, src in ((b1s, b1p), (b2s, b2p), (ones1, on1)):
            b.emit("sync", lambda s, d=dst, sr=src: s.dma_start(out=d[:], in_=sr[:, :]), inc=("ldc", 16))
        ld0 = [("ldc", b.at("ldc")), ("ldg", b.at("ldg"))]
        b.emit("vector", lambda v: v.memset(carry[:], 0.0), inc="dv")
        NSEG = N // SEG
        for sg in range(NSEG):
            dvv = b.at("dv")
            b.emit("sync", lambda s, sg=sg: s.dma_start(out=mr[:], in_=mrow[0:1, sg * SEG:(sg + 1) * SEG]),
                   wait=[("dv", dvv), ("st", b.at("st"))], inc=("ldx", 16))
            b.emit("sync", lambda s, sg=sg: s.dma_start(out=gr[:], in_=grow[0:1, sg * SEG:(sg + 1) * SEG]),
                   inc=("ldx", 16))
            ldw = b.at("ldx")
            b.emit("vector", lambda v: v.tensor_copy(cA[:], mr[:]), wait=[("ldx", ldw)], inc="dv")
            cur, oth = cA, cB
            sh = 1
            while sh < SEG:
                b.emit("vector", lambda v, sh=sh, cur=cur, oth=oth: v.tensor_add(
                    oth[0:1, sh:SEG], cur[0:1, sh:SEG], cur[0:1, 0:SEG - sh]), wait="dv", inc="dv")
                b.emit("vector", lambda v, sh=sh, cur=cur, oth=oth: v.tensor_copy(
                    oth[0:1, 0:sh], cur[0:1, 0:sh]), wait="dv", inc="dv")
                cur, oth = oth, cur
                sh *= 2
            b.emit("vector", lambda v, cur=cur: v.tensor_scalar(cur[:], cur[:], carry[0:1, 0:1], None, ALU.add),
                   wait="dv", inc="dv")
            b.emit("vector", lambda v, cur=cur: v.tensor_copy(carry[:], cur[0:1, SEG - 1:SEG]), wait="dv", inc="dv")
            b.emit("vector", lambda v, cur=cur, oth=oth: v.tensor_scalar(oth[:], cur[:], float(CAP), None, ALU.is_le),
                   wait="dv", inc="dv")
            b.emit("vector", lambda v, oth=oth: v.tensor_tensor(oth[:], oth[:], mr[:], ALU.mult), wait="dv", inc="dv")
            b.emit("vector", lambda v, oth=oth: v.tensor_tensor(oth[:], oth[:], gr[:], ALU.mult), wait="dv", inc="dv")
            b.emit("sync", lambda s, sg=sg, oth=oth: s.dma_start(out=wsc[0:1, sg * SEG:(sg + 1) * SEG], in_=oth[:]),
                   wait=["dv"], inc=("st", 16))
        dv_w = b.at("dv")
        st_w = b.at("st")
        for eng in ("scalar", "tensor"):
            b.emit(eng, lambda x: None, wait=ld0 + [("dv", dv_w)])

        for c in range(NCH):
            st0 = b.at("st")
            b.emit("gpsimd", lambda g, c=c: g.dma_start(
                out=xnb[:].rearrange("p (u n) -> p u n", u=8),
                in_=xnT.rearrange("p (u n) -> p u n", u=8)[:, :, c * CH:(c + 1) * CH]),
                wait=[("st", st0)] + ld0, inc=("ldg", 16))
            ldv = [("ldg", b.at("ldg")), ("st", st0)]
            for eng in ("vector", "scalar", "tensor"):
                b.emit(eng, lambda x: None, wait=ldv)
            b.emit("sync", lambda s, c=c: s.dma_start(out=wseg[:], in_=wsc[0:1, c * CH:(c + 1) * CH]),
                   wait=[("st", st_w), ("dv", b.at("dv")), ("mm", b.at("mm"))], inc=("ldx", 16))
            b.emit("tensor", lambda t, c=c: t.matmul(ps_w[:, :], ones1[0:1, :], wseg[0:1, :], start=True, stop=True),
                   wait=["dv", ("ldx", b.at("ldx"))], inc="mm")
            guard = {0: ("ac", 0), 1: ("ac", 0)}
            for hi in range(32):
                pb = ps_h0 if hi % 2 == 0 else ps_h1
                b.emit("tensor", mmgroup(pb[:, :],
                       [(w1s[:, (hi * 8 + k) * 128:(hi * 8 + k + 1) * 128], xnb[:, k * CH:(k + 1) * CH]) for k in range(8)]),
                       wait=[guard[hi % 2]], inc="mm")
                b.emit("scalar", lambda s, hi=hi, pb=pb: s.activation(
                    hg[:, hi * CH:(hi + 1) * CH], pb[:, :], AF.Gelu, bias=b1s[:, hi:hi + 1]),
                    wait="mm", inc="ac")
                guard[hi % 2] = ("ac", b.at("ac"))
            ac_hg = b.at("ac")
            guard = {0: ("dv", 0), 1: ("dv", 0)}
            for di in range(8):
                pb = ps_o0 if di % 2 == 0 else ps_o1
                b.emit("tensor", mmgroup(pb[:, :],
                       [(w2s[:, (di * 32 + k) * 128:(di * 32 + k + 1) * 128], hg[:, k * CH:(k + 1) * CH]) for k in range(32)]),
                       wait=[("ac", ac_hg), guard[di % 2]], inc="mm")
                b.emit("vector", lambda v, di=di, pb=pb: v.tensor_scalar(tmp[:, :], pb[:, :], b2s[:, di:di + 1], None, ALU.add),
                       wait=["mm", "dv"], inc="dv")
                b.emit("vector", lambda v, di=di: v.tensor_tensor(outc[:, di * CH:(di + 1) * CH], tmp[:, :], ps_w[:, :], ALU.mult),
                       wait="dv", inc="dv")
                guard[di % 2] = ("dv", b.at("dv"))
            b.emit("sync", lambda s, c=c: s.dma_start(
                out=ctr_o.rearrange("p (u n) -> p u n", u=8)[:, :, c * CH:(c + 1) * CH],
                in_=outc[:].rearrange("p (u n) -> p u n", u=8)),
                wait=["dv"], inc=("st", 16))
        b.emit("sync", lambda s: None, wait=["st"])
        b.install(block)
    return nc


def build_combF(ntok, CH=512):
    nc = bass.Bass("TRN2", target_bir_lowering=False)
    NCH = ntok // CH
    NTT = CH // 128
    xT = nc.dram_tensor("xT", [128, 8 * ntok], F32, kind="ExternalInput").ap()
    s0T = nc.dram_tensor("s0T", [128, 8 * ntok], F32, kind="ExternalInput").ap()
    h1T = nc.dram_tensor("h1T", [128, 8 * ntok], F32, kind="ExternalInput").ap()
    ctrT = nc.dram_tensor("ctrT", [128, 8 * 8 * ntok], F32, kind="ExternalInput").ap()
    wout = nc.dram_tensor("wout", [128, 8 * 8 * 128], F32, kind="ExternalInput").ap()
    b_out = nc.dram_tensor("b_out", [128, 8], F32, kind="ExternalInput").ap()
    id128 = nc.dram_tensor("id128", [128, 128], F32, kind="ExternalInput").ap()
    x1_o = nc.dram_tensor("x1_o", [128, 8 * ntok], F32, kind="ExternalOutput").ap()
    xn1_o = nc.dram_tensor("xn1_o", [128, 8 * ntok], F32, kind="ExternalOutput").ap()
    with ExitStack() as ctx:
        e = ctx.enter_context
        wo = e(nc.sbuf_tensor("wo", [128, 8 * 8 * 128], BF16))
        idm = e(nc.sbuf_tensor("idm", [128, 128], F32))
        ones = e(nc.sbuf_tensor("ones", [128, 1], F32))
        ones1 = e(nc.sbuf_tensor("ones1", [1, 128], F32))
        epsv = e(nc.sbuf_tensor("epsv", [128, 1], F32))
        bo = e(nc.sbuf_tensor("bo", [128, 8], F32))
        xf = e(nc.sbuf_tensor("xf", [128, 8 * CH], F32))
        s0f = e(nc.sbuf_tensor("s0f", [128, 8 * CH], F32))
        h1b = e(nc.sbuf_tensor("h1b", [128, 8 * CH], BF16))
        cbuf = e(nc.sbuf_tensor("cbuf", [128, 8 * CH], F32))
        acc = e(nc.sbuf_tensor("acc", [128, 8 * CH], F32))
        sq = e(nc.sbuf_tensor("sq", [128, 8 * CH], F32))
        xn1 = e(nc.sbuf_tensor("xn1", [128, 8 * CH], F32))
        stat = e(nc.sbuf_tensor("stat", [128, 8 * NTT], F32))
        murow = e(nc.sbuf_tensor("murow", [1, 128 * NTT], F32))
        rsrow = e(nc.sbuf_tensor("rsrow", [1, 128 * NTT], F32))
        ps_st = e(nc.psum_tensor("ps_st", [128, 2], F32))
        ps_tr = e(nc.psum_tensor("ps_tr", [1, 128], F32))
        ps_tr2 = e(nc.psum_tensor("ps_tr2", [1, 128], F32))
        ps_bc = e(nc.psum_tensor("ps_bc", [128, 256], F32))
        ps_a = e(nc.psum_tensor("ps_a", [128, CH], F32))
        ps_b = e(nc.psum_tensor("ps_b", [128, CH], F32))
        b = B(nc, ctx)
        for sname in ("ldc", "ldg", "ldx", "mm", "dv", "ac", "st"):
            b.sem(sname)
        block = e(nc.Block())
        S = dict(ones=ones, ones1=ones1, idm=idm, sq=sq, stat=stat, murow=murow,
                 rsrow=rsrow, ps_st=ps_st, ps_tr=ps_tr, ps_tr2=ps_tr2, ps_bc=ps_bc, epsv=epsv)

        for dst, src in ((idm, id128), (bo, b_out)):
            b.emit("sync", lambda s, d=dst, sr=src: s.dma_start(out=d[:], in_=sr[:, :]), inc=("ldc", 16))
        b.emit("gpsimd", lambda g: g.dma_start(out=wo[:], in_=wout[:, :]), inc=("ldg", 16))
        ld0 = [("ldc", b.at("ldc")), ("ldg", b.at("ldg"))]
        b.emit("vector", lambda v: v.memset(ones[:], 1.0))
        b.emit("vector", lambda v: v.memset(ones1[:], 1.0))
        b.emit("vector", lambda v: v.memset(epsv[:], EPS), inc="dv")
        for eng in ("scalar", "tensor", "vector"):
            b.emit(eng, lambda x: None, wait=ld0 + ["dv"])

        for c in range(NCH):
            st0 = b.at("st")
            b.emit("sync", lambda s, c=c: s.dma_start(
                out=xf[:].rearrange("p (u n) -> p u n", u=8),
                in_=xT.rearrange("p (u n) -> p u n", u=8)[:, :, c * CH:(c + 1) * CH]),
                wait=[("st", st0)] + ld0, inc=("ldx", 16))
            b.emit("sync", lambda s, c=c: s.dma_start(
                out=s0f[:].rearrange("p (u n) -> p u n", u=8),
                in_=s0T.rearrange("p (u n) -> p u n", u=8)[:, :, c * CH:(c + 1) * CH]),
                inc=("ldx", 16))
            b.emit("gpsimd", lambda g, c=c: g.dma_start(
                out=h1b[:].rearrange("p (u n) -> p u n", u=8),
                in_=h1T.rearrange("p (u n) -> p u n", u=8)[:, :, c * CH:(c + 1) * CH]),
                wait=[("st", st0)] + ld0, inc=("ldg", 16))
            ldv = [("ldx", b.at("ldx")), ("ldg", b.at("ldg")), ("st", st0)]
            for eng in ("vector", "scalar", "tensor"):
                b.emit(eng, lambda x: None, wait=ldv)
            for u in range(8):
                b.emit("vector", lambda v, u=u: v.tensor_add(acc[:, u * CH:(u + 1) * CH], xf[:, u * CH:(u + 1) * CH], s0f[:, u * CH:(u + 1) * CH]),
                       wait=("dv" if u == 0 else ()), inc="dv")
            for ee in range(8):
                dvv = b.at("dv")
                b.emit("sync", lambda s, c=c, ee=ee: s.dma_start(
                    out=cbuf[:].rearrange("p (u n) -> p u n", u=8),
                    in_=ctrT.rearrange("p (e u n) -> p e u n", e=8, u=8)[:, ee, :, c * CH:(c + 1) * CH]),
                    wait=[("dv", dvv)], inc=("ldx", 16))
                b.emit("vector", lambda x2: None, wait=[("ldx", b.at("ldx"))])
                for u in range(8):
                    b.emit("vector", lambda v, u=u: v.tensor_add(acc[:, u * CH:(u + 1) * CH], acc[:, u * CH:(u + 1) * CH], cbuf[:, u * CH:(u + 1) * CH]),
                           inc="dv")
            guard = {0: ("dv", 0), 1: ("dv", 0)}
            for ei in range(8):
                pb = ps_a if ei % 2 == 0 else ps_b
                b.emit("tensor", mmgroup(pb[:, :],
                       [(wo[:, (ei * 8 + k) * 128:(ei * 8 + k + 1) * 128], h1b[:, k * CH:(k + 1) * CH]) for k in range(8)]),
                       wait=[guard[ei % 2]], inc="mm")
                b.emit("vector", lambda v, ei=ei, pb=pb: v.tensor_scalar(sq[:, ei * CH:(ei + 1) * CH], pb[:, :], bo[:, ei:ei + 1], None, ALU.add),
                       wait=["mm", "dv"], inc="dv")
                b.emit("vector", lambda v, ei=ei: v.tensor_add(acc[:, ei * CH:(ei + 1) * CH], acc[:, ei * CH:(ei + 1) * CH], sq[:, ei * CH:(ei + 1) * CH]),
                       wait="dv", inc="dv")
                guard[ei % 2] = ("dv", b.at("dv"))
            dv_x1 = b.at("dv")
            b.emit("sync", lambda s, c=c: s.dma_start(
                out=x1_o.rearrange("p (u n) -> p u n", u=8)[:, :, c * CH:(c + 1) * CH],
                in_=acc[:].rearrange("p (u n) -> p u n", u=8)),
                wait=[("dv", dv_x1)], inc=("st", 16))
            emit_ln(b, S, acc, xn1, CH, src_tm=None, affine=None)
            b.emit("sync", lambda s, c=c: s.dma_start(
                out=xn1_o.rearrange("p (u n) -> p u n", u=8)[:, :, c * CH:(c + 1) * CH],
                in_=xn1[:].rearrange("p (u n) -> p u n", u=8)),
                wait=["dv"], inc=("st", 16))
        b.emit("sync", lambda s: None, wait=["st"])
        b.install(block)
    return nc


def build_qkvG1(ntok, CH=512):
    nc = bass.Bass("TRN2", target_bir_lowering=False)
    NCH = ntok // CH
    xn1T = nc.dram_tensor("xn1T", [128, 8 * ntok], F32, kind="ExternalInput").ap()
    wdq = nc.dram_tensor("wdq", [128, 32 * 8 * 128], F32, kind="ExternalInput").ap()
    wuq = nc.dram_tensor("wuq", [128, 8 * 32 * 128], F32, kind="ExternalInput").ap()
    wqr = nc.dram_tensor("wqr", [128, 4 * 32 * 128], F32, kind="ExternalInput").ap()
    wdkv = nc.dram_tensor("wdkv", [128, 2 * 8 * 128], F32, kind="ExternalInput").ap()
    wuk = nc.dram_tensor("wuk", [128, 8 * 2 * 128], F32, kind="ExternalInput").ap()
    wuv = nc.dram_tensor("wuv", [128, 8 * 2 * 128], F32, kind="ExternalInput").ap()
    wkr = nc.dram_tensor("wkr", [128, 1 * 8 * 32], F32, kind="ExternalInput").ap()
    scq = nc.dram_tensor("scq", [128, 4 * ntok], F32, kind="ExternalInput").ap()
    sck = nc.dram_tensor("sck", [32, ntok], F32, kind="ExternalInput").ap()
    qc_o = nc.dram_tensor("qc_o", [128, 8 * ntok], F32, kind="ExternalOutput").ap()
    qr_o = nc.dram_tensor("qr_o", [128, 4 * ntok], F32, kind="ExternalOutput").ap()
    kc_o = nc.dram_tensor("kc_o", [128, 8 * ntok], F32, kind="ExternalOutput").ap()
    kr_o = nc.dram_tensor("kr_o", [32, ntok], F32, kind="ExternalOutput").ap()
    v_o = nc.dram_tensor("v_o", [128, 8 * ntok], F32, kind="ExternalOutput").ap()
    with ExitStack() as ctx:
        e = ctx.enter_context
        wdqs = e(nc.sbuf_tensor("wdqs", [128, 32 * 8 * 128], BF16))
        wuqs = e(nc.sbuf_tensor("wuqs", [128, 8 * 32 * 128], BF16))
        wqrs = e(nc.sbuf_tensor("wqrs", [128, 4 * 32 * 128], BF16))
        wdkvs = e(nc.sbuf_tensor("wdkvs", [128, 2 * 8 * 128], BF16))
        wuks = e(nc.sbuf_tensor("wuks", [128, 8 * 2 * 128], BF16))
        wuvs = e(nc.sbuf_tensor("wuvs", [128, 8 * 2 * 128], BF16))
        wkrs = e(nc.sbuf_tensor("wkrs", [128, 8 * 32], BF16))
        scqs = e(nc.sbuf_tensor("scqs", [128, 4 * CH], F32))
        scks = e(nc.sbuf_tensor("scks", [32, CH], F32))
        xnb = e(nc.sbuf_tensor("xnb", [128, 8 * CH], BF16))
        cqb = e(nc.sbuf_tensor("cqb", [128, 32 * CH], BF16))
        ckvb = e(nc.sbuf_tensor("ckvb", [128, 2 * CH], BF16))
        ot = e(nc.sbuf_tensor("ot", [128, 8 * CH], F32))
        ps_a = e(nc.psum_tensor("ps_a", [128, CH], F32))
        ps_b = e(nc.psum_tensor("ps_b", [128, CH], F32))
        b = B(nc, ctx)
        for sname in ("ldc", "ldg", "ldx", "mm", "dv", "ac", "st"):
            b.sem(sname)
        block = e(nc.Block())
        for dst, src in ((wdqs, wdq), (wuqs, wuq), (wqrs, wqr), (wdkvs, wdkv),
                         (wuks, wuk), (wuvs, wuv), (wkrs, wkr)):
            b.emit("gpsimd", lambda g, d=dst, sr=src: g.dma_start(out=d[:], in_=sr[:, :]), inc=("ldg", 16))
        ld0 = [("ldg", b.at("ldg"))]
        for eng in ("scalar", "tensor", "vector"):
            b.emit(eng, lambda x: None, wait=ld0)

        for c in range(NCH):
            st0 = b.at("st")
            b.emit("gpsimd", lambda g, c=c: g.dma_start(
                out=xnb[:].rearrange("p (u n) -> p u n", u=8),
                in_=xn1T.rearrange("p (u n) -> p u n", u=8)[:, :, c * CH:(c + 1) * CH]),
                wait=[("st", st0)] + ld0, inc=("ldg", 16))
            b.emit("sync", lambda s, c=c: s.dma_start(
                out=scqs[:].rearrange("p (u n) -> p u n", u=4),
                in_=scq.rearrange("p (u n) -> p u n", u=4)[:, :, c * CH:(c + 1) * CH]),
                wait=[("st", st0)], inc=("ldx", 16))
            b.emit("sync", lambda s, c=c: s.dma_start(
                out=scks[:], in_=sck[:, c * CH:(c + 1) * CH]), inc=("ldx", 16))
            ldv = [("ldx", b.at("ldx")), ("ldg", b.at("ldg")), ("st", st0)]
            for eng in ("vector", "scalar", "tensor"):
                b.emit(eng, lambda x: None, wait=ldv)

            guard = {0: ("ac", 0), 1: ("ac", 0)}
            for ei in range(32):
                pb = ps_a if ei % 2 == 0 else ps_b
                b.emit("tensor", mmgroup(pb[:, :],
                       [(wdqs[:, (ei * 8 + k) * 128:(ei * 8 + k + 1) * 128], xnb[:, k * CH:(k + 1) * CH]) for k in range(8)]),
                       wait=[guard[ei % 2]], inc="mm")
                b.emit("scalar", lambda s, ei=ei, pb=pb: s.copy(cqb[:, ei * CH:(ei + 1) * CH], pb[:, :]),
                       wait="mm", inc="ac")
                guard[ei % 2] = ("ac", b.at("ac"))
            ac_cq = b.at("ac")
            guard = {0: ("dv", 0), 1: ("dv", 0)}
            for ei in range(8):
                pb = ps_a if ei % 2 == 0 else ps_b
                b.emit("tensor", mmgroup(pb[:, :],
                       [(wuqs[:, (ei * 32 + k) * 128:(ei * 32 + k + 1) * 128], cqb[:, k * CH:(k + 1) * CH]) for k in range(32)]),
                       wait=[("ac", ac_cq), guard[ei % 2]], inc="mm")
                b.emit("vector", lambda v, ei=ei, pb=pb: v.tensor_copy(ot[:, ei * CH:(ei + 1) * CH], pb[:, :]),
                       wait=["mm", "dv"], inc="dv")
                guard[ei % 2] = ("dv", b.at("dv"))
            b.emit("sync", lambda s, c=c: s.dma_start(
                out=qc_o.rearrange("p (u n) -> p u n", u=8)[:, :, c * CH:(c + 1) * CH],
                in_=ot[:].rearrange("p (u n) -> p u n", u=8)), wait=["dv"], inc=("st", 16))
            st_ot = ("st", b.at("st"))
            guard = {0: ("dv", b.at("dv")), 1: ("dv", b.at("dv"))}
            for ei in range(4):
                pb = ps_a if ei % 2 == 0 else ps_b
                b.emit("tensor", mmgroup(pb[:, :],
                       [(wqrs[:, (ei * 32 + k) * 128:(ei * 32 + k + 1) * 128], cqb[:, k * CH:(k + 1) * CH]) for k in range(32)]),
                       wait=[("ac", ac_cq), guard[ei % 2]], inc="mm")
                b.emit("vector", lambda v, ei=ei, pb=pb: v.tensor_tensor(ot[:, ei * CH:(ei + 1) * CH], pb[:, :], scqs[:, ei * CH:(ei + 1) * CH], ALU.mult),
                       wait=["mm", "dv", st_ot], inc="dv")
                guard[ei % 2] = ("dv", b.at("dv"))
            b.emit("sync", lambda s, c=c: s.dma_start(
                out=qr_o.rearrange("p (u n) -> p u n", u=4)[:, :, c * CH:(c + 1) * CH],
                in_=ot[:, 0:4 * CH].rearrange("p (u n) -> p u n", u=4)), wait=["dv"], inc=("st", 16))
            guard = {0: ("dv", b.at("dv")), 1: ("dv", b.at("dv"))}
            for ei in range(2):
                pb = ps_a if ei % 2 == 0 else ps_b
                b.emit("tensor", mmgroup(pb[:, :],
                       [(wdkvs[:, (ei * 8 + k) * 128:(ei * 8 + k + 1) * 128], xnb[:, k * CH:(k + 1) * CH]) for k in range(8)]),
                       wait=[guard[ei % 2]], inc="mm")
                b.emit("scalar", lambda s, ei=ei, pb=pb: s.copy(ckvb[:, ei * CH:(ei + 1) * CH], pb[:, :]),
                       wait="mm", inc="ac")
                guard[ei % 2] = ("ac", b.at("ac"))
            ac_ckv = b.at("ac")
            for wsrc, dst_o in ((wuks, kc_o), (wuvs, v_o)):
                st_ot = None
                guard = {0: ("dv", b.at("dv")), 1: ("dv", b.at("dv"))}
                for ei in range(8):
                    pb = ps_a if ei % 2 == 0 else ps_b
                    b.emit("tensor", mmgroup(pb[:, :],
                           [(wsrc[:, (ei * 2 + k) * 128:(ei * 2 + k + 1) * 128], ckvb[:, k * CH:(k + 1) * CH]) for k in range(2)]),
                           wait=[("ac", ac_ckv), guard[ei % 2]], inc="mm")
                    if st_ot is None:
                        st_ot = ("st", b.at("st"))
                    b.emit("vector", lambda v, ei=ei, pb=pb: v.tensor_copy(ot[:, ei * CH:(ei + 1) * CH], pb[:, :]),
                           wait=["mm", "dv", st_ot], inc="dv")
                    guard[ei % 2] = ("dv", b.at("dv"))
                b.emit("sync", lambda s, c=c, dst_o=dst_o: s.dma_start(
                    out=dst_o.rearrange("p (u n) -> p u n", u=8)[:, :, c * CH:(c + 1) * CH],
                    in_=ot[:].rearrange("p (u n) -> p u n", u=8)), wait=["dv"], inc=("st", 16))
            b.emit("tensor", mmgroup(ps_a[0:32, :],
                   [(wkrs[:, k * 32:(k + 1) * 32], xnb[:, k * CH:(k + 1) * CH]) for k in range(8)]),
                   wait=["dv"], inc="mm")
            b.emit("vector", lambda v: v.tensor_tensor(ot[0:32, 0:CH], ps_a[0:32, :], scks[0:32, :], ALU.mult),
                   wait=["mm", "dv", ("st", b.at("st"))], inc="dv")
            b.emit("sync", lambda s, c=c: s.dma_start(
                out=kr_o[:, c * CH:(c + 1) * CH], in_=ot[0:32, 0:CH]), wait=["dv"], inc=("st", 16))
        b.emit("sync", lambda s: None, wait=["st"])
        b.install(block)
    return nc


def build_ffnG2(ntok, CH=256):
    nc = bass.Bass("TRN2", target_bir_lowering=False)
    NCH = ntok // CH
    xn1T = nc.dram_tensor("xn1T", [128, 8 * ntok], F32, kind="ExternalInput").ap()
    x1T = nc.dram_tensor("x1T", [128, 8 * ntok], F32, kind="ExternalInput").ap()
    w1p = nc.dram_tensor("w1p", [128, 32 * 8 * 128], F32, kind="ExternalInput").ap()
    w2p = nc.dram_tensor("w2p", [128, 8 * 32 * 128], F32, kind="ExternalInput").ap()
    b1p = nc.dram_tensor("b1p", [128, 32], F32, kind="ExternalInput").ap()
    b2p = nc.dram_tensor("b2p", [128, 8], F32, kind="ExternalInput").ap()
    op_o = nc.dram_tensor("op_o", [128, 8 * ntok], F32, kind="ExternalOutput").ap()
    with ExitStack() as ctx:
        e = ctx.enter_context
        w1f = e(nc.sbuf_tensor("w1f", [128, 32 * 8 * 128], BF16))
        w2f = e(nc.sbuf_tensor("w2f", [128, 8 * 32 * 128], BF16))
        wbuf = e(nc.sbuf_tensor("wbuf", [128, 32 * 128], F32))
        abuf = e(nc.sbuf_tensor("abuf", [128, 32 * 128], BF16))
        b1s = e(nc.sbuf_tensor("b1s", [128, 32], F32))
        b2s = e(nc.sbuf_tensor("b2s", [128, 8], F32))
        al1 = e(nc.sbuf_tensor("al1", [128, 32], F32))
        al2 = e(nc.sbuf_tensor("al2", [128, 8], F32))
        ones = e(nc.sbuf_tensor("ones", [128, 1], BF16))
        xnb = e(nc.sbuf_tensor("xnb", [128, 8 * CH], BF16))
        x1f = e(nc.sbuf_tensor("x1f", [128, 8 * CH], F32))
        hgb = e(nc.sbuf_tensor("hgb", [128, 32 * CH], BF16))
        tmp = e(nc.sbuf_tensor("tmp", [128, CH], F32))
        ot = e(nc.sbuf_tensor("ot", [128, 8 * CH], F32))
        ps_a = e(nc.psum_tensor("ps_a", [128, CH], F32))
        ps_b = e(nc.psum_tensor("ps_b", [128, CH], F32))
        ps_al = e(nc.psum_tensor("ps_al", [128, 1], F32))
        b = B(nc, ctx)
        for sname in ("ldc", "ldg", "ldx", "mm", "dv", "ac", "st"):
            b.sem(sname)
        block = e(nc.Block())

        b.emit("sync", lambda s: s.dma_start(out=b1s[:], in_=b1p[:, :]), inc=("ldc", 16))
        b.emit("sync", lambda s: s.dma_start(out=b2s[:], in_=b2p[:, :]), inc=("ldc", 16))
        b.emit("vector", lambda v: v.memset(ones[:], 1.0), inc="dv")
        def prep(wp, wf, nI, nK, alo, denom):
            for i in range(nI):
                acv, mmv = b.at("ac"), b.at("mm")
                b.emit("sync", lambda s, i=i, wp=wp, nK=nK: s.dma_start(
                    out=wbuf[:, 0:nK * 128], in_=wp[:, i * nK * 128:(i + 1) * nK * 128]),
                    wait=[("ac", acv), ("mm", mmv)], inc=("ldx", 16))
                ldw = b.at("ldx")
                b.emit("scalar", lambda s, i=i, wf=wf, nK=nK: s.activation(
                    wf[:, i * nK * 128:(i + 1) * nK * 128], wbuf[:, 0:nK * 128], AF.Sign),
                    wait=[("ldx", ldw)], inc="ac")
                b.emit("scalar", lambda s, nK=nK: s.activation(abuf[:, 0:nK * 128], wbuf[:, 0:nK * 128], AF.Abs),
                    wait="ac", inc="ac")
                b.emit("tensor", mmgroup(ps_al[:, :],
                       [(abuf[:, k * 128:(k + 1) * 128], ones[:, :]) for k in range(nK)]),
                       wait=["ac", "dv"], inc="mm")
                b.emit("vector", lambda v, i=i, alo=alo, denom=denom: v.tensor_scalar(
                    alo[:, i:i + 1], ps_al[:, :], 1.0 / denom, None, ALU.mult),
                    wait="mm", inc="dv")
        prep(w1p, w1f, 32, 8, al1, 1024.0)
        prep(w2p, w2f, 8, 32, al2, 4096.0)
        ld0 = [("ldc", b.at("ldc")), ("dv", b.at("dv")), ("ac", b.at("ac"))]
        for eng in ("scalar", "tensor", "vector"):
            b.emit(eng, lambda x: None, wait=ld0)

        for c in range(NCH):
            st0 = b.at("st")
            b.emit("gpsimd", lambda g, c=c: g.dma_start(
                out=xnb[:].rearrange("p (u n) -> p u n", u=8),
                in_=xn1T.rearrange("p (u n) -> p u n", u=8)[:, :, c * CH:(c + 1) * CH]),
                wait=[("st", st0)] + ld0, inc=("ldg", 16))
            b.emit("sync", lambda s, c=c: s.dma_start(
                out=x1f[:].rearrange("p (u n) -> p u n", u=8),
                in_=x1T.rearrange("p (u n) -> p u n", u=8)[:, :, c * CH:(c + 1) * CH]),
                wait=[("st", st0)] + ([("ldx", b.at("ldx")), ("ldc", 32)] if c == 0 else []), inc=("ldc", 16))
            ldv = [("ldc", b.at("ldc")), ("ldg", b.at("ldg")), ("st", st0)]
            for eng in ("vector", "scalar", "tensor"):
                b.emit(eng, lambda x: None, wait=ldv)
            guard = {0: ("ac", 0), 1: ("ac", 0)}
            for hi in range(32):
                pb = ps_a if hi % 2 == 0 else ps_b
                b.emit("tensor", mmgroup(pb[:, :],
                       [(w1f[:, (hi * 8 + k) * 128:(hi * 8 + k + 1) * 128], xnb[:, k * CH:(k + 1) * CH]) for k in range(8)]),
                       wait=[guard[hi % 2]], inc="mm")
                b.emit("vector", lambda v, hi=hi, pb=pb: v.tensor_scalar(
                    tmp[:, :], pb[:, :], al1[:, hi:hi + 1], b1s[:, hi:hi + 1], ALU.mult, ALU.add),
                    wait=["mm", "dv", "ac"], inc="dv")
                b.emit("scalar", lambda s, hi=hi: s.activation(hgb[:, hi * CH:(hi + 1) * CH], tmp[:, :], AF.Gelu),
                       wait="dv", inc="ac")
                guard[hi % 2] = ("ac", b.at("ac"))
            ac_hg = b.at("ac")
            guard = {0: ("dv", 0), 1: ("dv", 0)}
            for di in range(8):
                pb = ps_a if di % 2 == 0 else ps_b
                b.emit("tensor", mmgroup(pb[:, :],
                       [(w2f[:, (di * 32 + k) * 128:(di * 32 + k + 1) * 128], hgb[:, k * CH:(k + 1) * CH]) for k in range(32)]),
                       wait=[("ac", ac_hg), guard[di % 2]], inc="mm")
                b.emit("vector", lambda v, di=di, pb=pb: v.tensor_scalar(
                    ot[:, di * CH:(di + 1) * CH], pb[:, :], al2[:, di:di + 1], b2s[:, di:di + 1], ALU.mult, ALU.add),
                    wait=["mm", "dv"], inc="dv")
                b.emit("vector", lambda v, di=di: v.tensor_add(
                    ot[:, di * CH:(di + 1) * CH], ot[:, di * CH:(di + 1) * CH], x1f[:, di * CH:(di + 1) * CH]),
                    wait="dv", inc="dv")
                guard[di % 2] = ("dv", b.at("dv"))
            b.emit("sync", lambda s, c=c: s.dma_start(
                out=op_o.rearrange("p (u n) -> p u n", u=8)[:, :, c * CH:(c + 1) * CH],
                in_=ot[:].rearrange("p (u n) -> p u n", u=8)), wait=["dv"], inc=("st", 16))
        b.emit("sync", lambda s: None, wait=["st"])
        b.install(block)
    return nc


def build_attnH(T=8192, NH=4):
    nc = bass.Bass("TRN2", target_bir_lowering=False)
    NQ = T // 128
    NKM = T // 512
    qd = [nc.dram_tensor(f"q{h}", [96, T], F32, kind="ExternalInput").ap() for h in range(NH)]
    kd = [nc.dram_tensor(f"k{h}", [96, T], F32, kind="ExternalInput").ap() for h in range(NH)]
    vd = [nc.dram_tensor(f"v{h}", [T, 64], F32, kind="ExternalInput").ap() for h in range(NH)]
    msk = nc.dram_tensor("msk", [128, 4 * 512], F32, kind="ExternalInput").ap()
    idd = nc.dram_tensor("idd", [128, 128], F32, kind="ExternalInput").ap()
    yo = [nc.dram_tensor(f"y{h}", [T, 64], F32, kind="ExternalOutput").ap() for h in range(NH)]
    SCALE = 1.0 / np.sqrt(96.0)
    with ExitStack() as ctx:
        e = ctx.enter_context
        qs = e(nc.sbuf_tensor("qs", [96, T], BF16))
        ks = e(nc.sbuf_tensor("ks", [96, T], BF16))
        vs = e(nc.sbuf_tensor("vs", [128, (T // 128) * 64], BF16))
        mks = e(nc.sbuf_tensor("mks", [128, 4 * 512], F32))
        idb = e(nc.sbuf_tensor("idb", [128, 128], BF16))
        sm = e(nc.sbuf_tensor("sm", [128, T], F32))
        pb_ = e(nc.sbuf_tensor("pb_", [128, T], BF16))
        pt = e(nc.sbuf_tensor("pt", [128, 512], BF16))
        osb = e(nc.sbuf_tensor("osb", [128, 64], F32))
        red = e(nc.sbuf_tensor("red", [128, 4], F32))
        ps_s = e(nc.psum_tensor("ps_s", [128, 512], F32))
        ps_t = e(nc.psum_tensor("ps_t", [128, 512], BF16))
        ps_pv = e(nc.psum_tensor("ps_pv", [128, 64], F32))
        b = B(nc, ctx)
        for sname in ("ldg", "mm", "dv", "ac", "st"):
            b.sem(sname)
        block = e(nc.Block())
        b.emit("gpsimd", lambda g: g.dma_start(out=mks[:], in_=msk[:, :]), inc=("ldg", 16))
        b.emit("gpsimd", lambda g: g.dma_start(out=idb[:], in_=idd[:, :]), inc=("ldg", 16))

        for h in range(NH):
            stH = b.at("st")
            dvH, acH, mmH = b.at("dv"), b.at("ac"), b.at("mm")
            b.emit("gpsimd", lambda g, h=h: g.dma_start(out=qs[0:96, :], in_=qd[h][:, :]),
                   wait=[("st", stH), ("dv", dvH), ("mm", mmH)], inc=("ldg", 16))
            b.emit("gpsimd", lambda g, h=h: g.dma_start(out=ks[0:96, :], in_=kd[h][:, :]), inc=("ldg", 16))
            b.emit("gpsimd", lambda g, h=h: g.dma_start(
                out=vs[:].rearrange("p (j d) -> p j d", d=64),
                in_=vd[h].rearrange("(j p) d -> p j d", p=128)), inc=("ldg", 16))
            ldv = [("ldg", b.at("ldg"))]
            for eng in ("vector", "scalar", "tensor"):
                b.emit(eng, lambda x: None, wait=ldv)
            for qi in range(NQ):
                nkm = qi // 4 + 1
                nk = nkm * 512
                dg = qi // 4
                r = qi % 4
                for km in range(nkm):
                    b.emit("tensor", lambda t, qi=qi, km=km: t.matmul(
                        ps_s[:, :], qs[0:96, 128 * qi:128 * (qi + 1)], ks[0:96, 512 * km:512 * (km + 1)],
                        start=True, stop=True), wait=["dv"], inc="mm")
                    if km == dg:
                        b.emit("vector", lambda v, km=km, r=r: v.scalar_tensor_tensor(
                            sm[:, 512 * km:512 * (km + 1)], ps_s[:, :], SCALE, mks[:, 512 * r:512 * (r + 1)],
                            ALU.mult, ALU.add), wait=["mm", "ac"], inc="dv")
                    else:
                        b.emit("vector", lambda v, km=km: v.tensor_scalar(
                            sm[:, 512 * km:512 * (km + 1)], ps_s[:, :], SCALE, None, ALU.mult),
                            wait=["mm", "ac"], inc="dv")
                b.emit("vector", lambda v, nk=nk: v.reduce_max(red[:, 0:1], sm[:, 0:nk], axis=AX), wait="dv", inc="dv")
                b.emit("vector", lambda v: v.tensor_scalar(red[:, 1:2], red[:, 0:1], -1.0, None, ALU.mult),
                       wait="dv", inc="dv")
                b.emit("scalar", lambda s, nk=nk: s.activation(pb_[:, 0:nk], sm[:, 0:nk], AF.Exp, bias=red[:, 1:2]),
                       wait="dv", inc="ac")
                b.emit("vector", lambda v, nk=nk: v.reduce_sum(red[:, 2:3], pb_[:, 0:nk], axis=AX), wait="ac", inc="dv")
                b.emit("vector", lambda v: v.reciprocal(red[:, 3:4], red[:, 2:3]), wait="dv", inc="dv")
                nkt = nk // 128
                for kt in range(nkt):
                    sl = kt % 4
                    if sl == 0:
                        dvv = b.at("dv")
                        nsl = min(4, nkt - kt)
                        b.emit("tensor", lambda t, qi=qi, kt=kt, nsl=nsl: [
                            t.transpose(ps_t[:, 128 * j:128 * (j + 1)], pb_[:, 128 * (kt + j):128 * (kt + j + 1)], idb[:, :])
                            for j in range(nsl)][-1], wait=[("dv", dvv), "ac"], inc="mm")
                        b.emit("vector", lambda v, nsl=nsl: v.tensor_copy(pt[:, 0:128 * nsl], ps_t[:, 0:128 * nsl]),
                               wait="mm", inc="dv")
                    b.emit("tensor", lambda t, kt=kt, sl=sl, qi=qi: t.matmul(
                        ps_pv[:, :], pt[:, 128 * sl:128 * (sl + 1)], vs[:, 64 * kt:64 * (kt + 1)],
                        start=(kt == 0), stop=(kt == nkt - 1)),
                        wait=(["dv"] if sl == 0 else ()), inc=("mm" if kt == nkt - 1 else None))
                b.emit("vector", lambda v: v.tensor_scalar(osb[:, :], ps_pv[:, :], red[:, 3:4], None, ALU.mult),
                       wait=["mm", "dv"], inc="dv")
                b.emit("sync", lambda s, h=h, qi=qi: s.dma_start(
                    out=yo[h][128 * qi:128 * (qi + 1), :], in_=osb[:, :]), wait=["dv"], inc=("st", 16))
        b.emit("sync", lambda s: None, wait=["st"])
        b.install(block)
    return nc


def build_finalI(ntok, CH=512):
    nc = bass.Bass("TRN2", target_bir_lowering=False)
    NCH = ntok // CH
    yaT = nc.dram_tensor("yaT", [128, 8 * ntok], F32, kind="ExternalInput").ap()
    opT = nc.dram_tensor("opT", [128, 8 * ntok], F32, kind="ExternalInput").ap()
    wop = nc.dram_tensor("wop", [128, 8 * 8 * 128], F32, kind="ExternalInput").ap()
    out_o = nc.dram_tensor("out_o", [128, 8 * ntok], F32, kind="ExternalOutput").ap()
    with ExitStack() as ctx:
        e = ctx.enter_context
        wo = e(nc.sbuf_tensor("wo", [128, 8 * 8 * 128], BF16))
        yab = e(nc.sbuf_tensor("yab", [128, 8 * CH], BF16))
        opf = e(nc.sbuf_tensor("opf", [128, 8 * CH], F32))
        ot = e(nc.sbuf_tensor("ot", [128, 8 * CH], F32))
        ps_a = e(nc.psum_tensor("ps_a", [128, CH], F32))
        ps_b = e(nc.psum_tensor("ps_b", [128, CH], F32))
        b = B(nc, ctx)
        for sname in ("ldc", "ldg", "ldx", "mm", "dv", "ac", "st"):
            b.sem(sname)
        block = e(nc.Block())
        b.emit("gpsimd", lambda g: g.dma_start(out=wo[:], in_=wop[:, :]), inc=("ldg", 16))
        ld0 = [("ldg", b.at("ldg"))]
        for c in range(NCH):
            st0 = b.at("st")
            b.emit("gpsimd", lambda g, c=c: g.dma_start(
                out=yab[:].rearrange("p (u n) -> p u n", u=8),
                in_=yaT.rearrange("p (u n) -> p u n", u=8)[:, :, c * CH:(c + 1) * CH]),
                wait=[("st", st0)] + ld0, inc=("ldg", 16))
            b.emit("sync", lambda s, c=c: s.dma_start(
                out=opf[:].rearrange("p (u n) -> p u n", u=8),
                in_=opT.rearrange("p (u n) -> p u n", u=8)[:, :, c * CH:(c + 1) * CH]),
                wait=[("st", st0)], inc=("ldx", 16))
            ldv = [("ldx", b.at("ldx")), ("ldg", b.at("ldg")), ("st", st0)]
            for eng in ("vector", "scalar", "tensor"):
                b.emit(eng, lambda x: None, wait=ldv)
            guard = {0: ("dv", 0), 1: ("dv", 0)}
            for ei in range(8):
                pb = ps_a if ei % 2 == 0 else ps_b
                b.emit("tensor", mmgroup(pb[:, :],
                       [(wo[:, (ei * 8 + k) * 128:(ei * 8 + k + 1) * 128], yab[:, k * CH:(k + 1) * CH]) for k in range(8)]),
                       wait=[guard[ei % 2]], inc="mm")
                b.emit("vector", lambda v, ei=ei, pb=pb: v.tensor_add(
                    ot[:, ei * CH:(ei + 1) * CH], pb[:, :], opf[:, ei * CH:(ei + 1) * CH]),
                    wait=["mm", "dv"], inc="dv")
                guard[ei % 2] = ("dv", b.at("dv"))
            b.emit("sync", lambda s, c=c: s.dma_start(
                out=out_o.rearrange("p (u n) -> p u n", u=8)[:, :, c * CH:(c + 1) * CH],
                in_=ot[:].rearrange("p (u n) -> p u n", u=8)), wait=["dv"], inc=("st", 16))
        b.emit("sync", lambda s: None, wait=["st"])
        b.install(block)
    return nc


"""Orchestration: multi-launch kernel using klib builders."""

N = 16384
T = 8192
NTOK = 2048
CORES = list(range(8))


def assemble_scan(parts):
    a = np.stack([p.reshape(128, 8, NTOK) for p in parts])
    a = a.reshape(2, 4, 128, 8, NTOK)
    a = a.transpose(2, 1, 4, 3, 0)
    return np.ascontiguousarray(a.reshape(128, T * 16))


def slice_scan(h, c):
    b, tc = c // 4, c % 4
    a = h.reshape(128, T, 8, 2)[:, tc * NTOK:(tc + 1) * NTOK, :, b]
    return np.ascontiguousarray(a.transpose(0, 2, 1).reshape(128, 8 * NTOK))


def rope_scale():
    t = np.arange(T, dtype=np.float32)
    inv = 1.0 / 10000.0 ** (np.arange(0, 32, 2, dtype=np.float32) / 32.0)
    fr = t[:, None] * inv[None, :]
    emb = np.concatenate([fr, fr], -1)
    return (np.cos(emb) + np.sin(emb)).astype(np.float32)


def kernel(**inp):
    x = np.asarray(inp["L0_attn_x"], dtype=np.float32)
    xflat = np.ascontiguousarray(x.reshape(N, 1024))
    ln_g, ln_b = inp["L0_attn_ln_g"], inp["L0_attn_ln_b"]
    W_in, b_in = inp["L0_attn_W_in"], inp["L0_attn_b_in"]
    W_h, b_h = inp["L0_attn_W_h"], inp["L0_attn_b_h"]
    W_out, b_out = inp["L0_attn_W_out"], inp["L0_attn_b_out"]

    ncA = build_preA(NTOK)
    imsA = []
    for c in CORES:
        im = preA_inputs(xflat[c * NTOK:(c + 1) * NTOK], W_in[0], ln_g[0], ln_b[0],
                         b_in[0], b_h[0], inp["L0_ffn_router_w"], inp["L0_ffn_router_b"])
        imsA.append(im)
    rA = run_bass_kernel_spmd(ncA, imsA, core_ids=CORES).results
    xn_parts = [r["xn_o"] for r in rA]
    vpre0 = assemble_scan([r["vpre_o"] for r in rA])
    g0 = assemble_scan([r["gg_o"] for r in rA])
    mask = np.concatenate([r["mask_o"] for r in rA], 0)
    gates = np.concatenate([r["gates_o"] for r in rA], 0)

    ncS = build_scan(T)
    w0 = pack_lhsT(np.ascontiguousarray(W_h[0].T))
    h0 = run_bass_kernel_spmd(ncS, [{"w": w0, "vpre": vpre0, "g": g0}], core_ids=[0]).results[0]["hout"]

    ncC = build_midC(NTOK)
    woutp0 = pack_lhsT(np.ascontiguousarray(W_out[0].T))
    win1p = pack_lhsT(np.ascontiguousarray(W_in[1].T))
    imsC = []
    for c in CORES:
        imsC.append({
            "h0T": slice_scan(h0, c), "xnT": xn_parts[c],
            "wout": woutp0, "win1": win1p,
            "ln_g": vec_pm(ln_g[1]), "ln_b": vec_pm(ln_b[1]),
            "b_out": vec_pm(b_out[0]),
            "b_u": vec_pm(b_in[1][:1024]), "b_v": vec_pm(b_in[1][1024:]), "b_h": vec_pm(b_h[1]),
            "id128": np.eye(128, dtype=np.float32),
        })
    rC = run_bass_kernel_spmd(ncC, imsC, core_ids=CORES).results
    s0o_parts = [r["s0o_o"] for r in rC]
    vpre1 = assemble_scan([r["vpre_o"] for r in rC])
    g1 = assemble_scan([r["gg_o"] for r in rC])

    w1s = pack_lhsT(np.ascontiguousarray(W_h[1].T))
    h1 = run_bass_kernel_spmd(ncS, [{"w": w1s, "vpre": vpre1, "g": g1}], core_ids=[0]).results[0]["hout"]

    ncE = build_moeE(N)
    xn_full = np.concatenate(
        [p.reshape(128, 8, NTOK) for p in xn_parts], axis=2)
    xn_full = np.ascontiguousarray(xn_full.reshape(128, 8 * N))
    imsE = []
    for c in CORES:
        imsE.append({
            "xnT": xn_full,
            "w1p": pack_lhsT(np.ascontiguousarray(inp["L0_ffn_w1"][c])),
            "w2p": pack_lhsT(np.ascontiguousarray(inp["L0_ffn_w2"][c])),
            "b1p": vec_pm(inp["L0_ffn_b1"][c]), "b2p": vec_pm(inp["L0_ffn_b2"][c]),
            "mrow": np.ascontiguousarray(mask[:, c].reshape(1, N)),
            "grow": np.ascontiguousarray(gates[:, c].reshape(1, N)),
            "on1": np.ones((1, 128), np.float32),
        })
    rE = run_bass_kernel_spmd(ncE, imsE, core_ids=CORES).results
    ctrs = [r["ctr_o"].reshape(128, 8, N) for r in rE]

    ncF = build_combF(NTOK)
    wout1p = pack_lhsT(np.ascontiguousarray(W_out[1].T))
    imsF = []
    for c in CORES:
        ctr_c = np.stack([ct[:, :, c * NTOK:(c + 1) * NTOK] for ct in ctrs], axis=1)
        imsF.append({
            "xT": fm(xflat[c * NTOK:(c + 1) * NTOK]),
            "s0T": s0o_parts[c], "h1T": slice_scan(h1, c),
            "ctrT": np.ascontiguousarray(ctr_c.reshape(128, 8 * 8 * NTOK)),
            "wout": wout1p, "b_out": vec_pm(b_out[1]),
            "id128": np.eye(128, dtype=np.float32),
        })
    rF = run_bass_kernel_spmd(ncF, imsF, core_ids=CORES).results
    x1_parts = [r["x1_o"] for r in rF]
    xn1_parts = [r["xn1_o"] for r in rF]

    ncG1 = build_qkvG1(NTOK, CH=256)
    sc = rope_scale()
    wdq = pack_lhsT(inp["L1_attn_W_DQ"])
    wuq = pack_lhsT(inp["L1_attn_W_UQ"])
    wqr = pack_lhsT(inp["L1_attn_W_QR"])
    wdkv = pack_lhsT(inp["L1_attn_W_DKV"])
    wuk = pack_lhsT(inp["L1_attn_W_UK"])
    wuv = pack_lhsT(inp["L1_attn_W_UV"])
    wkr = pack_lhsT(inp["L1_attn_W_KR"])
    imsG1 = []
    for c in CORES:
        tr = np.arange(c % 4 * NTOK, (c % 4 + 1) * NTOK)
        scq = np.empty((128, 4 * NTOK), np.float32)
        for u4 in range(4):
            rows = np.arange(128) + 128 * u4
            scq[:, u4 * NTOK:(u4 + 1) * NTOK] = sc[tr][:, rows % 32].T
        sck = np.ascontiguousarray(sc[tr].T)
        imsG1.append({
            "xn1T": xn1_parts[c], "wdq": wdq, "wuq": wuq, "wqr": wqr,
            "wdkv": wdkv, "wuk": wuk, "wuv": wuv, "wkr": wkr,
            "scq": scq, "sck": sck,
        })
    rG1 = run_bass_kernel_spmd(ncG1, imsG1, core_ids=CORES).results

    ncG2 = build_ffnG2(NTOK, CH=256)
    w1p = pack_lhsT(np.ascontiguousarray(inp["L1_ffn_w1"].T))
    w2p = pack_lhsT(np.ascontiguousarray(inp["L1_ffn_w2"].T))
    imsG2 = []
    for c in CORES:
        imsG2.append({"xn1T": xn1_parts[c], "x1T": x1_parts[c],
                      "w1p": w1p, "w2p": w2p,
                      "b1p": vec_pm(inp["L1_ffn_b1"]), "b2p": vec_pm(inp["L1_ffn_b2"])})
    rG2 = run_bass_kernel_spmd(ncG2, imsG2, core_ids=CORES).results
    op_parts = [r["op_o"] for r in rG2]

    def asm(key, rows):
        return np.concatenate([rG1[c][key].reshape(128, rows // 128, NTOK) for c in CORES], axis=2) \
            .reshape(128, rows // 128, N)
    qc = asm("qc_o", 1024)
    qr = asm("qr_o", 512)
    kc = asm("kc_o", 1024)
    vv = asm("v_o", 1024)
    kr = np.concatenate([rG1[c]["kr_o"] for c in CORES], axis=1)

    def frows(a, r0, r1):
        out = np.empty((r1 - r0, a.shape[2]), np.float32)
        for r in range(r0, r1):
            out[r - r0] = a[r % 128, r // 128]
        return out

    ncH = build_attnH(T, 4)
    mvar = np.full((128, 4, 512), -1e30, np.float32)
    for r in range(4):
        for p in range(128):
            mvar[p, r, :128 * r + p + 1] = 0.0
    mflat = np.ascontiguousarray(mvar.reshape(128, 4 * 512))
    imsH = []
    for c in CORES:
        im = {"msk": mflat, "idd": np.eye(128, dtype=np.float32)}
        for i in range(4):
            pair = 4 * c + i
            bb, hh = pair // 16, pair % 16
            tok = slice(bb * T, (bb + 1) * T)
            qC = frows(qc[:, :, tok], hh * 64, (hh + 1) * 64)
            qR = frows(qr[:, :, tok], hh * 32, (hh + 1) * 32)
            kC = frows(kc[:, :, tok], hh * 64, (hh + 1) * 64)
            vh = frows(vv[:, :, tok], hh * 64, (hh + 1) * 64)
            im[f"q{i}"] = np.ascontiguousarray(np.concatenate([qC, qR], 0))
            im[f"k{i}"] = np.ascontiguousarray(np.concatenate([kC, kr[:, tok]], 0))
            im[f"v{i}"] = np.ascontiguousarray(vh.T)
        imsH.append(im)
    rH = run_bass_kernel_spmd(ncH, imsH, core_ids=CORES).results
    y_att = np.empty((N, 1024), np.float32)
    for c in CORES:
        for i in range(4):
            pair = 4 * c + i
            bb, hh = pair // 16, pair % 16
            y_att[bb * T:(bb + 1) * T, hh * 64:(hh + 1) * 64] = rH[c][f"y{i}"]

    ncI = build_finalI(NTOK)
    wo = pack_lhsT(inp["L1_attn_W_O"])
    imsI = []
    for c in CORES:
        imsI.append({"yaT": fm(y_att[c * NTOK:(c + 1) * NTOK]),
                     "opT": op_parts[c], "wop": wo})
    rI = run_bass_kernel_spmd(ncI, imsI, core_ids=CORES).results
    out = np.concatenate([unfm(r["out_o"], NTOK, 1024) for r in rI], 0)
    return out.reshape(2, T, 1024).astype(np.float32)


# version 08 (final)
# speedup vs baseline: 1.0668x; 1.0644x over previous
import numpy as np
import ml_dtypes
from contextlib import ExitStack
import concourse.bass as bass
import concourse.mybir as mybir
from concourse.bass_utils import run_bass_kernel_spmd

F32 = mybir.dt.float32
BF16 = mybir.dt.bfloat16
AF = mybir.ActivationFunctionType
ALU = mybir.AluOpType
AX = mybir.AxisListType.X
BFNP = ml_dtypes.bfloat16
ENGINES = ("sync", "gpsimd", "vector", "scalar", "tensor")
EPS = 1e-5


class B:

    def __init__(self, nc, ctx):
        self.nc = nc
        self.ctx = ctx
        self.ops = {e: [] for e in ENGINES}
        self.sems = {}
        self.cnt = {}

    def sem(self, name):
        h = self.ctx.enter_context(self.nc.semaphore(name))
        self.sems[name] = h
        self.cnt[name] = 0
        return h

    def at(self, name):
        return self.cnt[name]

    def emit(self, eng, fn, wait=(), inc=None):
        if isinstance(wait, (str, tuple)) and (not wait or isinstance(wait, str) or isinstance(wait[1], int)):
            wait = [wait] if wait else []
        wl = []
        for w in wait:
            if isinstance(w, str):
                wl.append((self.sems[w], self.cnt[w]))
            else:
                wl.append((self.sems[w[0]], w[1]))
        inch = None
        amt = 0
        if inc is not None:
            if isinstance(inc, tuple):
                name, amt = inc
            else:
                name, amt = inc, 1
            self.cnt[name] += amt
            inch = self.sems[name]

        def runner(e, fn=fn, wl=tuple(wl), inch=inch, amt=amt):
            for h, v in wl:
                if v > 0:
                    e.wait_ge(h, v)
            inst = fn(e)
            if inch is not None:
                assert inst is not None, "emit with inc= must return an instruction"
                inst.then_inc(inch, amt)
        self.ops[eng].append(runner)

    def install(self, block):
        for name, fns in self.ops.items():
            def runner(e, fns=tuple(fns)):
                for f in fns:
                    f(e)
            getattr(block, name)(runner)


def mmgroup(pb, pairs):
    def fn(t, pb=pb, pairs=tuple(pairs)):
        n = len(pairs)
        inst = None
        for q, (l, r) in enumerate(pairs):
            inst = t.matmul(pb, l, r, start=(q == 0), stop=(q == n - 1))
        return inst
    return fn


def pack_lhsT(wt):
    K, M = wt.shape
    KT = K // 128
    Mw = min(M, 128)
    MT = M // Mw
    return np.ascontiguousarray(
        wt.reshape(KT, 128, MT, Mw).transpose(2, 0, 1, 3).reshape(MT * KT, 128, Mw)
        .transpose(1, 0, 2).reshape(128, MT * KT * Mw))


def vec_pm(v):
    D = v.shape[0]
    return np.ascontiguousarray(v.reshape(D // 128, 128).T)


def fm(x):
    N, D = x.shape
    NT = D // 128
    return np.ascontiguousarray(x.T.reshape(NT, 128, N).transpose(1, 0, 2).reshape(128, NT * N))


def unfm(a, N, D):
    NT = D // 128
    return np.ascontiguousarray(a.reshape(128, NT, N).transpose(1, 0, 2).reshape(D, N).T)


def tm(x):
    N, D = x.shape
    return np.ascontiguousarray(x.reshape(N // 128, 128, D).transpose(1, 0, 2).reshape(128, (N // 128) * D))


def untm(a, N, D):
    return np.ascontiguousarray(a.reshape(128, N // 128, D).transpose(1, 0, 2).reshape(N, D))


def emit_ln(b, S, src_f32, dst, CH, *, src_tm=None, affine=None, out_bf=None):
    NTT = CH // 128
    sq, stat, murow, rsrow = S["sq"], S["stat"], S["murow"], S["rsrow"]
    ps_st, ps_tr, ps_tr2, ps_bc = S["ps_st"], S["ps_tr"], S["ps_tr2"], S["ps_bc"]
    ones, ones1, idm = S["ones"], S["ones1"], S["idm"]

    for j in range(NTT):
        if src_tm is not None:
            b.emit("scalar", lambda s, j=j: s.square(sq[:, 0:1024], src_tm[:, 1024 * j:1024 * (j + 1)]),
                   wait=("dv" if j > 0 else ()), inc="ac")
            b.emit("vector", lambda v, j=j: v.reduce_sum(stat[:, 8 * j:8 * j + 1], src_tm[:, 1024 * j:1024 * (j + 1)], axis=AX))
            b.emit("vector", lambda v, j=j: v.reduce_sum(stat[:, 8 * j + 1:8 * j + 2], sq[:, 0:1024], axis=AX),
                   wait="ac", inc="dv")
            b.emit("vector", lambda v, j=j: v.tensor_scalar(stat[:, 8 * j:8 * j + 2], stat[:, 8 * j:8 * j + 2], 1.0 / 1024.0, None, ALU.mult),
                   wait="dv", inc="dv")
        else:
            if j == 0:
                b.emit("scalar", lambda s: s.square(sq[:], src_f32[:]), wait="dv", inc="ac")
            b.emit("tensor", mmgroup(ps_st[:, 0:1],
                   [(src_f32[:, u * CH + 128 * j:u * CH + 128 * (j + 1)], ones[:, :]) for u in range(8)]),
                   wait=("ac" if j == 0 else "dv"), inc="mm")
            b.emit("tensor", mmgroup(ps_st[:, 1:2],
                   [(sq[:, u * CH + 128 * j:u * CH + 128 * (j + 1)], ones[:, :]) for u in range(8)]),
                   inc="mm")
            b.emit("vector", lambda v, j=j: v.tensor_scalar(stat[:, 8 * j:8 * j + 2], ps_st[:, 0:2], 1.0 / 1024.0, None, ALU.mult),
                   wait="mm", inc="dv")
        b.emit("vector", lambda v, j=j: v.tensor_tensor(stat[:, 8 * j + 2:8 * j + 3], stat[:, 8 * j:8 * j + 1], stat[:, 8 * j:8 * j + 1], ALU.mult),
               wait="dv", inc="dv")
        b.emit("vector", lambda v, j=j: v.tensor_tensor(stat[:, 8 * j + 3:8 * j + 4], stat[:, 8 * j + 1:8 * j + 2], stat[:, 8 * j + 2:8 * j + 3], ALU.subtract),
               wait="dv", inc="dv")
        b.emit("scalar", lambda s, j=j: s.activation(stat[:, 8 * j + 4:8 * j + 5], stat[:, 8 * j + 3:8 * j + 4], AF.Sqrt, bias=S["epsv"][:, 0:1]),
               wait="dv", inc="ac")
        b.emit("vector", lambda v, j=j: v.reciprocal(stat[:, 8 * j + 5:8 * j + 6], stat[:, 8 * j + 4:8 * j + 5]),
               wait="ac", inc="dv")
        def trfn(t, j=j):
            t.transpose(ps_tr[0:1, 0:128], stat[:, 8 * j:8 * j + 1], idm[:, :])
            return t.transpose(ps_tr2[0:1, 0:128], stat[:, 8 * j + 5:8 * j + 6], idm[:, :])
        b.emit("tensor", trfn, wait="dv", inc="mm")
        b.emit("vector", lambda v, j=j: v.tensor_copy(murow[0:1, 128 * j:128 * (j + 1)], ps_tr[0:1, 0:128]),
               wait="mm")
        b.emit("vector", lambda v, j=j: v.tensor_copy(rsrow[0:1, 128 * j:128 * (j + 1)], ps_tr2[0:1, 0:128]),
               inc="dv")
        def bcfn(t, j=j):
            t.matmul(ps_bc[:, 0:128], ones1[:, :], murow[0:1, 128 * j:128 * (j + 1)], start=True, stop=True)
            return t.matmul(ps_bc[:, 128:256], ones1[:, :], rsrow[0:1, 128 * j:128 * (j + 1)], start=True, stop=True)
        b.emit("tensor", bcfn, wait="dv", inc="mm")
        for u in range(8):
            sl = slice(u * CH + 128 * j, u * CH + 128 * (j + 1))
            b.emit("vector", lambda v, sl=sl: v.tensor_tensor(dst[:, sl], src_f32[:, sl], ps_bc[:, 0:128], ALU.subtract),
                   wait=("mm" if u == 0 else ()), inc="dv")
            b.emit("vector", lambda v, sl=sl: v.tensor_tensor(dst[:, sl], dst[:, sl], ps_bc[:, 128:256], ALU.mult),
                   wait="dv", inc="dv")
            if affine is not None:
                ga, ba = affine
                b.emit("vector", lambda v, sl=sl, u=u: v.tensor_scalar(dst[:, sl], dst[:, sl], ga[:, u:u + 1], ba[:, u:u + 1], ALU.mult, ALU.add),
                       wait="dv", inc="dv")
    if out_bf is not None:
        b.emit("scalar", lambda s: s.copy(out_bf[:], dst[:]), wait="dv", inc="ac")


def build_preA(ntok, CH=512, stages=(1, 2, 3, 4, 5, 6, 7)):
    nc = bass.Bass("TRN2", target_bir_lowering=False)
    NCH = ntok // CH
    NTT = CH // 128
    xT = nc.dram_tensor("xT", [128, 8 * ntok], F32, kind="ExternalInput").ap()
    xTM = nc.dram_tensor("xTM", [ntok, 1024], F32, kind="ExternalInput").ap()
    win = nc.dram_tensor("win", [128, 16 * 8 * 128], F32, kind="ExternalInput").ap()
    ln_g = nc.dram_tensor("ln_g", [128, 8], F32, kind="ExternalInput").ap()
    ln_b = nc.dram_tensor("ln_b", [128, 8], F32, kind="ExternalInput").ap()
    b_u = nc.dram_tensor("b_u", [128, 8], F32, kind="ExternalInput").ap()
    b_v = nc.dram_tensor("b_v", [128, 8], F32, kind="ExternalInput").ap()
    b_h = nc.dram_tensor("b_h", [128, 8], F32, kind="ExternalInput").ap()
    rw = nc.dram_tensor("rw", [128, 8 * 8], F32, kind="ExternalInput").ap()
    rb_bc = nc.dram_tensor("rb_bc", [128, 8], F32, kind="ExternalInput").ap()
    iota8 = nc.dram_tensor("iota8", [128, 8], F32, kind="ExternalInput").ap()
    id128 = nc.dram_tensor("id128", [128, 128], F32, kind="ExternalInput").ap()
    xn_o = nc.dram_tensor("xn_o", [128, 8 * ntok], F32, kind="ExternalOutput").ap()
    vpre_o = nc.dram_tensor("vpre_o", [128, 8 * ntok], F32, kind="ExternalOutput").ap()
    gg_o = nc.dram_tensor("gg_o", [128, 8 * ntok], F32, kind="ExternalOutput").ap()
    mask_o = nc.dram_tensor("mask_o", [ntok, 8], F32, kind="ExternalOutput").ap()
    gates_o = nc.dram_tensor("gates_o", [ntok, 8], F32, kind="ExternalOutput").ap()

    with ExitStack() as ctx:
        e = ctx.enter_context
        wsb = e(nc.sbuf_tensor("wsb", [128, 16 * 8 * 128], BF16))
        idm = e(nc.sbuf_tensor("idm", [128, 128], F32))
        ones = e(nc.sbuf_tensor("ones", [128, 1], F32))
        ones1 = e(nc.sbuf_tensor("ones1", [1, 128], F32))
        epsv = e(nc.sbuf_tensor("epsv", [128, 1], F32))
        lng = e(nc.sbuf_tensor("lng", [128, 8], F32))
        lnb = e(nc.sbuf_tensor("lnb", [128, 8], F32))
        bu = e(nc.sbuf_tensor("bu", [128, 8], F32))
        bv = e(nc.sbuf_tensor("bv", [128, 8], F32))
        bh = e(nc.sbuf_tensor("bh", [128, 8], F32))
        rws = e(nc.sbuf_tensor("rws", [128, 64], F32))
        rwh = e(nc.sbuf_tensor("rwh", [128, 64], BF16))
        rwl = e(nc.sbuf_tensor("rwl", [128, 64], BF16))
        rwt = e(nc.sbuf_tensor("rwt", [128, 64], F32))
        xlb = e(nc.sbuf_tensor("xlb", [128, 8 * CH], BF16))
        rbbc = e(nc.sbuf_tensor("rbbc", [128, 8], F32))
        iot = e(nc.sbuf_tensor("iot", [128, 8], F32))
        xf = e(nc.sbuf_tensor("xf", [128, 8 * CH], F32))
        xtm = e(nc.sbuf_tensor("xtm", [128, NTT * 1024], F32))
        sq = e(nc.sbuf_tensor("sq", [128, 8 * CH], F32))
        xn = e(nc.sbuf_tensor("xn", [128, 8 * CH], F32))
        xnbf = e(nc.sbuf_tensor("xnbf", [128, 8 * CH], BF16))
        s0l = e(nc.sbuf_tensor("s0l", [128, 8 * CH], F32))
        s0lb = e(nc.sbuf_tensor("s0lb", [128, 8 * CH], BF16))
        stat = e(nc.sbuf_tensor("stat", [128, 8 * NTT], F32))
        murow = e(nc.sbuf_tensor("murow", [1, 128 * NTT], F32))
        rsrow = e(nc.sbuf_tensor("rsrow", [1, 128 * NTT], F32))
        vout = e(nc.sbuf_tensor("vout", [128, 8 * CH], F32))
        gout = e(nc.sbuf_tensor("gout", [128, 8 * CH], F32))
        rmask = e(nc.sbuf_tensor("rmask", [128, 8 * NTT], F32))
        rgate = e(nc.sbuf_tensor("rgate", [128, 8 * NTT], F32))
        rt1 = e(nc.sbuf_tensor("rt1", [128, 8], F32))
        rt2 = e(nc.sbuf_tensor("rt2", [128, 8], F32))
        rt3 = e(nc.sbuf_tensor("rt3", [128, 8], F32))
        rsc = e(nc.sbuf_tensor("rsc", [128, 4], F32))
        ps_st = e(nc.psum_tensor("ps_st", [128, 2], F32))
        ps_tr = e(nc.psum_tensor("ps_tr", [1, 128], F32))
        ps_tr2 = e(nc.psum_tensor("ps_tr2", [1, 128], F32))
        ps_bc = e(nc.psum_tensor("ps_bc", [128, 256], F32))
        ps_uv0 = e(nc.psum_tensor("ps_uv0", [128, CH], F32))
        ps_uv1 = e(nc.psum_tensor("ps_uv1", [128, CH], F32))
        ps_rt = e(nc.psum_tensor("ps_rt", [128, 8], F32))
        b = B(nc, ctx)
        for sname in ("ldc", "ldg", "ldx", "mm", "dv", "ac", "st"):
            b.sem(sname)
        block = e(nc.Block())
        S = dict(ones=ones, ones1=ones1, idm=idm, sq=sq, stat=stat, murow=murow,
                 rsrow=rsrow, ps_st=ps_st, ps_tr=ps_tr, ps_tr2=ps_tr2, ps_bc=ps_bc, epsv=epsv)

        def dve(fn):
            b.emit("vector", fn, wait=["dv", "ac"], inc="dv")

        for dst, src in ((idm, id128), (lng, ln_g), (lnb, ln_b), (bu, b_u),
                         (bv, b_v), (bh, b_h), (rbbc, rb_bc), (iot, iota8)):
            b.emit("sync", lambda s, d=dst, sr=src: s.dma_start(out=d[:], in_=sr[:, :]), inc=("ldc", 16))
        b.emit("gpsimd", lambda g: g.dma_start(out=wsb[:], in_=win[:, :]), inc=("ldg", 16))
        b.emit("gpsimd", lambda g: g.dma_start(out=rws[:], in_=rw[:, :]), inc=("ldg", 16))
        ld0 = [("ldc", b.at("ldc")), ("ldg", b.at("ldg"))]
        b.emit("vector", lambda v: v.memset(ones[:], 1.0))
        b.emit("vector", lambda v: v.memset(ones1[:], 1.0))
        b.emit("vector", lambda v: v.memset(epsv[:], EPS), inc="dv")
        b.emit("vector", lambda v: v.tensor_add(bv[:], bv[:], bh[:]), wait=ld0, inc="dv")
        b.emit("scalar", lambda s: s.copy(rwh[:], rws[:]), wait=ld0, inc="ac")
        b.emit("scalar", lambda s: s.copy(rwt[:], rwh[:]), wait="ac", inc="ac")
        b.emit("vector", lambda v: v.tensor_tensor(rwt[:], rws[:], rwt[:], ALU.subtract), wait="ac", inc="dv")
        b.emit("scalar", lambda s: s.copy(rwl[:], rwt[:]), wait="dv", inc="ac")
        for eng in ("scalar", "tensor"):
            b.emit(eng, lambda x: None, wait=ld0 + ["dv"])

        for c in range(NCH):
            st0 = b.at("st")
            b.emit("sync", lambda s, c=c: s.dma_start(
                out=xf[:].rearrange("p (u n) -> p u n", u=8),
                in_=xT.rearrange("p (u n) -> p u n", u=8)[:, :, c * CH:(c + 1) * CH]),
                wait=[("st", st0)] + ld0, inc=("ldx", 16))
            b.emit("sync", lambda s, c=c: s.dma_start(
                out=xtm[:].rearrange("p (j d) -> p j d", d=1024),
                in_=xTM[c * CH:(c + 1) * CH, :].rearrange("(j p) d -> p j d", p=128)),
                inc=("ldx", 16))
            ldv = b.at("ldx")
            for eng in ("vector", "scalar", "tensor"):
                b.emit(eng, lambda x: None, wait=[("ldx", ldv), ("ldg", b.at("ldg")), ("st", st0)] + ld0)

            emit_ln(b, S, xf, xn, CH, src_tm=xtm, affine=None, out_bf=xnbf)
            dv_xn = b.at("dv")
            b.emit("sync", lambda s, c=c: s.dma_start(
                out=xn_o.rearrange("p (u n) -> p u n", u=8)[:, :, c * CH:(c + 1) * CH],
                in_=xn[:].rearrange("p (u n) -> p u n", u=8)),
                wait=[("dv", dv_xn)], inc=("st", 16))

            if 2 not in stages:
                continue
            emit_ln(b, S, xn, s0l, CH, src_tm=None, affine=(lng, lnb), out_bf=s0lb)
            ac_s0lb = b.at("ac")

            do_uv = 3 in stages
            bank_guard = {0: ("ac", 0), 1: ("ac", 0)}
            for ei in (range(16) if do_uv else ()):
                pb = ps_uv0 if ei % 2 == 0 else ps_uv1
                waits = [("ac", ac_s0lb), bank_guard[ei % 2]]
                b.emit("tensor", mmgroup(pb[:, :],
                       [(wsb[:, (ei * 8 + k) * 128:(ei * 8 + k + 1) * 128], s0lb[:, k * CH:(k + 1) * CH]) for k in range(8)]),
                       wait=waits, inc="mm")
                if ei < 8:
                    b.emit("scalar", lambda s, ei=ei, pb=pb: s.activation(
                        gout[:, ei * CH:(ei + 1) * CH], pb[:, :], AF.Sigmoid, bias=bu[:, ei:ei + 1]),
                        wait="mm", inc="ac")
                    bank_guard[ei % 2] = ("ac", b.at("ac"))
                else:
                    u = ei - 8
                    b.emit("vector", lambda v, u=u, pb=pb: v.tensor_scalar(
                        vout[:, u * CH:(u + 1) * CH], pb[:, :], bv[:, u:u + 1], None, ALU.add),
                        wait="mm", inc="dv")
                    bank_guard[ei % 2] = ("dv", b.at("dv"))
            if do_uv:
                b.emit("sync", lambda s, c=c: s.dma_start(
                    out=vpre_o.rearrange("p (u n) -> p u n", u=8)[:, :, c * CH:(c + 1) * CH],
                    in_=vout[:].rearrange("p (u n) -> p u n", u=8)),
                    wait=["dv"], inc=("st", 16))
                b.emit("sync", lambda s, c=c: s.dma_start(
                    out=gg_o.rearrange("p (u n) -> p u n", u=8)[:, :, c * CH:(c + 1) * CH],
                    in_=gout[:].rearrange("p (u n) -> p u n", u=8)),
                    wait=["ac"], inc=("st", 16))

            if 4 not in stages:
                continue
            b.emit("scalar", lambda s: s.copy(s0l[:], xnbf[:]), wait=["dv", "mm"], inc="ac")
            b.emit("vector", lambda v: v.tensor_tensor(sq[:], xn[:], s0l[:], ALU.subtract), wait="ac", inc="dv")
            b.emit("scalar", lambda s: s.copy(xlb[:], sq[:]), wait="dv", inc="ac")
            ac_split = b.at("ac")
            rt_guard = ("dv", 0)
            for j in range(NTT):
                b.emit("tensor", mmgroup(ps_rt[:, :],
                       [(xa[:, u * CH + 128 * j:u * CH + 128 * (j + 1)], wb[:, u * 8:(u + 1) * 8])
                        for (xa, wb) in ((xnbf, rwh), (xnbf, rwl), (xlb, rwh), (xlb, rwl))
                        for u in range(8)]),
                       wait=[("ac", ac_split), rt_guard], inc="mm")
                base = 8 * j
                b.emit("vector", lambda v: v.tensor_tensor(rt1[:, :], ps_rt[:, :], rbbc[:, :], ALU.add),
                       wait=["dv", "mm", "ac"], inc="dv")
                rt_guard = ("dv", b.at("dv"))
                if 5 not in stages:
                    continue
                dve(lambda v: v.reduce_max(rsc[:, 0:1], rt1[:, :], axis=AX))
                dve(lambda v: v.tensor_scalar(rsc[:, 1:2], rsc[:, 0:1], -1.0, None, ALU.mult))
                b.emit("scalar", lambda s: s.activation(rt2[:, :], rt1[:, :], AF.Exp, bias=rsc[:, 1:2]),
                       wait="dv", inc="ac")
                b.emit("vector", lambda v: v.reduce_sum(rsc[:, 2:3], rt2[:, :], axis=AX), wait="ac", inc="dv")
                dve(lambda v: v.reciprocal(rsc[:, 3:4], rsc[:, 2:3]))
                dve(lambda v, b8=base: v.tensor_scalar(rgate[:, b8:b8 + 8], rt2[:, :], rsc[:, 3:4], None, ALU.mult))
                if 6 not in stages:
                    continue
                dve(lambda v, b8=base: v.reduce_max(rsc[:, 0:1], rgate[:, b8:b8 + 8], axis=AX))
                dve(lambda v, b8=base: v.tensor_scalar(rt1[:, :], rgate[:, b8:b8 + 8], rsc[:, 0:1], None, ALU.is_ge))
                dve(lambda v: v.tensor_scalar(rt2[:, :], rt1[:, :], -1e9, 1e9, ALU.mult, ALU.add))
                dve(lambda v: v.tensor_tensor(rt2[:, :], rt2[:, :], iot[:, :], ALU.add))
                dve(lambda v: v.tensor_reduce(rsc[:, 0:1], rt2[:, :], AX, ALU.min))
                dve(lambda v: v.tensor_scalar(rt1[:, :], rt2[:, :], rsc[:, 0:1], None, ALU.is_le))
                if 7 not in stages:
                    dve(lambda v, b8=base: v.tensor_copy(rmask[:, b8:b8 + 8], rt1[:, :]))
                    continue
                dve(lambda v, b8=base: v.tensor_scalar(rt2[:, :], rt1[:, :], -2.0, None, ALU.mult))
                dve(lambda v, b8=base: v.tensor_tensor(rt2[:, :], rt2[:, :], rgate[:, b8:b8 + 8], ALU.add))
                dve(lambda v: v.reduce_max(rsc[:, 0:1], rt2[:, :], axis=AX))
                dve(lambda v: v.tensor_scalar(rt3[:, :], rt2[:, :], rsc[:, 0:1], None, ALU.is_ge))
                dve(lambda v: v.tensor_scalar(rt2[:, :], rt3[:, :], -1e9, 1e9, ALU.mult, ALU.add))
                dve(lambda v: v.tensor_tensor(rt2[:, :], rt2[:, :], iot[:, :], ALU.add))
                dve(lambda v: v.tensor_reduce(rsc[:, 0:1], rt2[:, :], AX, ALU.min))
                dve(lambda v: v.tensor_scalar(rt3[:, :], rt2[:, :], rsc[:, 0:1], None, ALU.is_le))
                dve(lambda v, b8=base: v.tensor_tensor(rmask[:, b8:b8 + 8], rt1[:, :], rt3[:, :], ALU.max))
            b.emit("sync", lambda s, c=c: s.dma_start(
                out=mask_o[c * CH:(c + 1) * CH, :].rearrange("(j p) d -> p j d", p=128),
                in_=rmask[:, 0:8 * NTT].rearrange("p (j d) -> p j d", d=8)), wait=["dv"], inc=("st", 16))
            b.emit("sync", lambda s, c=c: s.dma_start(
                out=gates_o[c * CH:(c + 1) * CH, :].rearrange("(j p) d -> p j d", p=128),
                in_=rgate[:, 0:8 * NTT].rearrange("p (j d) -> p j d", d=8)), inc=("st", 16))

        b.emit("sync", lambda s: None, wait=["st"])
        b.install(block)
    return nc


def preA_inputs(x_slice, W_in0, ln_g0, ln_b0, b_in0, b_h0, router_w, router_b):
    ntok = x_slice.shape[0]
    return {
        "xT": fm(x_slice),
        "xTM": np.ascontiguousarray(x_slice),
        "win": pack_lhsT(np.ascontiguousarray(W_in0.T)),
        "ln_g": vec_pm(ln_g0), "ln_b": vec_pm(ln_b0),
        "b_u": vec_pm(b_in0[:1024]), "b_v": vec_pm(b_in0[1024:]), "b_h": vec_pm(b_h0),
        "rw": np.ascontiguousarray(router_w.reshape(8, 128, 8).transpose(1, 0, 2).reshape(128, 64)),
        "rb_bc": np.ascontiguousarray(np.tile(router_b[None, :], (128, 1)).astype(np.float32)),
        "iota8": np.ascontiguousarray(np.tile(np.arange(8, dtype=np.float32)[None, :], (128, 1))),
        "id128": np.eye(128, dtype=np.float32),
    }



def build_scan(T, CHUNK=128):
    nc = bass.Bass("TRN2", target_bir_lowering=False)
    w = nc.dram_tensor("w", [128, 64 * 128], F32, kind="ExternalInput").ap()
    vpre = nc.dram_tensor("vpre", [128, 16 * T], F32, kind="ExternalInput").ap()
    g = nc.dram_tensor("g", [128, 16 * T], F32, kind="ExternalInput").ap()
    hout = nc.dram_tensor("hout", [128, 16 * T], F32, kind="ExternalOutput").ap()
    nch = T // CHUNK
    with ExitStack() as ctx:
        e = ctx.enter_context
        ws = e(nc.sbuf_tensor("ws", [128, 64 * 128], BF16))
        vs = e(nc.sbuf_tensor("vs", [128, 16 * CHUNK * 2], F32))
        gs = e(nc.sbuf_tensor("gs", [128, 16 * CHUNK * 2], F32))
        gms = e(nc.sbuf_tensor("gms", [128, 16 * CHUNK * 2], F32))
        hall = e(nc.sbuf_tensor("hall", [128, 16 * CHUNK * 2], F32))
        z16 = e(nc.sbuf_tensor("z16", [128, 16], F32))
        hb = e(nc.sbuf_tensor("hb", [128, 16], BF16))
        zs = e(nc.sbuf_tensor("zs", [128, 16], F32))
        cs = e(nc.sbuf_tensor("cs", [128, 16], F32))
        t1 = e(nc.sbuf_tensor("t1", [128, 16], F32))
        t2 = e(nc.sbuf_tensor("t2", [128, 16], F32))
        ps = e(nc.psum_tensor("ps", [128, 16], F32))
        ldg = e(nc.semaphore("ldg"))
        ldx0 = e(nc.semaphore("ldx0"))
        ldx1 = e(nc.semaphore("ldx1"))
        mm_s = e(nc.semaphore("mm_s"))
        z_s = e(nc.semaphore("z_s"))
        c_s = e(nc.semaphore("c_s"))
        h_s = e(nc.semaphore("h_s"))
        e_s = e(nc.semaphore("e_s"))
        gm_s = e(nc.semaphore("gm_s"))
        hb_s = e(nc.semaphore("hb_s"))
        stc = e(nc.semaphore("stc"))
        block = e(nc.Block())

        @block.gpsimd
        def _(gp):
            gp.dma_start(out=ws[:], in_=w[:, :]).then_inc(ldg, 16)

        @block.sync
        def _(sy):
            for c in range(nch):
                buf = c % 2
                sem = ldx0 if buf == 0 else ldx1
                if c >= 2:
                    sy.wait_ge(h_s, (c - 1) * CHUNK)
                    sy.wait_ge(stc, 16 * (c - 1))
                sy.dma_start(out=vs[:, 16 * CHUNK * buf:16 * CHUNK * (buf + 1)],
                             in_=vpre[:, 16 * CHUNK * c:16 * CHUNK * (c + 1)]).then_inc(sem, 16)
                sy.dma_start(out=gs[:, 16 * CHUNK * buf:16 * CHUNK * (buf + 1)],
                             in_=g[:, 16 * CHUNK * c:16 * CHUNK * (c + 1)]).then_inc(sem, 16)
                sy.wait_ge(h_s, (c + 1) * CHUNK)
                sy.dma_start(out=hout[:, 16 * CHUNK * c:16 * CHUNK * (c + 1)],
                             in_=hall[:, 16 * CHUNK * buf:16 * CHUNK * (buf + 1)]).then_inc(stc, 16)
            sy.wait_ge(stc, 16 * nch)

        @block.vector
        def _(v):
            v.memset(z16[:], 0.0)
            v.memset(hb[:], 0.0).then_inc(hb_s, 1)
            for c in range(nch):
                buf = c % 2
                sem = ldx0 if buf == 0 else ldx1
                v.wait_ge(sem, 32 * (c // 2 + 1))
                v.tensor_scalar(gms[:, 16 * CHUNK * buf:16 * CHUNK * (buf + 1)],
                                gs[:, 16 * CHUNK * buf:16 * CHUNK * (buf + 1)],
                                -1.0, 1.0, ALU.mult, ALU.add).then_inc(gm_s, 1)
                v.wait_ge(gm_s, c + 1)
                for tt in range(CHUNK):
                    t = c * CHUNK + tt
                    col = slice(16 * (CHUNK * buf + tt), 16 * (CHUNK * buf + tt + 1))
                    if tt == 0:
                        pcol = None if c == 0 else slice(16 * (CHUNK * ((c - 1) % 2) + CHUNK - 1),
                                                         16 * (CHUNK * ((c - 1) % 2) + CHUNK))
                    else:
                        pcol = slice(16 * (CHUNK * buf + tt - 1), 16 * (CHUNK * buf + tt))
                    hprev = z16[:, 0:16] if pcol is None else hall[:, pcol]
                    v.wait_ge(mm_s, t + 1)
                    v.tensor_add(zs[:], ps[:, 0:16], vs[:, col]).then_inc(z_s, 1)
                    if t > 0:
                        v.wait_ge(h_s, t)
                    v.tensor_mul(t2[:], gms[:, col], hprev)
                    v.wait_ge(c_s, t + 1)
                    v.tensor_mul(t1[:], gs[:, col], cs[:]).then_inc(e_s, 1)
                    v.wait_ge(e_s, t + 1)
                    v.tensor_add(hb[:], t1[:], t2[:]).then_inc(hb_s, 1)
                    v.tensor_add(hall[:, col], t1[:], t2[:]).then_inc(h_s, 1)

        @block.scalar
        def _(s):
            for t in range(T):
                s.wait_ge(z_s, t + 1)
                s.activation(out=cs[:], in_=zs[:], func=AF.Tanh).then_inc(c_s, 1)

        @block.tensor
        def _(te):
            te.wait_ge(ldg, 16)
            with te.Fori(0, T) as ii:
                te.wait_ge(hb_s, ii + 1)
                for i in range(8):
                    for k in range(8):
                        inst = te.matmul(ps[:, 2 * i:2 * i + 2],
                                         ws[:, (i * 8 + k) * 128:(i * 8 + k + 1) * 128],
                                         hb[:, 2 * k:2 * k + 2],
                                         start=(k == 0), stop=(k == 7))
                inst.then_inc(mm_s, 1)
    return nc


def scan_inputs(W_h, vpre_tdb, g_tdb):
    T = vpre_tdb.shape[0]

    def pk(x):
        return np.ascontiguousarray(x.reshape(T, 8, 128, 2).transpose(2, 0, 1, 3).reshape(128, T * 16))
    return {"w": pack_lhsT(np.ascontiguousarray(W_h.T)), "vpre": pk(vpre_tdb), "g": pk(g_tdb)}


def scan_h_unpack(hout, T):
    return np.ascontiguousarray(hout.reshape(128, T, 8, 2).transpose(1, 2, 0, 3).reshape(T, 1024, 2))


def build_midC(ntok, CH=512):
    nc = bass.Bass("TRN2", target_bir_lowering=False)
    NCH = ntok // CH
    NTT = CH // 128
    h0T = nc.dram_tensor("h0T", [128, 8 * ntok], F32, kind="ExternalInput").ap()
    xnT = nc.dram_tensor("xnT", [128, 8 * ntok], F32, kind="ExternalInput").ap()
    wout = nc.dram_tensor("wout", [128, 8 * 8 * 128], F32, kind="ExternalInput").ap()
    win1 = nc.dram_tensor("win1", [128, 16 * 8 * 128], F32, kind="ExternalInput").ap()
    ln_g = nc.dram_tensor("ln_g", [128, 8], F32, kind="ExternalInput").ap()
    ln_b = nc.dram_tensor("ln_b", [128, 8], F32, kind="ExternalInput").ap()
    b_out = nc.dram_tensor("b_out", [128, 8], F32, kind="ExternalInput").ap()
    b_u = nc.dram_tensor("b_u", [128, 8], F32, kind="ExternalInput").ap()
    b_v = nc.dram_tensor("b_v", [128, 8], F32, kind="ExternalInput").ap()
    b_h = nc.dram_tensor("b_h", [128, 8], F32, kind="ExternalInput").ap()
    id128 = nc.dram_tensor("id128", [128, 128], F32, kind="ExternalInput").ap()
    s0o_o = nc.dram_tensor("s0o_o", [128, 8 * ntok], F32, kind="ExternalOutput").ap()
    vpre_o = nc.dram_tensor("vpre_o", [128, 8 * ntok], F32, kind="ExternalOutput").ap()
    gg_o = nc.dram_tensor("gg_o", [128, 8 * ntok], F32, kind="ExternalOutput").ap()
    with ExitStack() as ctx:
        e = ctx.enter_context
        wo = e(nc.sbuf_tensor("wo", [128, 8 * 8 * 128], BF16))
        w1 = e(nc.sbuf_tensor("w1", [128, 16 * 8 * 128], BF16))
        idm = e(nc.sbuf_tensor("idm", [128, 128], F32))
        ones = e(nc.sbuf_tensor("ones", [128, 1], F32))
        ones1 = e(nc.sbuf_tensor("ones1", [1, 128], F32))
        epsv = e(nc.sbuf_tensor("epsv", [128, 1], F32))
        lng = e(nc.sbuf_tensor("lng", [128, 8], F32))
        lnb = e(nc.sbuf_tensor("lnb", [128, 8], F32))
        bo = e(nc.sbuf_tensor("bo", [128, 8], F32))
        bu = e(nc.sbuf_tensor("bu", [128, 8], F32))
        bv = e(nc.sbuf_tensor("bv", [128, 8], F32))
        bh = e(nc.sbuf_tensor("bh", [128, 8], F32))
        h0b = e(nc.sbuf_tensor("h0b", [128, 8 * CH], BF16))
        xnf = e(nc.sbuf_tensor("xnf", [128, 8 * CH], F32))
        s0o = e(nc.sbuf_tensor("s0o", [128, 8 * CH], F32))
        sq = e(nc.sbuf_tensor("sq", [128, 8 * CH], F32))
        s1l = e(nc.sbuf_tensor("s1l", [128, 8 * CH], F32))
        s1lb = e(nc.sbuf_tensor("s1lb", [128, 8 * CH], BF16))
        stat = e(nc.sbuf_tensor("stat", [128, 8 * NTT], F32))
        murow = e(nc.sbuf_tensor("murow", [1, 128 * NTT], F32))
        rsrow = e(nc.sbuf_tensor("rsrow", [1, 128 * NTT], F32))
        vout = e(nc.sbuf_tensor("vout", [128, 8 * CH], F32))
        gout = e(nc.sbuf_tensor("gout", [128, 8 * CH], F32))
        ps_st = e(nc.psum_tensor("ps_st", [128, 2], F32))
        ps_tr = e(nc.psum_tensor("ps_tr", [1, 128], F32))
        ps_tr2 = e(nc.psum_tensor("ps_tr2", [1, 128], F32))
        ps_bc = e(nc.psum_tensor("ps_bc", [128, 256], F32))
        ps_a = e(nc.psum_tensor("ps_a", [128, CH], F32))
        ps_b = e(nc.psum_tensor("ps_b", [128, CH], F32))
        b = B(nc, ctx)
        for sname in ("ldc", "ldg", "ldx", "mm", "dv", "ac", "st"):
            b.sem(sname)
        block = e(nc.Block())
        S = dict(ones=ones, ones1=ones1, idm=idm, sq=sq, stat=stat, murow=murow,
                 rsrow=rsrow, ps_st=ps_st, ps_tr=ps_tr, ps_tr2=ps_tr2, ps_bc=ps_bc, epsv=epsv)

        for dst, src in ((idm, id128), (lng, ln_g), (lnb, ln_b), (bo, b_out),
                         (bu, b_u), (bv, b_v), (bh, b_h)):
            b.emit("sync", lambda s, d=dst, sr=src: s.dma_start(out=d[:], in_=sr[:, :]), inc=("ldc", 16))
        b.emit("gpsimd", lambda g: g.dma_start(out=wo[:], in_=wout[:, :]), inc=("ldg", 16))
        b.emit("gpsimd", lambda g: g.dma_start(out=w1[:], in_=win1[:, :]), inc=("ldg", 16))
        ld0 = [("ldc", b.at("ldc")), ("ldg", b.at("ldg"))]
        b.emit("vector", lambda v: v.memset(ones[:], 1.0))
        b.emit("vector", lambda v: v.memset(ones1[:], 1.0))
        b.emit("vector", lambda v: v.memset(epsv[:], EPS), inc="dv")
        b.emit("vector", lambda v: v.tensor_add(bv[:], bv[:], bh[:]), wait=ld0, inc="dv")
        for eng in ("scalar", "tensor"):
            b.emit(eng, lambda x: None, wait=ld0 + ["dv"])

        for c in range(NCH):
            st0 = b.at("st")
            b.emit("gpsimd", lambda g, c=c: g.dma_start(
                out=h0b[:].rearrange("p (u n) -> p u n", u=8),
                in_=h0T.rearrange("p (u n) -> p u n", u=8)[:, :, c * CH:(c + 1) * CH]),
                wait=[("st", st0)] + ld0, inc=("ldg", 16))
            b.emit("sync", lambda s, c=c: s.dma_start(
                out=xnf[:].rearrange("p (u n) -> p u n", u=8),
                in_=xnT.rearrange("p (u n) -> p u n", u=8)[:, :, c * CH:(c + 1) * CH]),
                wait=[("st", st0)] + ld0, inc=("ldx", 16))
            ldv = [("ldx", b.at("ldx")), ("ldg", b.at("ldg")), ("st", st0)]
            for eng in ("vector", "scalar", "tensor"):
                b.emit(eng, lambda x: None, wait=ldv)

            bank_guard = {0: ("dv", 0), 1: ("dv", 0)}
            for ei in range(8):
                pb = ps_a if ei % 2 == 0 else ps_b
                b.emit("tensor", mmgroup(pb[:, :],
                       [(wo[:, (ei * 8 + k) * 128:(ei * 8 + k + 1) * 128], h0b[:, k * CH:(k + 1) * CH]) for k in range(8)]),
                       wait=[bank_guard[ei % 2]], inc="mm")
                b.emit("vector", lambda v, ei=ei, pb=pb: v.tensor_scalar(sq[:, ei * CH:(ei + 1) * CH], pb[:, :], bo[:, ei:ei + 1], None, ALU.add),
                       wait="mm", inc="dv")
                b.emit("vector", lambda v, ei=ei: v.tensor_add(s0o[:, ei * CH:(ei + 1) * CH], sq[:, ei * CH:(ei + 1) * CH], xnf[:, ei * CH:(ei + 1) * CH]),
                       wait="dv", inc="dv")
                bank_guard[ei % 2] = ("dv", b.at("dv"))
            dv_s0 = b.at("dv")
            b.emit("sync", lambda s, c=c: s.dma_start(
                out=s0o_o.rearrange("p (u n) -> p u n", u=8)[:, :, c * CH:(c + 1) * CH],
                in_=s0o[:].rearrange("p (u n) -> p u n", u=8)),
                wait=[("dv", dv_s0)], inc=("st", 16))

            emit_ln(b, S, s0o, s1l, CH, src_tm=None, affine=(lng, lnb), out_bf=s1lb)
            ac_s1lb = b.at("ac")

            bank_guard = {0: ("dv", 0), 1: ("dv", 0)}
            for ei in range(16):
                pb = ps_a if ei % 2 == 0 else ps_b
                b.emit("tensor", mmgroup(pb[:, :],
                       [(w1[:, (ei * 8 + k) * 128:(ei * 8 + k + 1) * 128], s1lb[:, k * CH:(k + 1) * CH]) for k in range(8)]),
                       wait=[("ac", ac_s1lb), bank_guard[ei % 2]], inc="mm")
                if ei < 8:
                    b.emit("scalar", lambda s, ei=ei, pb=pb: s.activation(
                        gout[:, ei * CH:(ei + 1) * CH], pb[:, :], AF.Sigmoid, bias=bu[:, ei:ei + 1]),
                        wait="mm", inc="ac")
                    bank_guard[ei % 2] = ("ac", b.at("ac"))
                else:
                    u = ei - 8
                    b.emit("vector", lambda v, u=u, pb=pb: v.tensor_scalar(
                        vout[:, u * CH:(u + 1) * CH], pb[:, :], bv[:, u:u + 1], None, ALU.add),
                        wait="mm", inc="dv")
                    bank_guard[ei % 2] = ("dv", b.at("dv"))
            b.emit("sync", lambda s, c=c: s.dma_start(
                out=vpre_o.rearrange("p (u n) -> p u n", u=8)[:, :, c * CH:(c + 1) * CH],
                in_=vout[:].rearrange("p (u n) -> p u n", u=8)),
                wait=["dv"], inc=("st", 16))
            b.emit("sync", lambda s, c=c: s.dma_start(
                out=gg_o.rearrange("p (u n) -> p u n", u=8)[:, :, c * CH:(c + 1) * CH],
                in_=gout[:].rearrange("p (u n) -> p u n", u=8)),
                wait=["ac"], inc=("st", 16))
        b.emit("sync", lambda s: None, wait=["st"])
        b.install(block)
    return nc


def build_moeE(N, CH=256, CAP=2560):
    nc = bass.Bass("TRN2", target_bir_lowering=False)
    NCH = N // CH
    xnT = nc.dram_tensor("xnT", [128, 8 * N], F32, kind="ExternalInput").ap()
    w1p = nc.dram_tensor("w1p", [128, 32 * 8 * 128], F32, kind="ExternalInput").ap()
    w2p = nc.dram_tensor("w2p", [128, 8 * 32 * 128], F32, kind="ExternalInput").ap()
    b1p = nc.dram_tensor("b1p", [128, 32], F32, kind="ExternalInput").ap()
    b2p = nc.dram_tensor("b2p", [128, 8], F32, kind="ExternalInput").ap()
    mrow = nc.dram_tensor("mrow", [1, N], F32, kind="ExternalInput").ap()
    grow = nc.dram_tensor("grow", [1, N], F32, kind="ExternalInput").ap()
    on1 = nc.dram_tensor("on1", [1, 128], F32, kind="ExternalInput").ap()
    ctr_o = nc.dram_tensor("ctr_o", [128, 8 * N], F32, kind="ExternalOutput").ap()
    with ExitStack() as ctx:
        e = ctx.enter_context
        w1s = e(nc.sbuf_tensor("w1s", [128, 32 * 8 * 128], BF16))
        w2s = e(nc.sbuf_tensor("w2s", [128, 8 * 32 * 128], BF16))
        b1s = e(nc.sbuf_tensor("b1s", [128, 32], F32))
        b2s = e(nc.sbuf_tensor("b2s", [128, 8], F32))
        SEG = 2048
        mr = e(nc.sbuf_tensor("mr", [1, SEG], F32))
        gr = e(nc.sbuf_tensor("gr", [1, SEG], F32))
        cA = e(nc.sbuf_tensor("cA", [1, SEG], F32))
        cB = e(nc.sbuf_tensor("cB", [1, SEG], F32))
        carry = e(nc.sbuf_tensor("carry", [1, 1], F32))
        wseg = e(nc.sbuf_tensor("wseg", [1, CH], F32))
        wsc = nc.dram_tensor("wsc", [1, N], F32).ap()
        ones1 = e(nc.sbuf_tensor("ones1", [1, 128], F32))
        xnb = e(nc.sbuf_tensor("xnb", [128, 8 * CH], BF16))
        hg = e(nc.sbuf_tensor("hg", [128, 32 * CH], BF16))
        tmp = e(nc.sbuf_tensor("tmp", [128, CH], F32))
        outc = e(nc.sbuf_tensor("outc", [128, 8 * CH], F32))
        ps_w = e(nc.psum_tensor("ps_w", [128, CH], F32))
        ps_h0 = e(nc.psum_tensor("ps_h0", [128, CH], F32))
        ps_h1 = e(nc.psum_tensor("ps_h1", [128, CH], F32))
        ps_o0 = e(nc.psum_tensor("ps_o0", [128, CH], F32))
        ps_o1 = e(nc.psum_tensor("ps_o1", [128, CH], F32))
        b = B(nc, ctx)
        for sname in ("ldc", "ldg", "ldx", "mm", "dv", "ac", "st"):
            b.sem(sname)
        block = e(nc.Block())

        b.emit("gpsimd", lambda g: g.dma_start(out=w1s[:], in_=w1p[:, :]), inc=("ldg", 16))
        b.emit("gpsimd", lambda g: g.dma_start(out=w2s[:], in_=w2p[:, :]), inc=("ldg", 16))
        for dst, src in ((b1s, b1p), (b2s, b2p), (ones1, on1)):
            b.emit("sync", lambda s, d=dst, sr=src: s.dma_start(out=d[:], in_=sr[:, :]), inc=("ldc", 16))
        ld0 = [("ldc", b.at("ldc")), ("ldg", b.at("ldg"))]
        b.emit("vector", lambda v: v.memset(carry[:], 0.0), inc="dv")
        NSEG = N // SEG
        for sg in range(NSEG):
            dvv = b.at("dv")
            b.emit("sync", lambda s, sg=sg: s.dma_start(out=mr[:], in_=mrow[0:1, sg * SEG:(sg + 1) * SEG]),
                   wait=[("dv", dvv), ("st", b.at("st"))], inc=("ldx", 16))
            b.emit("sync", lambda s, sg=sg: s.dma_start(out=gr[:], in_=grow[0:1, sg * SEG:(sg + 1) * SEG]),
                   inc=("ldx", 16))
            ldw = b.at("ldx")
            b.emit("vector", lambda v: v.tensor_copy(cA[:], mr[:]), wait=[("ldx", ldw)], inc="dv")
            cur, oth = cA, cB
            sh = 1
            while sh < SEG:
                b.emit("vector", lambda v, sh=sh, cur=cur, oth=oth: v.tensor_add(
                    oth[0:1, sh:SEG], cur[0:1, sh:SEG], cur[0:1, 0:SEG - sh]), wait="dv", inc="dv")
                b.emit("vector", lambda v, sh=sh, cur=cur, oth=oth: v.tensor_copy(
                    oth[0:1, 0:sh], cur[0:1, 0:sh]), wait="dv", inc="dv")
                cur, oth = oth, cur
                sh *= 2
            b.emit("vector", lambda v, cur=cur: v.tensor_scalar(cur[:], cur[:], carry[0:1, 0:1], None, ALU.add),
                   wait="dv", inc="dv")
            b.emit("vector", lambda v, cur=cur: v.tensor_copy(carry[:], cur[0:1, SEG - 1:SEG]), wait="dv", inc="dv")
            b.emit("vector", lambda v, cur=cur, oth=oth: v.tensor_scalar(oth[:], cur[:], float(CAP), None, ALU.is_le),
                   wait="dv", inc="dv")
            b.emit("vector", lambda v, oth=oth: v.tensor_tensor(oth[:], oth[:], mr[:], ALU.mult), wait="dv", inc="dv")
            b.emit("vector", lambda v, oth=oth: v.tensor_tensor(oth[:], oth[:], gr[:], ALU.mult), wait="dv", inc="dv")
            b.emit("sync", lambda s, sg=sg, oth=oth: s.dma_start(out=wsc[0:1, sg * SEG:(sg + 1) * SEG], in_=oth[:]),
                   wait=["dv"], inc=("st", 16))
        dv_w = b.at("dv")
        st_w = b.at("st")
        for eng in ("scalar", "tensor"):
            b.emit(eng, lambda x: None, wait=ld0 + [("dv", dv_w)])

        for c in range(NCH):
            st0 = b.at("st")
            b.emit("gpsimd", lambda g, c=c: g.dma_start(
                out=xnb[:].rearrange("p (u n) -> p u n", u=8),
                in_=xnT.rearrange("p (u n) -> p u n", u=8)[:, :, c * CH:(c + 1) * CH]),
                wait=[("st", st0)] + ld0, inc=("ldg", 16))
            ldv = [("ldg", b.at("ldg")), ("st", st0)]
            for eng in ("vector", "scalar", "tensor"):
                b.emit(eng, lambda x: None, wait=ldv)
            b.emit("sync", lambda s, c=c: s.dma_start(out=wseg[:], in_=wsc[0:1, c * CH:(c + 1) * CH]),
                   wait=[("st", st_w), ("dv", b.at("dv")), ("mm", b.at("mm"))], inc=("ldx", 16))
            b.emit("tensor", lambda t, c=c: t.matmul(ps_w[:, :], ones1[0:1, :], wseg[0:1, :], start=True, stop=True),
                   wait=["dv", ("ldx", b.at("ldx"))], inc="mm")
            guard = {0: ("ac", 0), 1: ("ac", 0)}
            for hi in range(32):
                pb = ps_h0 if hi % 2 == 0 else ps_h1
                b.emit("tensor", mmgroup(pb[:, :],
                       [(w1s[:, (hi * 8 + k) * 128:(hi * 8 + k + 1) * 128], xnb[:, k * CH:(k + 1) * CH]) for k in range(8)]),
                       wait=[guard[hi % 2]], inc="mm")
                b.emit("scalar", lambda s, hi=hi, pb=pb: s.activation(
                    hg[:, hi * CH:(hi + 1) * CH], pb[:, :], AF.Gelu, bias=b1s[:, hi:hi + 1]),
                    wait="mm", inc="ac")
                guard[hi % 2] = ("ac", b.at("ac"))
            ac_hg = b.at("ac")
            guard = {0: ("dv", 0), 1: ("dv", 0)}
            for di in range(8):
                pb = ps_o0 if di % 2 == 0 else ps_o1
                b.emit("tensor", mmgroup(pb[:, :],
                       [(w2s[:, (di * 32 + k) * 128:(di * 32 + k + 1) * 128], hg[:, k * CH:(k + 1) * CH]) for k in range(32)]),
                       wait=[("ac", ac_hg), guard[di % 2]], inc="mm")
                b.emit("vector", lambda v, di=di, pb=pb: v.tensor_scalar(tmp[:, :], pb[:, :], b2s[:, di:di + 1], None, ALU.add),
                       wait=["mm", "dv"], inc="dv")
                b.emit("vector", lambda v, di=di: v.tensor_tensor(outc[:, di * CH:(di + 1) * CH], tmp[:, :], ps_w[:, :], ALU.mult),
                       wait="dv", inc="dv")
                guard[di % 2] = ("dv", b.at("dv"))
            b.emit("sync", lambda s, c=c: s.dma_start(
                out=ctr_o.rearrange("p (u n) -> p u n", u=8)[:, :, c * CH:(c + 1) * CH],
                in_=outc[:].rearrange("p (u n) -> p u n", u=8)),
                wait=["dv"], inc=("st", 16))
        b.emit("sync", lambda s: None, wait=["st"])
        b.install(block)
    return nc


def build_combF(ntok, CH=512):
    nc = bass.Bass("TRN2", target_bir_lowering=False)
    NCH = ntok // CH
    NTT = CH // 128
    xT = nc.dram_tensor("xT", [128, 8 * ntok], F32, kind="ExternalInput").ap()
    s0T = nc.dram_tensor("s0T", [128, 8 * ntok], F32, kind="ExternalInput").ap()
    h1T = nc.dram_tensor("h1T", [128, 8 * ntok], F32, kind="ExternalInput").ap()
    ctrT = nc.dram_tensor("ctrT", [128, 8 * 8 * ntok], F32, kind="ExternalInput").ap()
    wout = nc.dram_tensor("wout", [128, 8 * 8 * 128], F32, kind="ExternalInput").ap()
    b_out = nc.dram_tensor("b_out", [128, 8], F32, kind="ExternalInput").ap()
    id128 = nc.dram_tensor("id128", [128, 128], F32, kind="ExternalInput").ap()
    x1_o = nc.dram_tensor("x1_o", [128, 8 * ntok], F32, kind="ExternalOutput").ap()
    xn1_o = nc.dram_tensor("xn1_o", [128, 8 * ntok], F32, kind="ExternalOutput").ap()
    with ExitStack() as ctx:
        e = ctx.enter_context
        wo = e(nc.sbuf_tensor("wo", [128, 8 * 8 * 128], BF16))
        idm = e(nc.sbuf_tensor("idm", [128, 128], F32))
        ones = e(nc.sbuf_tensor("ones", [128, 1], F32))
        ones1 = e(nc.sbuf_tensor("ones1", [1, 128], F32))
        epsv = e(nc.sbuf_tensor("epsv", [128, 1], F32))
        bo = e(nc.sbuf_tensor("bo", [128, 8], F32))
        xf = e(nc.sbuf_tensor("xf", [128, 8 * CH], F32))
        s0f = e(nc.sbuf_tensor("s0f", [128, 8 * CH], F32))
        h1b = e(nc.sbuf_tensor("h1b", [128, 8 * CH], BF16))
        cbuf = e(nc.sbuf_tensor("cbuf", [128, 8 * CH], F32))
        acc = e(nc.sbuf_tensor("acc", [128, 8 * CH], F32))
        sq = e(nc.sbuf_tensor("sq", [128, 8 * CH], F32))
        xn1 = e(nc.sbuf_tensor("xn1", [128, 8 * CH], F32))
        stat = e(nc.sbuf_tensor("stat", [128, 8 * NTT], F32))
        murow = e(nc.sbuf_tensor("murow", [1, 128 * NTT], F32))
        rsrow = e(nc.sbuf_tensor("rsrow", [1, 128 * NTT], F32))
        ps_st = e(nc.psum_tensor("ps_st", [128, 2], F32))
        ps_tr = e(nc.psum_tensor("ps_tr", [1, 128], F32))
        ps_tr2 = e(nc.psum_tensor("ps_tr2", [1, 128], F32))
        ps_bc = e(nc.psum_tensor("ps_bc", [128, 256], F32))
        ps_a = e(nc.psum_tensor("ps_a", [128, CH], F32))
        ps_b = e(nc.psum_tensor("ps_b", [128, CH], F32))
        b = B(nc, ctx)
        for sname in ("ldc", "ldg", "ldx", "mm", "dv", "ac", "st"):
            b.sem(sname)
        block = e(nc.Block())
        S = dict(ones=ones, ones1=ones1, idm=idm, sq=sq, stat=stat, murow=murow,
                 rsrow=rsrow, ps_st=ps_st, ps_tr=ps_tr, ps_tr2=ps_tr2, ps_bc=ps_bc, epsv=epsv)

        for dst, src in ((idm, id128), (bo, b_out)):
            b.emit("sync", lambda s, d=dst, sr=src: s.dma_start(out=d[:], in_=sr[:, :]), inc=("ldc", 16))
        b.emit("gpsimd", lambda g: g.dma_start(out=wo[:], in_=wout[:, :]), inc=("ldg", 16))
        ld0 = [("ldc", b.at("ldc")), ("ldg", b.at("ldg"))]
        b.emit("vector", lambda v: v.memset(ones[:], 1.0))
        b.emit("vector", lambda v: v.memset(ones1[:], 1.0))
        b.emit("vector", lambda v: v.memset(epsv[:], EPS), inc="dv")
        for eng in ("scalar", "tensor", "vector"):
            b.emit(eng, lambda x: None, wait=ld0 + ["dv"])

        for c in range(NCH):
            st0 = b.at("st")
            b.emit("sync", lambda s, c=c: s.dma_start(
                out=xf[:].rearrange("p (u n) -> p u n", u=8),
                in_=xT.rearrange("p (u n) -> p u n", u=8)[:, :, c * CH:(c + 1) * CH]),
                wait=[("st", st0)] + ld0, inc=("ldx", 16))
            b.emit("sync", lambda s, c=c: s.dma_start(
                out=s0f[:].rearrange("p (u n) -> p u n", u=8),
                in_=s0T.rearrange("p (u n) -> p u n", u=8)[:, :, c * CH:(c + 1) * CH]),
                inc=("ldx", 16))
            b.emit("gpsimd", lambda g, c=c: g.dma_start(
                out=h1b[:].rearrange("p (u n) -> p u n", u=8),
                in_=h1T.rearrange("p (u n) -> p u n", u=8)[:, :, c * CH:(c + 1) * CH]),
                wait=[("st", st0)] + ld0, inc=("ldg", 16))
            ldv = [("ldx", b.at("ldx")), ("ldg", b.at("ldg")), ("st", st0)]
            for eng in ("vector", "scalar", "tensor"):
                b.emit(eng, lambda x: None, wait=ldv)
            for u in range(8):
                b.emit("vector", lambda v, u=u: v.tensor_add(acc[:, u * CH:(u + 1) * CH], xf[:, u * CH:(u + 1) * CH], s0f[:, u * CH:(u + 1) * CH]),
                       wait=("dv" if u == 0 else ()), inc="dv")
            for ee in range(8):
                dvv = b.at("dv")
                b.emit("sync", lambda s, c=c, ee=ee: s.dma_start(
                    out=cbuf[:].rearrange("p (u n) -> p u n", u=8),
                    in_=ctrT.rearrange("p (e u n) -> p e u n", e=8, u=8)[:, ee, :, c * CH:(c + 1) * CH]),
                    wait=[("dv", dvv)], inc=("ldx", 16))
                b.emit("vector", lambda x2: None, wait=[("ldx", b.at("ldx"))])
                for u in range(8):
                    b.emit("vector", lambda v, u=u: v.tensor_add(acc[:, u * CH:(u + 1) * CH], acc[:, u * CH:(u + 1) * CH], cbuf[:, u * CH:(u + 1) * CH]),
                           inc="dv")
            guard = {0: ("dv", 0), 1: ("dv", 0)}
            for ei in range(8):
                pb = ps_a if ei % 2 == 0 else ps_b
                b.emit("tensor", mmgroup(pb[:, :],
                       [(wo[:, (ei * 8 + k) * 128:(ei * 8 + k + 1) * 128], h1b[:, k * CH:(k + 1) * CH]) for k in range(8)]),
                       wait=[guard[ei % 2]], inc="mm")
                b.emit("vector", lambda v, ei=ei, pb=pb: v.tensor_scalar(sq[:, ei * CH:(ei + 1) * CH], pb[:, :], bo[:, ei:ei + 1], None, ALU.add),
                       wait=["mm", "dv"], inc="dv")
                b.emit("vector", lambda v, ei=ei: v.tensor_add(acc[:, ei * CH:(ei + 1) * CH], acc[:, ei * CH:(ei + 1) * CH], sq[:, ei * CH:(ei + 1) * CH]),
                       wait="dv", inc="dv")
                guard[ei % 2] = ("dv", b.at("dv"))
            dv_x1 = b.at("dv")
            b.emit("sync", lambda s, c=c: s.dma_start(
                out=x1_o.rearrange("p (u n) -> p u n", u=8)[:, :, c * CH:(c + 1) * CH],
                in_=acc[:].rearrange("p (u n) -> p u n", u=8)),
                wait=[("dv", dv_x1)], inc=("st", 16))
            emit_ln(b, S, acc, xn1, CH, src_tm=None, affine=None)
            b.emit("sync", lambda s, c=c: s.dma_start(
                out=xn1_o.rearrange("p (u n) -> p u n", u=8)[:, :, c * CH:(c + 1) * CH],
                in_=xn1[:].rearrange("p (u n) -> p u n", u=8)),
                wait=["dv"], inc=("st", 16))
        b.emit("sync", lambda s: None, wait=["st"])
        b.install(block)
    return nc


def build_qkvG1(ntok, CH=512):
    nc = bass.Bass("TRN2", target_bir_lowering=False)
    NCH = ntok // CH
    xn1T = nc.dram_tensor("xn1T", [128, 8 * ntok], F32, kind="ExternalInput").ap()
    wdq = nc.dram_tensor("wdq", [128, 32 * 8 * 128], F32, kind="ExternalInput").ap()
    wuq = nc.dram_tensor("wuq", [128, 8 * 32 * 128], F32, kind="ExternalInput").ap()
    wqr = nc.dram_tensor("wqr", [128, 4 * 32 * 128], F32, kind="ExternalInput").ap()
    wdkv = nc.dram_tensor("wdkv", [128, 2 * 8 * 128], F32, kind="ExternalInput").ap()
    wuk = nc.dram_tensor("wuk", [128, 8 * 2 * 128], F32, kind="ExternalInput").ap()
    wuv = nc.dram_tensor("wuv", [128, 8 * 2 * 128], F32, kind="ExternalInput").ap()
    wkr = nc.dram_tensor("wkr", [128, 1 * 8 * 32], F32, kind="ExternalInput").ap()
    scq = nc.dram_tensor("scq", [128, 4 * ntok], F32, kind="ExternalInput").ap()
    sck = nc.dram_tensor("sck", [32, ntok], F32, kind="ExternalInput").ap()
    qc_o = nc.dram_tensor("qc_o", [128, 8 * ntok], F32, kind="ExternalOutput").ap()
    qr_o = nc.dram_tensor("qr_o", [128, 4 * ntok], F32, kind="ExternalOutput").ap()
    kc_o = nc.dram_tensor("kc_o", [128, 8 * ntok], F32, kind="ExternalOutput").ap()
    kr_o = nc.dram_tensor("kr_o", [32, ntok], F32, kind="ExternalOutput").ap()
    v_o = nc.dram_tensor("v_o", [128, 8 * ntok], F32, kind="ExternalOutput").ap()
    with ExitStack() as ctx:
        e = ctx.enter_context
        wdqs = e(nc.sbuf_tensor("wdqs", [128, 32 * 8 * 128], BF16))
        wuqs = e(nc.sbuf_tensor("wuqs", [128, 8 * 32 * 128], BF16))
        wqrs = e(nc.sbuf_tensor("wqrs", [128, 4 * 32 * 128], BF16))
        wdkvs = e(nc.sbuf_tensor("wdkvs", [128, 2 * 8 * 128], BF16))
        wuks = e(nc.sbuf_tensor("wuks", [128, 8 * 2 * 128], BF16))
        wuvs = e(nc.sbuf_tensor("wuvs", [128, 8 * 2 * 128], BF16))
        wkrs = e(nc.sbuf_tensor("wkrs", [128, 8 * 32], BF16))
        scqs = e(nc.sbuf_tensor("scqs", [128, 4 * CH], F32))
        scks = e(nc.sbuf_tensor("scks", [32, CH], F32))
        xnb = e(nc.sbuf_tensor("xnb", [128, 8 * CH], BF16))
        cqb = e(nc.sbuf_tensor("cqb", [128, 32 * CH], BF16))
        ckvb = e(nc.sbuf_tensor("ckvb", [128, 2 * CH], BF16))
        ot = e(nc.sbuf_tensor("ot", [128, 8 * CH], F32))
        ps_a = e(nc.psum_tensor("ps_a", [128, CH], F32))
        ps_b = e(nc.psum_tensor("ps_b", [128, CH], F32))
        b = B(nc, ctx)
        for sname in ("ldc", "ldg", "ldx", "mm", "dv", "ac", "st"):
            b.sem(sname)
        block = e(nc.Block())
        for dst, src in ((wdqs, wdq), (wuqs, wuq), (wqrs, wqr), (wdkvs, wdkv),
                         (wuks, wuk), (wuvs, wuv), (wkrs, wkr)):
            b.emit("gpsimd", lambda g, d=dst, sr=src: g.dma_start(out=d[:], in_=sr[:, :]), inc=("ldg", 16))
        ld0 = [("ldg", b.at("ldg"))]
        for eng in ("scalar", "tensor", "vector"):
            b.emit(eng, lambda x: None, wait=ld0)

        for c in range(NCH):
            st0 = b.at("st")
            b.emit("gpsimd", lambda g, c=c: g.dma_start(
                out=xnb[:].rearrange("p (u n) -> p u n", u=8),
                in_=xn1T.rearrange("p (u n) -> p u n", u=8)[:, :, c * CH:(c + 1) * CH]),
                wait=[("st", st0)] + ld0, inc=("ldg", 16))
            b.emit("sync", lambda s, c=c: s.dma_start(
                out=scqs[:].rearrange("p (u n) -> p u n", u=4),
                in_=scq.rearrange("p (u n) -> p u n", u=4)[:, :, c * CH:(c + 1) * CH]),
                wait=[("st", st0)], inc=("ldx", 16))
            b.emit("sync", lambda s, c=c: s.dma_start(
                out=scks[:], in_=sck[:, c * CH:(c + 1) * CH]), inc=("ldx", 16))
            ldv = [("ldx", b.at("ldx")), ("ldg", b.at("ldg")), ("st", st0)]
            for eng in ("vector", "scalar", "tensor"):
                b.emit(eng, lambda x: None, wait=ldv)

            guard = {0: ("ac", 0), 1: ("ac", 0)}
            for ei in range(32):
                pb = ps_a if ei % 2 == 0 else ps_b
                b.emit("tensor", mmgroup(pb[:, :],
                       [(wdqs[:, (ei * 8 + k) * 128:(ei * 8 + k + 1) * 128], xnb[:, k * CH:(k + 1) * CH]) for k in range(8)]),
                       wait=[guard[ei % 2]], inc="mm")
                b.emit("scalar", lambda s, ei=ei, pb=pb: s.copy(cqb[:, ei * CH:(ei + 1) * CH], pb[:, :]),
                       wait="mm", inc="ac")
                guard[ei % 2] = ("ac", b.at("ac"))
            ac_cq = b.at("ac")
            guard = {0: ("dv", 0), 1: ("dv", 0)}
            for ei in range(8):
                pb = ps_a if ei % 2 == 0 else ps_b
                b.emit("tensor", mmgroup(pb[:, :],
                       [(wuqs[:, (ei * 32 + k) * 128:(ei * 32 + k + 1) * 128], cqb[:, k * CH:(k + 1) * CH]) for k in range(32)]),
                       wait=[("ac", ac_cq), guard[ei % 2]], inc="mm")
                b.emit("vector", lambda v, ei=ei, pb=pb: v.tensor_copy(ot[:, ei * CH:(ei + 1) * CH], pb[:, :]),
                       wait=["mm", "dv"], inc="dv")
                guard[ei % 2] = ("dv", b.at("dv"))
            b.emit("sync", lambda s, c=c: s.dma_start(
                out=qc_o.rearrange("p (u n) -> p u n", u=8)[:, :, c * CH:(c + 1) * CH],
                in_=ot[:].rearrange("p (u n) -> p u n", u=8)), wait=["dv"], inc=("st", 16))
            st_ot = ("st", b.at("st"))
            guard = {0: ("dv", b.at("dv")), 1: ("dv", b.at("dv"))}
            for ei in range(4):
                pb = ps_a if ei % 2 == 0 else ps_b
                b.emit("tensor", mmgroup(pb[:, :],
                       [(wqrs[:, (ei * 32 + k) * 128:(ei * 32 + k + 1) * 128], cqb[:, k * CH:(k + 1) * CH]) for k in range(32)]),
                       wait=[("ac", ac_cq), guard[ei % 2]], inc="mm")
                b.emit("vector", lambda v, ei=ei, pb=pb: v.tensor_tensor(ot[:, ei * CH:(ei + 1) * CH], pb[:, :], scqs[:, ei * CH:(ei + 1) * CH], ALU.mult),
                       wait=["mm", "dv", st_ot], inc="dv")
                guard[ei % 2] = ("dv", b.at("dv"))
            b.emit("sync", lambda s, c=c: s.dma_start(
                out=qr_o.rearrange("p (u n) -> p u n", u=4)[:, :, c * CH:(c + 1) * CH],
                in_=ot[:, 0:4 * CH].rearrange("p (u n) -> p u n", u=4)), wait=["dv"], inc=("st", 16))
            guard = {0: ("dv", b.at("dv")), 1: ("dv", b.at("dv"))}
            for ei in range(2):
                pb = ps_a if ei % 2 == 0 else ps_b
                b.emit("tensor", mmgroup(pb[:, :],
                       [(wdkvs[:, (ei * 8 + k) * 128:(ei * 8 + k + 1) * 128], xnb[:, k * CH:(k + 1) * CH]) for k in range(8)]),
                       wait=[guard[ei % 2]], inc="mm")
                b.emit("scalar", lambda s, ei=ei, pb=pb: s.copy(ckvb[:, ei * CH:(ei + 1) * CH], pb[:, :]),
                       wait="mm", inc="ac")
                guard[ei % 2] = ("ac", b.at("ac"))
            ac_ckv = b.at("ac")
            for wsrc, dst_o in ((wuks, kc_o), (wuvs, v_o)):
                st_ot = None
                guard = {0: ("dv", b.at("dv")), 1: ("dv", b.at("dv"))}
                for ei in range(8):
                    pb = ps_a if ei % 2 == 0 else ps_b
                    b.emit("tensor", mmgroup(pb[:, :],
                           [(wsrc[:, (ei * 2 + k) * 128:(ei * 2 + k + 1) * 128], ckvb[:, k * CH:(k + 1) * CH]) for k in range(2)]),
                           wait=[("ac", ac_ckv), guard[ei % 2]], inc="mm")
                    if st_ot is None:
                        st_ot = ("st", b.at("st"))
                    b.emit("vector", lambda v, ei=ei, pb=pb: v.tensor_copy(ot[:, ei * CH:(ei + 1) * CH], pb[:, :]),
                           wait=["mm", "dv", st_ot], inc="dv")
                    guard[ei % 2] = ("dv", b.at("dv"))
                b.emit("sync", lambda s, c=c, dst_o=dst_o: s.dma_start(
                    out=dst_o.rearrange("p (u n) -> p u n", u=8)[:, :, c * CH:(c + 1) * CH],
                    in_=ot[:].rearrange("p (u n) -> p u n", u=8)), wait=["dv"], inc=("st", 16))
            b.emit("tensor", mmgroup(ps_a[0:32, :],
                   [(wkrs[:, k * 32:(k + 1) * 32], xnb[:, k * CH:(k + 1) * CH]) for k in range(8)]),
                   wait=["dv"], inc="mm")
            b.emit("vector", lambda v: v.tensor_tensor(ot[0:32, 0:CH], ps_a[0:32, :], scks[0:32, :], ALU.mult),
                   wait=["mm", "dv", ("st", b.at("st"))], inc="dv")
            b.emit("sync", lambda s, c=c: s.dma_start(
                out=kr_o[:, c * CH:(c + 1) * CH], in_=ot[0:32, 0:CH]), wait=["dv"], inc=("st", 16))
        b.emit("sync", lambda s: None, wait=["st"])
        b.install(block)
    return nc


def build_ffnG2(ntok, CH=256):
    nc = bass.Bass("TRN2", target_bir_lowering=False)
    NCH = ntok // CH
    xn1T = nc.dram_tensor("xn1T", [128, 8 * ntok], F32, kind="ExternalInput").ap()
    x1T = nc.dram_tensor("x1T", [128, 8 * ntok], F32, kind="ExternalInput").ap()
    w1p = nc.dram_tensor("w1p", [128, 32 * 8 * 128], F32, kind="ExternalInput").ap()
    w2p = nc.dram_tensor("w2p", [128, 8 * 32 * 128], F32, kind="ExternalInput").ap()
    b1p = nc.dram_tensor("b1p", [128, 32], F32, kind="ExternalInput").ap()
    b2p = nc.dram_tensor("b2p", [128, 8], F32, kind="ExternalInput").ap()
    op_o = nc.dram_tensor("op_o", [128, 8 * ntok], F32, kind="ExternalOutput").ap()
    with ExitStack() as ctx:
        e = ctx.enter_context
        w1f = e(nc.sbuf_tensor("w1f", [128, 32 * 8 * 128], BF16))
        w2f = e(nc.sbuf_tensor("w2f", [128, 8 * 32 * 128], BF16))
        wbuf = e(nc.sbuf_tensor("wbuf", [128, 32 * 128], F32))
        abuf = e(nc.sbuf_tensor("abuf", [128, 32 * 128], BF16))
        b1s = e(nc.sbuf_tensor("b1s", [128, 32], F32))
        b2s = e(nc.sbuf_tensor("b2s", [128, 8], F32))
        al1 = e(nc.sbuf_tensor("al1", [128, 32], F32))
        al2 = e(nc.sbuf_tensor("al2", [128, 8], F32))
        ones = e(nc.sbuf_tensor("ones", [128, 1], BF16))
        xnb = e(nc.sbuf_tensor("xnb", [128, 8 * CH], BF16))
        x1f = e(nc.sbuf_tensor("x1f", [128, 8 * CH], F32))
        hgb = e(nc.sbuf_tensor("hgb", [128, 32 * CH], BF16))
        tmp = e(nc.sbuf_tensor("tmp", [128, CH], F32))
        ot = e(nc.sbuf_tensor("ot", [128, 8 * CH], F32))
        ps_a = e(nc.psum_tensor("ps_a", [128, CH], F32))
        ps_b = e(nc.psum_tensor("ps_b", [128, CH], F32))
        ps_al = e(nc.psum_tensor("ps_al", [128, 1], F32))
        b = B(nc, ctx)
        for sname in ("ldc", "ldg", "ldx", "mm", "dv", "ac", "st"):
            b.sem(sname)
        block = e(nc.Block())

        b.emit("sync", lambda s: s.dma_start(out=b1s[:], in_=b1p[:, :]), inc=("ldc", 16))
        b.emit("sync", lambda s: s.dma_start(out=b2s[:], in_=b2p[:, :]), inc=("ldc", 16))
        b.emit("vector", lambda v: v.memset(ones[:], 1.0), inc="dv")
        def prep(wp, wf, nI, nK, alo, denom):
            for i in range(nI):
                acv, mmv = b.at("ac"), b.at("mm")
                b.emit("sync", lambda s, i=i, wp=wp, nK=nK: s.dma_start(
                    out=wbuf[:, 0:nK * 128], in_=wp[:, i * nK * 128:(i + 1) * nK * 128]),
                    wait=[("ac", acv), ("mm", mmv)], inc=("ldx", 16))
                ldw = b.at("ldx")
                b.emit("scalar", lambda s, i=i, wf=wf, nK=nK: s.activation(
                    wf[:, i * nK * 128:(i + 1) * nK * 128], wbuf[:, 0:nK * 128], AF.Sign),
                    wait=[("ldx", ldw)], inc="ac")
                b.emit("scalar", lambda s, nK=nK: s.activation(abuf[:, 0:nK * 128], wbuf[:, 0:nK * 128], AF.Abs),
                    wait="ac", inc="ac")
                b.emit("tensor", mmgroup(ps_al[:, :],
                       [(abuf[:, k * 128:(k + 1) * 128], ones[:, :]) for k in range(nK)]),
                       wait=["ac", "dv"], inc="mm")
                b.emit("vector", lambda v, i=i, alo=alo, denom=denom: v.tensor_scalar(
                    alo[:, i:i + 1], ps_al[:, :], 1.0 / denom, None, ALU.mult),
                    wait="mm", inc="dv")
        prep(w1p, w1f, 32, 8, al1, 1024.0)
        prep(w2p, w2f, 8, 32, al2, 4096.0)
        ld0 = [("ldc", b.at("ldc")), ("dv", b.at("dv")), ("ac", b.at("ac"))]
        for eng in ("scalar", "tensor", "vector"):
            b.emit(eng, lambda x: None, wait=ld0)

        for c in range(NCH):
            st0 = b.at("st")
            b.emit("gpsimd", lambda g, c=c: g.dma_start(
                out=xnb[:].rearrange("p (u n) -> p u n", u=8),
                in_=xn1T.rearrange("p (u n) -> p u n", u=8)[:, :, c * CH:(c + 1) * CH]),
                wait=[("st", st0)] + ld0, inc=("ldg", 16))
            b.emit("sync", lambda s, c=c: s.dma_start(
                out=x1f[:].rearrange("p (u n) -> p u n", u=8),
                in_=x1T.rearrange("p (u n) -> p u n", u=8)[:, :, c * CH:(c + 1) * CH]),
                wait=[("st", st0)] + ([("ldx", b.at("ldx")), ("ldc", 32)] if c == 0 else []), inc=("ldc", 16))
            ldv = [("ldc", b.at("ldc")), ("ldg", b.at("ldg")), ("st", st0)]
            for eng in ("vector", "scalar", "tensor"):
                b.emit(eng, lambda x: None, wait=ldv)
            guard = {0: ("ac", 0), 1: ("ac", 0)}
            for hi in range(32):
                pb = ps_a if hi % 2 == 0 else ps_b
                b.emit("tensor", mmgroup(pb[:, :],
                       [(w1f[:, (hi * 8 + k) * 128:(hi * 8 + k + 1) * 128], xnb[:, k * CH:(k + 1) * CH]) for k in range(8)]),
                       wait=[guard[hi % 2]], inc="mm")
                b.emit("vector", lambda v, hi=hi, pb=pb: v.tensor_scalar(
                    tmp[:, :], pb[:, :], al1[:, hi:hi + 1], b1s[:, hi:hi + 1], ALU.mult, ALU.add),
                    wait=["mm", "dv", "ac"], inc="dv")
                b.emit("scalar", lambda s, hi=hi: s.activation(hgb[:, hi * CH:(hi + 1) * CH], tmp[:, :], AF.Gelu),
                       wait="dv", inc="ac")
                guard[hi % 2] = ("ac", b.at("ac"))
            ac_hg = b.at("ac")
            guard = {0: ("dv", 0), 1: ("dv", 0)}
            for di in range(8):
                pb = ps_a if di % 2 == 0 else ps_b
                b.emit("tensor", mmgroup(pb[:, :],
                       [(w2f[:, (di * 32 + k) * 128:(di * 32 + k + 1) * 128], hgb[:, k * CH:(k + 1) * CH]) for k in range(32)]),
                       wait=[("ac", ac_hg), guard[di % 2]], inc="mm")
                b.emit("vector", lambda v, di=di, pb=pb: v.tensor_scalar(
                    ot[:, di * CH:(di + 1) * CH], pb[:, :], al2[:, di:di + 1], b2s[:, di:di + 1], ALU.mult, ALU.add),
                    wait=["mm", "dv"], inc="dv")
                b.emit("vector", lambda v, di=di: v.tensor_add(
                    ot[:, di * CH:(di + 1) * CH], ot[:, di * CH:(di + 1) * CH], x1f[:, di * CH:(di + 1) * CH]),
                    wait="dv", inc="dv")
                guard[di % 2] = ("dv", b.at("dv"))
            b.emit("sync", lambda s, c=c: s.dma_start(
                out=op_o.rearrange("p (u n) -> p u n", u=8)[:, :, c * CH:(c + 1) * CH],
                in_=ot[:].rearrange("p (u n) -> p u n", u=8)), wait=["dv"], inc=("st", 16))
        b.emit("sync", lambda s: None, wait=["st"])
        b.install(block)
    return nc


def build_attnH(T=8192, NH=4):
    nc = bass.Bass("TRN2", target_bir_lowering=False)
    NQ = T // 128
    NKM = T // 512
    qd = [nc.dram_tensor(f"q{h}", [96, T], F32, kind="ExternalInput").ap() for h in range(NH)]
    kd = [nc.dram_tensor(f"k{h}", [96, T], F32, kind="ExternalInput").ap() for h in range(NH)]
    vd = [nc.dram_tensor(f"v{h}", [T, 64], F32, kind="ExternalInput").ap() for h in range(NH)]
    msk = nc.dram_tensor("msk", [128, 4 * 512], F32, kind="ExternalInput").ap()
    idd = nc.dram_tensor("idd", [128, 128], F32, kind="ExternalInput").ap()
    yo = [nc.dram_tensor(f"y{h}", [T, 64], F32, kind="ExternalOutput").ap() for h in range(NH)]
    SCALE = 1.0 / np.sqrt(96.0)
    with ExitStack() as ctx:
        e = ctx.enter_context
        qs = e(nc.sbuf_tensor("qs", [96, T], BF16))
        ks = e(nc.sbuf_tensor("ks", [96, T], BF16))
        vs = e(nc.sbuf_tensor("vs", [128, (T // 128) * 64], BF16))
        mks = e(nc.sbuf_tensor("mks", [128, 4 * 512], F32))
        idb = e(nc.sbuf_tensor("idb", [128, 128], BF16))
        sm = e(nc.sbuf_tensor("sm", [128, T], F32))
        pb_ = e(nc.sbuf_tensor("pb_", [128, T], BF16))
        pt = e(nc.sbuf_tensor("pt", [128, 512], BF16))
        osb = e(nc.sbuf_tensor("osb", [128, 64], F32))
        red = e(nc.sbuf_tensor("red", [128, 4], F32))
        ps_s = e(nc.psum_tensor("ps_s", [128, 512], F32))
        ps_t = e(nc.psum_tensor("ps_t", [128, 512], BF16))
        ps_pv = e(nc.psum_tensor("ps_pv", [128, 64], F32))
        b = B(nc, ctx)
        for sname in ("ldg", "mm", "dv", "ac", "st"):
            b.sem(sname)
        block = e(nc.Block())
        b.emit("gpsimd", lambda g: g.dma_start(out=mks[:], in_=msk[:, :]), inc=("ldg", 16))
        b.emit("gpsimd", lambda g: g.dma_start(out=idb[:], in_=idd[:, :]), inc=("ldg", 16))

        for h in range(NH):
            stH = b.at("st")
            dvH, acH, mmH = b.at("dv"), b.at("ac"), b.at("mm")
            b.emit("gpsimd", lambda g, h=h: g.dma_start(out=qs[0:96, :], in_=qd[h][:, :]),
                   wait=[("st", stH), ("dv", dvH), ("mm", mmH)], inc=("ldg", 16))
            b.emit("gpsimd", lambda g, h=h: g.dma_start(out=ks[0:96, :], in_=kd[h][:, :]), inc=("ldg", 16))
            b.emit("gpsimd", lambda g, h=h: g.dma_start(
                out=vs[:].rearrange("p (j d) -> p j d", d=64),
                in_=vd[h].rearrange("(j p) d -> p j d", p=128)), inc=("ldg", 16))
            ldv = [("ldg", b.at("ldg"))]
            for eng in ("vector", "scalar", "tensor"):
                b.emit(eng, lambda x: None, wait=ldv)
            for qi in range(NQ):
                nkm = qi // 4 + 1
                nk = nkm * 512
                dg = qi // 4
                r = qi % 4
                for km in range(nkm):
                    b.emit("tensor", lambda t, qi=qi, km=km: t.matmul(
                        ps_s[:, :], qs[0:96, 128 * qi:128 * (qi + 1)], ks[0:96, 512 * km:512 * (km + 1)],
                        start=True, stop=True), wait=["dv"], inc="mm")
                    if km == dg:
                        b.emit("vector", lambda v, km=km, r=r: v.scalar_tensor_tensor(
                            sm[:, 512 * km:512 * (km + 1)], ps_s[:, :], SCALE, mks[:, 512 * r:512 * (r + 1)],
                            ALU.mult, ALU.add), wait=["mm", "ac"], inc="dv")
                    else:
                        b.emit("vector", lambda v, km=km: v.tensor_scalar(
                            sm[:, 512 * km:512 * (km + 1)], ps_s[:, :], SCALE, None, ALU.mult),
                            wait=["mm", "ac"], inc="dv")
                b.emit("vector", lambda v, nk=nk: v.reduce_max(red[:, 0:1], sm[:, 0:nk], axis=AX), wait="dv", inc="dv")
                b.emit("vector", lambda v: v.tensor_scalar(red[:, 1:2], red[:, 0:1], -1.0, None, ALU.mult),
                       wait="dv", inc="dv")
                b.emit("scalar", lambda s, nk=nk: s.activation(pb_[:, 0:nk], sm[:, 0:nk], AF.Exp, bias=red[:, 1:2]),
                       wait="dv", inc="ac")
                b.emit("vector", lambda v, nk=nk: v.reduce_sum(red[:, 2:3], pb_[:, 0:nk], axis=AX), wait="ac", inc="dv")
                b.emit("vector", lambda v: v.reciprocal(red[:, 3:4], red[:, 2:3]), wait="dv", inc="dv")
                nkt = nk // 128
                for kt in range(nkt):
                    sl = kt % 4
                    if sl == 0:
                        dvv = b.at("dv")
                        nsl = min(4, nkt - kt)
                        b.emit("tensor", lambda t, qi=qi, kt=kt, nsl=nsl: [
                            t.transpose(ps_t[:, 128 * j:128 * (j + 1)], pb_[:, 128 * (kt + j):128 * (kt + j + 1)], idb[:, :])
                            for j in range(nsl)][-1], wait=[("dv", dvv), "ac"], inc="mm")
                        b.emit("vector", lambda v, nsl=nsl: v.tensor_copy(pt[:, 0:128 * nsl], ps_t[:, 0:128 * nsl]),
                               wait="mm", inc="dv")
                    b.emit("tensor", lambda t, kt=kt, sl=sl, qi=qi: t.matmul(
                        ps_pv[:, :], pt[:, 128 * sl:128 * (sl + 1)], vs[:, 64 * kt:64 * (kt + 1)],
                        start=(kt == 0), stop=(kt == nkt - 1)),
                        wait=(["dv"] if sl == 0 else ()), inc=("mm" if kt == nkt - 1 else None))
                b.emit("vector", lambda v: v.tensor_scalar(osb[:, :], ps_pv[:, :], red[:, 3:4], None, ALU.mult),
                       wait=["mm", "dv"], inc="dv")
                b.emit("sync", lambda s, h=h, qi=qi: s.dma_start(
                    out=yo[h][128 * qi:128 * (qi + 1), :], in_=osb[:, :]), wait=["dv"], inc=("st", 16))
        b.emit("sync", lambda s: None, wait=["st"])
        b.install(block)
    return nc


def build_finalI(ntok, CH=512):
    nc = bass.Bass("TRN2", target_bir_lowering=False)
    NCH = ntok // CH
    yaT = nc.dram_tensor("yaT", [128, 8 * ntok], F32, kind="ExternalInput").ap()
    opT = nc.dram_tensor("opT", [128, 8 * ntok], F32, kind="ExternalInput").ap()
    wop = nc.dram_tensor("wop", [128, 8 * 8 * 128], F32, kind="ExternalInput").ap()
    out_o = nc.dram_tensor("out_o", [128, 8 * ntok], F32, kind="ExternalOutput").ap()
    with ExitStack() as ctx:
        e = ctx.enter_context
        wo = e(nc.sbuf_tensor("wo", [128, 8 * 8 * 128], BF16))
        yab = e(nc.sbuf_tensor("yab", [128, 8 * CH], BF16))
        opf = e(nc.sbuf_tensor("opf", [128, 8 * CH], F32))
        ot = e(nc.sbuf_tensor("ot", [128, 8 * CH], F32))
        ps_a = e(nc.psum_tensor("ps_a", [128, CH], F32))
        ps_b = e(nc.psum_tensor("ps_b", [128, CH], F32))
        b = B(nc, ctx)
        for sname in ("ldc", "ldg", "ldx", "mm", "dv", "ac", "st"):
            b.sem(sname)
        block = e(nc.Block())
        b.emit("gpsimd", lambda g: g.dma_start(out=wo[:], in_=wop[:, :]), inc=("ldg", 16))
        ld0 = [("ldg", b.at("ldg"))]
        for c in range(NCH):
            st0 = b.at("st")
            b.emit("gpsimd", lambda g, c=c: g.dma_start(
                out=yab[:].rearrange("p (u n) -> p u n", u=8),
                in_=yaT.rearrange("p (u n) -> p u n", u=8)[:, :, c * CH:(c + 1) * CH]),
                wait=[("st", st0)] + ld0, inc=("ldg", 16))
            b.emit("sync", lambda s, c=c: s.dma_start(
                out=opf[:].rearrange("p (u n) -> p u n", u=8),
                in_=opT.rearrange("p (u n) -> p u n", u=8)[:, :, c * CH:(c + 1) * CH]),
                wait=[("st", st0)], inc=("ldx", 16))
            ldv = [("ldx", b.at("ldx")), ("ldg", b.at("ldg")), ("st", st0)]
            for eng in ("vector", "scalar", "tensor"):
                b.emit(eng, lambda x: None, wait=ldv)
            guard = {0: ("dv", 0), 1: ("dv", 0)}
            for ei in range(8):
                pb = ps_a if ei % 2 == 0 else ps_b
                b.emit("tensor", mmgroup(pb[:, :],
                       [(wo[:, (ei * 8 + k) * 128:(ei * 8 + k + 1) * 128], yab[:, k * CH:(k + 1) * CH]) for k in range(8)]),
                       wait=[guard[ei % 2]], inc="mm")
                b.emit("vector", lambda v, ei=ei, pb=pb: v.tensor_add(
                    ot[:, ei * CH:(ei + 1) * CH], pb[:, :], opf[:, ei * CH:(ei + 1) * CH]),
                    wait=["mm", "dv"], inc="dv")
                guard[ei % 2] = ("dv", b.at("dv"))
            b.emit("sync", lambda s, c=c: s.dma_start(
                out=out_o.rearrange("p (u n) -> p u n", u=8)[:, :, c * CH:(c + 1) * CH],
                in_=ot[:].rearrange("p (u n) -> p u n", u=8)), wait=["dv"], inc=("st", 16))
        b.emit("sync", lambda s: None, wait=["st"])
        b.install(block)
    return nc


"""Orchestration: multi-launch kernel using klib builders."""

N = 16384
T = 8192
NTOK = 2048
CORES = list(range(8))


def assemble_scan(parts):
    a = np.stack([p.reshape(128, 8, NTOK) for p in parts])
    a = a.reshape(2, 4, 128, 8, NTOK)
    a = a.transpose(2, 1, 4, 3, 0)
    return np.ascontiguousarray(a.reshape(128, T * 16))


def slice_scan(h, c):
    b, tc = c // 4, c % 4
    a = h.reshape(128, T, 8, 2)[:, tc * NTOK:(tc + 1) * NTOK, :, b]
    return np.ascontiguousarray(a.transpose(0, 2, 1).reshape(128, 8 * NTOK))


def rope_scale():
    t = np.arange(T, dtype=np.float32)
    inv = 1.0 / 10000.0 ** (np.arange(0, 32, 2, dtype=np.float32) / 32.0)
    fr = t[:, None] * inv[None, :]
    emb = np.concatenate([fr, fr], -1)
    return (np.cos(emb) + np.sin(emb)).astype(np.float32)


def kernel(**inp):
    x = np.asarray(inp["L0_attn_x"], dtype=np.float32)
    xflat = np.ascontiguousarray(x.reshape(N, 1024))
    ln_g, ln_b = inp["L0_attn_ln_g"], inp["L0_attn_ln_b"]
    W_in, b_in = inp["L0_attn_W_in"], inp["L0_attn_b_in"]
    W_h, b_h = inp["L0_attn_W_h"], inp["L0_attn_b_h"]
    W_out, b_out = inp["L0_attn_W_out"], inp["L0_attn_b_out"]

    ncA = build_preA(NTOK)
    imsA = []
    for c in CORES:
        im = preA_inputs(xflat[c * NTOK:(c + 1) * NTOK], W_in[0], ln_g[0], ln_b[0],
                         b_in[0], b_h[0], inp["L0_ffn_router_w"], inp["L0_ffn_router_b"])
        imsA.append(im)
    rA = run_bass_kernel_spmd(ncA, imsA, core_ids=CORES).results
    xn_parts = [r["xn_o"] for r in rA]
    vpre0 = assemble_scan([r["vpre_o"] for r in rA])
    g0 = assemble_scan([r["gg_o"] for r in rA])
    mask = np.concatenate([r["mask_o"] for r in rA], 0)
    gates = np.concatenate([r["gates_o"] for r in rA], 0)

    ncS = build_scan(T)
    w0 = pack_lhsT(np.ascontiguousarray(W_h[0].T))
    h0 = run_bass_kernel_spmd(ncS, [{"w": w0, "vpre": vpre0, "g": g0}], core_ids=[0]).results[0]["hout"]

    ncC = build_midC(NTOK)
    woutp0 = pack_lhsT(np.ascontiguousarray(W_out[0].T))
    win1p = pack_lhsT(np.ascontiguousarray(W_in[1].T))
    imsC = []
    for c in CORES:
        imsC.append({
            "h0T": slice_scan(h0, c), "xnT": xn_parts[c],
            "wout": woutp0, "win1": win1p,
            "ln_g": vec_pm(ln_g[1]), "ln_b": vec_pm(ln_b[1]),
            "b_out": vec_pm(b_out[0]),
            "b_u": vec_pm(b_in[1][:1024]), "b_v": vec_pm(b_in[1][1024:]), "b_h": vec_pm(b_h[1]),
            "id128": np.eye(128, dtype=np.float32),
        })
    rC = run_bass_kernel_spmd(ncC, imsC, core_ids=CORES).results
    s0o_parts = [r["s0o_o"] for r in rC]
    vpre1 = assemble_scan([r["vpre_o"] for r in rC])
    g1 = assemble_scan([r["gg_o"] for r in rC])

    w1s = pack_lhsT(np.ascontiguousarray(W_h[1].T))
    h1 = run_bass_kernel_spmd(ncS, [{"w": w1s, "vpre": vpre1, "g": g1}], core_ids=[0]).results[0]["hout"]

    ncE = build_moeE(N)
    xn_full = np.concatenate(
        [p.reshape(128, 8, NTOK) for p in xn_parts], axis=2)
    xn_full = np.ascontiguousarray(xn_full.reshape(128, 8 * N))
    imsE = []
    for c in CORES:
        imsE.append({
            "xnT": xn_full,
            "w1p": pack_lhsT(np.ascontiguousarray(inp["L0_ffn_w1"][c])),
            "w2p": pack_lhsT(np.ascontiguousarray(inp["L0_ffn_w2"][c])),
            "b1p": vec_pm(inp["L0_ffn_b1"][c]), "b2p": vec_pm(inp["L0_ffn_b2"][c]),
            "mrow": np.ascontiguousarray(mask[:, c].reshape(1, N)),
            "grow": np.ascontiguousarray(gates[:, c].reshape(1, N)),
            "on1": np.ones((1, 128), np.float32),
        })
    rE = run_bass_kernel_spmd(ncE, imsE, core_ids=CORES).results
    ctrs = [r["ctr_o"].reshape(128, 8, N) for r in rE]

    ncF = build_combF(NTOK)
    wout1p = pack_lhsT(np.ascontiguousarray(W_out[1].T))
    imsF = []
    for c in CORES:
        ctr_c = np.stack([ct[:, :, c * NTOK:(c + 1) * NTOK] for ct in ctrs], axis=1)
        imsF.append({
            "xT": fm(xflat[c * NTOK:(c + 1) * NTOK]),
            "s0T": s0o_parts[c], "h1T": slice_scan(h1, c),
            "ctrT": np.ascontiguousarray(ctr_c.reshape(128, 8 * 8 * NTOK)),
            "wout": wout1p, "b_out": vec_pm(b_out[1]),
            "id128": np.eye(128, dtype=np.float32),
        })
    rF = run_bass_kernel_spmd(ncF, imsF, core_ids=CORES).results
    x1_parts = [r["x1_o"] for r in rF]
    xn1_parts = [r["xn1_o"] for r in rF]

    ncG1 = build_qkvG1(NTOK, CH=256)
    sc = rope_scale()
    wdq = pack_lhsT(inp["L1_attn_W_DQ"])
    wuq = pack_lhsT(inp["L1_attn_W_UQ"])
    wqr = pack_lhsT(inp["L1_attn_W_QR"])
    wdkv = pack_lhsT(inp["L1_attn_W_DKV"])
    wuk = pack_lhsT(inp["L1_attn_W_UK"])
    wuv = pack_lhsT(inp["L1_attn_W_UV"])
    wkr = pack_lhsT(inp["L1_attn_W_KR"])
    imsG1 = []
    for c in CORES:
        tr = np.arange(c % 4 * NTOK, (c % 4 + 1) * NTOK)
        scq = np.empty((128, 4 * NTOK), np.float32)
        for u4 in range(4):
            rows = np.arange(128) + 128 * u4
            scq[:, u4 * NTOK:(u4 + 1) * NTOK] = sc[tr][:, rows % 32].T
        sck = np.ascontiguousarray(sc[tr].T)
        imsG1.append({
            "xn1T": xn1_parts[c], "wdq": wdq, "wuq": wuq, "wqr": wqr,
            "wdkv": wdkv, "wuk": wuk, "wuv": wuv, "wkr": wkr,
            "scq": scq, "sck": sck,
        })
    rG1 = run_bass_kernel_spmd(ncG1, imsG1, core_ids=CORES).results

    ncG2 = build_ffnG2(NTOK, CH=256)
    w1p = pack_lhsT(np.ascontiguousarray(inp["L1_ffn_w1"].T))
    w2p = pack_lhsT(np.ascontiguousarray(inp["L1_ffn_w2"].T))
    imsG2 = []
    for c in CORES:
        imsG2.append({"xn1T": xn1_parts[c], "x1T": x1_parts[c],
                      "w1p": w1p, "w2p": w2p,
                      "b1p": vec_pm(inp["L1_ffn_b1"]), "b2p": vec_pm(inp["L1_ffn_b2"])})
    rG2 = run_bass_kernel_spmd(ncG2, imsG2, core_ids=CORES).results
    op_parts = [r["op_o"] for r in rG2]

    def asm(key, rows):
        return np.concatenate([rG1[c][key].reshape(128, rows // 128, NTOK) for c in CORES], axis=2) \
            .reshape(128, rows // 128, N)
    qc = asm("qc_o", 1024)
    qr = asm("qr_o", 512)
    kc = asm("kc_o", 1024)
    vv = asm("v_o", 1024)
    kr = np.concatenate([rG1[c]["kr_o"] for c in CORES], axis=1)

    def frows(a, r0, r1):
        out = np.empty((r1 - r0, a.shape[2]), np.float32)
        for r in range(r0, r1):
            out[r - r0] = a[r % 128, r // 128]
        return out

    ncH = build_attnH(T, 4)
    mvar = np.full((128, 4, 512), -1e30, np.float32)
    for r in range(4):
        for p in range(128):
            mvar[p, r, :128 * r + p + 1] = 0.0
    mflat = np.ascontiguousarray(mvar.reshape(128, 4 * 512))
    imsH = []
    for c in CORES:
        im = {"msk": mflat, "idd": np.eye(128, dtype=np.float32)}
        for i in range(4):
            pair = 4 * c + i
            bb, hh = pair // 16, pair % 16
            tok = slice(bb * T, (bb + 1) * T)
            qC = frows(qc[:, :, tok], hh * 64, (hh + 1) * 64)
            qR = frows(qr[:, :, tok], hh * 32, (hh + 1) * 32)
            kC = frows(kc[:, :, tok], hh * 64, (hh + 1) * 64)
            vh = frows(vv[:, :, tok], hh * 64, (hh + 1) * 64)
            im[f"q{i}"] = np.ascontiguousarray(np.concatenate([qC, qR], 0))
            im[f"k{i}"] = np.ascontiguousarray(np.concatenate([kC, kr[:, tok]], 0))
            im[f"v{i}"] = np.ascontiguousarray(vh.T)
        imsH.append(im)
    rH = run_bass_kernel_spmd(ncH, imsH, core_ids=CORES).results
    y_att = np.empty((N, 1024), np.float32)
    for c in CORES:
        for i in range(4):
            pair = 4 * c + i
            bb, hh = pair // 16, pair % 16
            y_att[bb * T:(bb + 1) * T, hh * 64:(hh + 1) * 64] = rH[c][f"y{i}"]

    ncI = build_finalI(NTOK)
    wo = pack_lhsT(inp["L1_attn_W_O"])
    imsI = []
    for c in CORES:
        imsI.append({"yaT": fm(y_att[c * NTOK:(c + 1) * NTOK]),
                     "opT": op_parts[c], "wop": wo})
    rI = run_bass_kernel_spmd(ncI, imsI, core_ids=CORES).results
    out = np.concatenate([unfm(r["out_o"], NTOK, 1024) for r in rI], 0)
    return out.reshape(2, T, 1024).astype(np.float32)
